# version 02
# baseline (speedup 1.0000x reference)
import numpy as np
from contextlib import ExitStack, contextmanager

import concourse.bass as bass
import concourse.mybir as mybir
from concourse.bass_utils import run_bass_kernel_spmd

F32 = mybir.dt.float32
BF16 = mybir.dt.bfloat16
AF = mybir.ActivationFunctionType
ALU = mybir.AluOpType
AX = mybir.AxisListType
U16 = mybir.dt.uint16
I32 = mybir.dt.int32

ENGINES = ("tensor", "vector", "scalar", "gpsimd", "sync")

D = 2048
DC = D // 128
T_OWN = 2048
CTX = 128
XT = T_OWN + CTX
NCORES = 8
TG = 512
NG = T_OWN // TG

LN_EPS = 1e-5
RMS_EPS = float(np.finfo(np.float32).eps)
LOG_DECAY = float(np.log(0.45))
NEG_BIG = -1e30
TERN_INV_C = float(1.0 / (0.5 + 2.0 ** -9))



class _Op:
    __slots__ = ("id", "engine", "fn", "is_dma", "deps", "needs_inc",
                 "done_sem", "done_val", "tag", "pre_wait")

    def __init__(self, oid, engine, fn, is_dma, deps, tag):
        self.id = oid
        self.engine = engine
        self.fn = fn
        self.is_dma = is_dma
        self.deps = deps
        self.needs_inc = False
        self.done_sem = None
        self.done_val = None
        self.tag = tag
        self.pre_wait = None


class Buf:
    __slots__ = ("name", "last_w", "readers")

    def __init__(self, name):
        self.name = name
        self.last_w = None
        self.readers = []


class Prog:
    def __init__(self, nc, strict=("gpsimd", "vector", "scalar")):
        self.nc = nc
        self.strict = set(strict)
        self.stack = ExitStack()
        self.scopes = [self.stack]
        self.ops = {e: [] for e in ENGINES}
        self._oid = 0
        self.sbuf_bytes = 0
        self.out_dmas = []
        self._names = set()
        self.psbanks = [self.stack.enter_context(
            nc.psum_tensor(f"psb{i}", [128, 512], F32)) for i in range(8)]
        self.bankbuf = [Buf(f"bank{i}") for i in range(8)]

    def bank(self, i):
        return self.psbanks[i][:, :], self.bankbuf[i]

    def sbuf(self, name, shape, dtype):
        assert name not in self._names, name
        self._names.add(name)
        t = self.scopes[-1].enter_context(
            self.nc.sbuf_tensor(name, list(shape), dtype))
        esz = 2 if dtype == BF16 else 4
        self.sbuf_bytes += int(np.prod(shape[1:])) * esz
        assert self.sbuf_bytes < 207500, \
            f"SBUF budget blown at {name}: {self.sbuf_bytes}"
        return t

    def buf(self, name):
        return Buf(name)

    @contextmanager
    def scope(self):
        st = ExitStack()
        self.scopes.append(st)
        saved = self.sbuf_bytes
        try:
            yield
        finally:
            lasts = []
            for e in ENGINES:
                if self.ops[e]:
                    lasts.append(self.ops[e][-1])
                    lasts.extend([o for o in self.ops[e] if o.is_dma][-8:])
            joiner = self.emit("sync", lambda eng: eng.nop(), deps=lasts,
                               tag="scope_join")
            for e in ENGINES:
                if e != "sync":
                    self.emit(e, lambda eng: eng.nop(), deps=[joiner],
                              tag=f"scope_rel_{e}")
            self.scopes.pop()
            st.close()
            self.sbuf_bytes = saved

    def emit(self, engine, fn, reads=(), writes=(), slots=(), deps=(), dma=False,
             tag=""):
        d = []
        fstrict = engine in self.strict
        for b in reads:
            if b.last_w is not None:
                d.append((b.last_w, fstrict))
        for b in slots:
            if b.last_w is not None:
                d.append((b.last_w, True))
        for b in writes:
            if b.last_w is not None:
                d.append((b.last_w, fstrict))
            for r in b.readers:
                d.append((r, fstrict))
        for x in deps:
            if x is None:
                continue
            d.append(x if isinstance(x, tuple) else (x, fstrict))
        op = _Op(self._oid, engine, fn, dma, d, tag)
        self._oid += 1
        self.ops[engine].append(op)
        for b in reads:
            b.readers.append(op)
        for b in slots:
            b.readers.append(op)
        for b in writes:
            b.last_w = op
            b.readers = []
        return op

    def out_dma(self, engine, fn, reads=(), deps=(), tag="out"):
        op = self.emit(engine, fn, reads=reads, deps=deps, dma=True, tag=tag)
        self.out_dmas.append(op)
        return op

    def _verify_schedule(self):
        idx = {e: 0 for e in ENGINES}
        semval = {}
        progressed = True
        total = sum(len(v) for v in self.ops.values())
        done = 0
        while progressed:
            progressed = False
            for e in ENGINES:
                while idx[e] < len(self.ops[e]):
                    op = self.ops[e][idx[e]]
                    ok = True
                    if op.pre_wait is not None and \
                            semval.get(id(op.pre_wait[0]), 0) < op.pre_wait[1]:
                        ok = False
                    if ok:
                        for (y, force) in op.deps:
                            if y.done_sem is None:
                                continue
                            if (not force) and (not y.is_dma) and (not op.is_dma) \
                                    and y.engine == e:
                                continue
                            if semval.get(id(y.done_sem), 0) < y.done_val:
                                ok = False
                                break
                    if not ok:
                        break
                    if op.needs_inc:
                        k = id(op.done_sem)
                        semval[k] = semval.get(k, 0) + (16 if op.is_dma else 1)
                    idx[e] += 1
                    done += 1
                    progressed = True
        if done != total:
            stuck = []
            for e in ENGINES:
                if idx[e] < len(self.ops[e]):
                    op = self.ops[e][idx[e]]
                    why = [(y.engine, y.tag) for (y, f) in op.deps
                           if y.done_sem is not None
                           and semval.get(id(y.done_sem), 0) < y.done_val]
                    stuck.append((e, op.tag, why[:4]))
            raise RuntimeError(f"schedule deadlock: {done}/{total} ran; "
                               f"stuck: {stuck}")

    def realize(self):
        nc = self.nc
        tail_deps = list(self.out_dmas)
        for e in ENGINES:
            if self.ops[e]:
                tail_deps.append(self.ops[e][-1])
                tail_deps.extend([o for o in self.ops[e] if o.is_dma][-8:])
        self.emit("sync", lambda eng: eng.nop(), deps=tail_deps, tag="final")

        NS = 8
        csem, dsem = {}, {}
        for e in ENGINES:
            if any(not o.is_dma for o in self.ops[e]):
                csem[e] = self.stack.enter_context(nc.semaphore(f"c_{e}"))
            if any(o.is_dma for o in self.ops[e]):
                dsem[e] = [self.stack.enter_context(nc.semaphore(f"d_{e}{j}"))
                           for j in range(NS)]

        for e in ENGINES:
            for op in self.ops[e]:
                for (y, force) in op.deps:
                    assert y.id < op.id, (op.tag, y.tag)
                    if force or (y.engine != op.engine) or y.is_dma or op.is_dma:
                        y.needs_inc = True
        for e in ENGINES:
            c, dc = 0, 0
            uses = [0] * NS
            for op in self.ops[e]:
                if op.is_dma:
                    slot = dc % NS
                    dc += 1
                    uses[slot] += 1
                    op.done_sem, op.done_val = dsem[e][slot], uses[slot] * 16
                    op.needs_inc = True
                    if uses[slot] > 1:
                        op.pre_wait = (dsem[e][slot], (uses[slot] - 1) * 16)
                elif op.needs_inc:
                    c += 1
                    op.done_sem, op.done_val = csem[e], c

        self._verify_schedule()
        blk = self.stack.enter_context(nc.Block())

        def make_body(e):
            ops = self.ops[e]

            def body(eng):
                hiwater = {}
                for op in ops:
                    need = {}
                    for (y, force) in op.deps:
                        if y.done_sem is None:
                            continue
                        if (not force) and (not y.is_dma) and (not op.is_dma) \
                                and y.engine == e:
                            continue
                        k = y.done_sem
                        if need.get(k, -1) < y.done_val:
                            need[k] = y.done_val
                    if op.pre_wait is not None:
                        sw, vw = op.pre_wait
                        if need.get(sw, -1) < vw:
                            need[sw] = vw
                    for s, v in need.items():
                        if hiwater.get(s, 0) >= v:
                            continue
                        eng.wait_ge(s, v)
                        hiwater[s] = v
                    instr = op.fn(eng)
                    if op.needs_inc:
                        assert instr is not None, f"op {op.tag} returned no instr"
                        instr.then_inc(op.done_sem, 16 if op.is_dma else 1)
            return body

        for e in ENGINES:
            if not self.ops[e]:
                continue
            getattr(blk, e)(make_body(e))
        self.stack.close()



class Consts:
    def __init__(self, p: Prog):
        self.ident = p.sbuf("c_ident", [128, 128], BF16)
        self.ones_col_b = p.sbuf("c_ones_col_b", [128, 1], BF16)
        self.ones_row_f = p.sbuf("c_ones_row_f", [1, 128], F32)
        self.ones_row_b = p.sbuf("c_ones_row_b", [1, 128], BF16)
        self.b_ident = p.buf("ident")
        self.b_ones_col_b = p.buf("ones_col_b")
        self.b_ones_row_f = p.buf("ones_row_f")
        self.b_ones_row_b = p.buf("ones_row_b")

        p.emit("gpsimd", lambda g: g.memset(self.ident[:], 0.0),
               writes=[self.b_ident], tag="ident0")
        p.emit("gpsimd", lambda g: g.affine_select(
            out=self.ident[:], in_=self.ident[:], compare_op=ALU.not_equal,
            fill=1.0, base=0, pattern=[[-1, 128]], channel_multiplier=1),
            writes=[self.b_ident], tag="ident1")
        p.emit("gpsimd", lambda g: g.memset(self.ones_col_b[:], 1.0),
               writes=[self.b_ones_col_b], tag="ones_col")
        p.emit("gpsimd", lambda g: g.memset(self.ones_row_f[:], 1.0),
               writes=[self.b_ones_row_f], tag="ones_row")
        p.emit("gpsimd", lambda g: g.memset(self.ones_row_b[:], 1.0),
               writes=[self.b_ones_row_b], tag="ones_row_b")



class LNKit:
    MAXTT = 512

    def __init__(self, p: Prog, C: Consts, banks=(4, 5, 6, 7)):
        self.p, self.C = p, C
        TT = self.MAXTT
        self.xb = p.sbuf("ln_xb", [128, 2, 2, TT], BF16)
        self.rows = p.sbuf("ln_rows", [1, 3, TT], F32)
        self.mbs = p.sbuf("ln_mbs", [128, TT], F32)
        self.rbs = p.sbuf("ln_rbs", [128, TT], F32)
        self.tmp = p.sbuf("ln_tmp", [128, 2, TT], F32)
        self.bk = banks
        self.b_xb = [p.buf("ln_xb0"), p.buf("ln_xb1")]
        self.b_rows = p.buf("ln_rows")
        self.b_mbs, self.b_rbs = p.buf("ln_mbs"), p.buf("ln_rbs")
        self.b_tmp = [p.buf("ln_tmp0"), p.buf("ln_tmp1")]

    def emit(self, tag, TT, src, dst, resid=None, nchunks=DC, eps=LN_EPS):
        p, C = self.p, self.C
        st0, b_st0 = p.bank(self.bk[0])
        st1, b_st1 = p.bank(self.bk[1])
        psmb, b_psmb = p.bank(self.bk[2])
        psrb, b_psrb = p.bank(self.bk[3])
        sts = [(st0, b_st0), (st1, b_st1)]
        for c in range(nchunks):
            s_ap, s_b = src(c)
            r = c % 2
            xb_ap = self.xb[:, r, 0, 0:TT]
            sq_ap = self.xb[:, r, 1, 0:TT]
            p.emit("vector", lambda e, o=xb_ap, i=s_ap: e.tensor_copy(o, i),
                   reads=[s_b], writes=[self.b_xb[r]], tag=f"{tag}c{c}cast")
            p.emit("vector", lambda e, o=sq_ap, i=xb_ap: e.tensor_tensor(
                o, i, i, op=ALU.mult), writes=[self.b_xb[r]], tag=f"{tag}c{c}sq")
            for h in range(2):
                p.emit("tensor", lambda e, r_=r, c_=c, h_=h: e.matmul(
                    sts[h_][0][0:1, 0:TT], C.ones_col_b[:],
                    self.xb[:, r_, h_, 0:TT],
                    start=(c_ == 0), stop=(c_ == nchunks - 1)),
                    reads=[self.b_xb[r], C.b_ones_col_b], writes=[sts[h][1]],
                    tag=f"{tag}c{c}mm{h}")
        b_rows = self.b_rows
        m, ms, t = (self.rows[0:1, i, 0:TT] for i in range(3))
        var = ms
        rr = ms
        nfeat = float(nchunks * 128)
        p.emit("vector", lambda e: e.tensor_scalar(m, st0[0:1, 0:TT],
               1.0 / nfeat, None, op0=ALU.mult), reads=[b_st0], writes=[b_rows],
               tag=f"{tag}m")
        p.emit("vector", lambda e: e.tensor_scalar(ms, st1[0:1, 0:TT],
               1.0 / nfeat, None, op0=ALU.mult), reads=[b_st1], writes=[b_rows],
               tag=f"{tag}ms")
        p.emit("vector", lambda e: e.tensor_tensor(t, m, m, op=ALU.mult),
               writes=[b_rows], tag=f"{tag}mm")
        p.emit("vector", lambda e: e.tensor_tensor(var, ms, t, op=ALU.subtract),
               writes=[b_rows], tag=f"{tag}var")
        p.emit("scalar", lambda e: e.activation(t, var, AF.Ln,
               bias=np.float32(eps)), reads=[b_rows], writes=[b_rows],
               tag=f"{tag}ln")
        p.emit("scalar", lambda e: e.activation(rr, t, AF.Exp,
               scale=np.float32(-0.5)), writes=[b_rows], tag=f"{tag}exp")
        p.emit("tensor", lambda e: e.matmul(psmb[:, 0:TT], C.ones_row_f[:],
               m, start=True, stop=True), reads=[b_rows, C.b_ones_row_f],
               writes=[b_psmb], tag=f"{tag}bm")
        p.emit("tensor", lambda e: e.matmul(psrb[:, 0:TT], C.ones_row_f[:],
               rr, start=True, stop=True), reads=[b_rows, C.b_ones_row_f],
               writes=[b_psrb], tag=f"{tag}br")
        p.emit("vector", lambda e: e.tensor_copy(self.mbs[:, 0:TT],
               psmb[:, 0:TT]), reads=[b_psmb], writes=[self.b_mbs],
               tag=f"{tag}mbs")
        p.emit("vector", lambda e: e.tensor_copy(self.rbs[:, 0:TT],
               psrb[:, 0:TT]), reads=[b_psrb], writes=[self.b_rbs],
               tag=f"{tag}rbs")
        for c in range(nchunks):
            s_ap, s_b = src(c)
            on_pool = (c % 2 == 1)
            eng = "gpsimd" if on_pool else "vector"
            mb = (self.mbs[:, 0:TT], self.b_mbs) if on_pool else \
                 (psmb[:, 0:TT], b_psmb)
            rb = (self.rbs[:, 0:TT], self.b_rbs) if on_pool else \
                 (psrb[:, 0:TT], b_psrb)
            r = c % 2
            tm = self.tmp[:, r, 0:TT]
            p.emit(eng, lambda e, o=tm, i=s_ap, b=mb[0]: e.tensor_tensor(
                o, i, b, op=ALU.subtract), reads=[s_b, mb[1]],
                writes=[self.b_tmp[r]], tag=f"{tag}c{c}sub")
            if dst is not None:
                d_ap, d_b = dst(c)
                p.emit(eng, lambda e, o=d_ap, i=tm, b=rb[0]: e.tensor_tensor(
                    o, i, b, op=ALU.mult), reads=[self.b_tmp[r], rb[1]],
                    writes=[d_b], tag=f"{tag}c{c}mul")
            if resid is not None:
                assert dst is None
                x_ap, x_b = resid(c)
                p.emit(eng, lambda e, o=tm, b=rb[0]: e.tensor_tensor(
                    o, o, b, op=ALU.mult), reads=[rb[1]],
                    writes=[self.b_tmp[r]], tag=f"{tag}c{c}mul")
                p.emit(eng, lambda e, o=x_ap, i=tm: e.tensor_tensor(
                    o, o, i, op=ALU.add), reads=[self.b_tmp[r]],
                    writes=[x_b], tag=f"{tag}c{c}add")



def emit_l0_ffn_quant(p, C, wsrc, scr, n_cc_total, n_cols, tag):
    with p.scope():
        qind = p.sbuf(f"{tag}_qind", [128, 16, 32], BF16)
        qbc = p.sbuf(f"{tag}_qbc", [32, 16, 128], BF16)
        b_qind = p.buf(f"{tag}_qind")
        b_qbc = p.buf(f"{tag}_qbc")
        p.emit("gpsimd", lambda g: g.memset(qind[:], 0.0),
               writes=[b_qind], tag=f"{tag}qind0")
        p.emit("gpsimd", lambda g: g.affine_select(
            out=qind[0:64], in_=qind[0:64], compare_op=ALU.not_equal,
            fill=1.0, base=0, pattern=[[-2, 16], [1, 32]], channel_multiplier=0),
            writes=[b_qind], tag=f"{tag}qind1")
        p.emit("gpsimd", lambda g: g.affine_select(
            out=qind[64:128], in_=qind[64:128],
            compare_op=ALU.not_equal, fill=1.0, base=-1,
            pattern=[[-2, 16], [1, 32]], channel_multiplier=0),
            writes=[b_qind], tag=f"{tag}qind2")
        p.emit("gpsimd", lambda g: g.memset(qbc[:], 0.0),
               writes=[b_qbc], tag=f"{tag}qbc0")
        p.emit("gpsimd", lambda g: g.affine_select(
            out=qbc[:], in_=qbc[:], compare_op=ALU.not_equal,
            fill=1.0, base=0, pattern=[[2, 16], [1, 2], [0, 64]],
            channel_multiplier=-1),
            writes=[b_qbc], tag=f"{tag}qbc1")
        wb = p.sbuf(f"{tag}_wb", [128, 16, 512], BF16)
        awb = p.sbuf(f"{tag}_awb", [128, 2, 512], BF16)
        srow = p.sbuf(f"{tag}_srow", [32, 512], BF16)
        bsb = p.sbuf(f"{tag}_bsb", [128, 2, 512], BF16)
        ab = p.sbuf(f"{tag}_ab", [128, 2, 2, 512], BF16)
        wq = p.sbuf(f"{tag}_wq", [128, 2, 512], BF16)
        b_wb = [p.buf(f"{tag}_wb{i}") for i in range(16)]
        b_awb = [p.buf(f"{tag}_awb{i}") for i in range(2)]
        b_srow = p.buf(f"{tag}_srow")
        b_bsb = [p.buf(f"{tag}_bsb{i}") for i in range(2)]
        b_ab = [p.buf(f"{tag}_ab{i}") for i in range(2)]
        b_wq = [p.buf(f"{tag}_wq{i}") for i in range(2)]

        n_cs = n_cols // 512
        for qb in range(n_cc_total // 16 * n_cs):
            ccg, cs = divmod(qb, n_cs)
            ps_sc, b_ps_sc = p.bank(0)
            ps_bs, b_ps_bs = p.bank(1 + (qb % 2))
            for i in range(16):
                cc = ccg * 16 + i
                r = i % 2
                p.emit("gpsimd", lambda e, cc_=cc, cs_=cs, i_=i: e.dma_start(
                    out=wb[:, i_, :],
                    in_=wsrc[cc_ * 128:(cc_ + 1) * 128, cs_ * 512:(cs_ + 1) * 512]),
                    writes=[b_wb[i]], dma=True, tag=f"{tag}q{qb}ld{i}")
                p.emit("vector", lambda e, i_=i, r_=r: e.tensor_scalar(
                    awb[:, r_, :].bitcast(U16), wb[:, i_, :].bitcast(U16),
                    0x7fff, None, op0=ALU.bitwise_and),
                    reads=[b_wb[i]], writes=[b_awb[r]], tag=f"{tag}q{qb}abs{i}")
                p.emit("tensor", lambda e, i_=i, r_=r, ps_=ps_sc: e.matmul(
                    ps_[0:32, :], qind[:, i_, :], awb[:, r_, :],
                    start=(i_ == 0), stop=(i_ == 15)),
                    reads=[b_awb[r], b_qind], writes=[b_ps_sc],
                    tag=f"{tag}q{qb}smm{i}")
            p.emit("vector", lambda e, ps_=ps_sc: e.tensor_scalar(
                srow[:], ps_[0:32, :], 1.0 / 64.0, 1e-8, op0=ALU.mult,
                op1=ALU.max), reads=[b_ps_sc], writes=[b_srow],
                tag=f"{tag}q{qb}fin")
            for i in range(16):
                cc = ccg * 16 + i
                r = i % 2
                p.emit("tensor", lambda e, i_=i, ps_=ps_bs: e.matmul(
                    ps_[:, :], qbc[:, i_, :], srow[:],
                    start=True, stop=True),
                    reads=[b_srow, b_qbc], writes=[b_ps_bs],
                    tag=f"{tag}q{qb}bmm{i}")
                p.emit("vector", lambda e, r_=r, ps_=ps_bs: e.tensor_copy(
                    bsb[:, r_, :], ps_[:, :]),
                    reads=[b_ps_bs], writes=[b_bsb[r]], tag=f"{tag}q{qb}bsc{i}")
                p.emit("vector", lambda e, i_=i, r_=r: e.scalar_tensor_tensor(
                    ab[:, r_, 0, :], wb[:, i_, :], TERN_INV_C, bsb[:, r_, :],
                    op0=ALU.mult, op1=ALU.is_gt),
                    reads=[b_wb[i], b_bsb[r]], writes=[b_ab[r]],
                    tag=f"{tag}q{qb}A{i}")
                p.emit("vector", lambda e, i_=i, r_=r: e.scalar_tensor_tensor(
                    ab[:, r_, 1, :], wb[:, i_, :], -TERN_INV_C, bsb[:, r_, :],
                    op0=ALU.mult, op1=ALU.is_gt),
                    reads=[b_wb[i], b_bsb[r]], writes=[b_ab[r]],
                    tag=f"{tag}q{qb}B{i}")
                p.emit("gpsimd", lambda e, r_=r: e.tensor_tensor(
                    ab[:, r_, 0, :], ab[:, r_, 0, :], ab[:, r_, 1, :],
                    op=ALU.subtract), writes=[b_ab[r]], tag=f"{tag}q{qb}D{i}")
                p.emit("vector", lambda e, r_=r: e.tensor_tensor(
                    wq[:, r_, :], ab[:, r_, 0, :], bsb[:, r_, :], op=ALU.mult),
                    reads=[b_ab[r], b_bsb[r]], writes=[b_wq[r]],
                    tag=f"{tag}q{qb}W{i}")
                p.emit("gpsimd", lambda e, cc_=cc, cs_=cs, r_=r: e.dma_start(
                    out=scr[cc_, :, cs_ * 512:(cs_ + 1) * 512],
                    in_=wq[:, r_, :]),
                    reads=[b_wq[r]], dma=True, tag=f"{tag}q{qb}st{i}")


def emit_l0_ffn(p, C, ln, io, xacc):
    emit_l0_ffn_quant(p, C, io["wgu"], io["scr_gu"], DC, 16384, "qgu")
    emit_l0_ffn_quant(p, C, io["wpr"], io["scr_pr"], 64, 2048, "qpr")

    with p.scope():
        rmsrow = p.sbuf("ff_rms", [1, 2, TG], F32)
        b_rmsrow = p.buf("ff_rms")
        rbf = p.sbuf("ff_rbf", [128, TG], F32)
        b_rbf = p.buf("ff_rbf")
        for g in range(NG):
            _l0_ffn_gu(p, C, ln, io, xacc, g, rmsrow, b_rmsrow, rbf, b_rbf)
            _l0_ffn_proj(p, C, ln, io, xacc, g, rbf, b_rbf)


def _l0_ffn_gu(p, C, ln, io, xacc, g, rmsrow, b_rmsrow, rbf, b_rbf):
    with p.scope():
        xln = p.sbuf(f"ff{g}_xln", [128, DC, TG], BF16)
        b_xln = [p.buf(f"ff{g}_xln{c}") for c in range(DC)]
        wgu_sb = p.sbuf(f"ff{g}_wgu", [128, 2, DC, 256], BF16)
        b_wgu = [p.buf(f"ff{g}_wgu{i}") for i in range(2)]
        sg = p.sbuf(f"ff{g}_sg", [128, 2, TG], F32)
        b_sg = [p.buf(f"ff{g}_sg{i}") for i in range(2)]
        hh = p.sbuf(f"ff{g}_h", [128, 2, TG], BF16)
        b_hh = [p.buf(f"ff{g}_h{i}") for i in range(2)]
        hsq = p.sbuf(f"ff{g}_hsq", [128, 2, TG], BF16)
        b_hsq = [p.buf(f"ff{g}_hsq{i}") for i in range(2)]

        ln.emit(f"ff{g}ln1", TG,
                src=lambda c: (xacc(c)[0][:, g * TG:(g + 1) * TG], xacc(c)[1]),
                dst=lambda c: (xln[:, c, :], b_xln[c]))
        ps_rms, b_ps_rms = p.bank(3)
        for sc in range(32):
            rs = sc % 2
            p.emit("sync", lambda e, sc_=sc, rs_=rs: e.dma_start(
                out=wgu_sb[:, rs_, :, :],
                in_=io["scr_gu"][:, :, sc_ * 256:(sc_ + 1) * 256]
                .rearrange("c p n -> p c n")),
                writes=[b_wgu[rs]], dma=True, tag=f"ff{g}s{sc}ldg")
            for sub in range(2):
                hc = sc * 2 + sub
                pg, b_pg = p.bank(sub)
                for k in range(DC):
                    p.emit("tensor", lambda e, k_=k, rs_=rs, sub_=sub, pg_=pg: e.matmul(
                        pg_[:, 0:TG],
                        wgu_sb[:, rs_, k_, sub_ * 128:(sub_ + 1) * 128],
                        xln[:, k_, :],
                        start=(k_ == 0), stop=(k_ == DC - 1)),
                        reads=[b_wgu[rs], b_xln[k]], writes=[b_pg],
                        tag=f"ff{g}h{hc}g{k}")
                p.emit("scalar", lambda e, sub_=sub, pg_=pg: e.activation(
                    sg[:, sub_, :], pg_[:, 0:TG], AF.Silu),
                    reads=[b_pg], writes=[b_sg[sub]], tag=f"ff{g}h{hc}silu")
            p.emit("sync", lambda e, sc_=sc, rs_=rs: e.dma_start(
                out=wgu_sb[:, rs_, :, :],
                in_=io["scr_gu"][:, :, 8192 + sc_ * 256:8192 + (sc_ + 1) * 256]
                .rearrange("c p n -> p c n")),
                writes=[b_wgu[rs]], dma=True, tag=f"ff{g}s{sc}ldu")
            for sub in range(2):
                hc = sc * 2 + sub
                pg, b_pg = p.bank(sub)
                for k in range(DC):
                    p.emit("tensor", lambda e, k_=k, rs_=rs, sub_=sub, pg_=pg: e.matmul(
                        pg_[:, 0:TG],
                        wgu_sb[:, rs_, k_, sub_ * 128:(sub_ + 1) * 128],
                        xln[:, k_, :],
                        start=(k_ == 0), stop=(k_ == DC - 1)),
                        reads=[b_wgu[rs], b_xln[k]], writes=[b_pg],
                        tag=f"ff{g}h{hc}u{k}")
                p.emit("vector", lambda e, sub_=sub, pg_=pg: e.tensor_tensor(
                    hh[:, sub_, :], sg[:, sub_, :], pg_[:, 0:TG], op=ALU.mult),
                    reads=[b_sg[sub], b_pg], writes=[b_hh[sub]],
                    tag=f"ff{g}h{hc}mul")
                p.emit("gpsimd", lambda e, sub_=sub: e.tensor_tensor(
                    hsq[:, sub_, :], hh[:, sub_, :], hh[:, sub_, :],
                    op=ALU.mult),
                    reads=[b_hh[sub]], writes=[b_hsq[sub]], tag=f"ff{g}h{hc}sq")
                p.emit("tensor", lambda e, hc_=hc, sub_=sub: e.matmul(
                    ps_rms[0:1, 0:TG], C.ones_col_b[:], hsq[:, sub_, :],
                    start=(hc_ == 0), stop=(hc_ == 63)),
                    reads=[b_hsq[sub], C.b_ones_col_b], writes=[b_ps_rms],
                    tag=f"ff{g}h{hc}rmm")
                p.emit("gpsimd", lambda e, hc_=hc, g_=g, sub_=sub: e.dma_start(
                    out=io["scr_h"][hc_, :, g_ * TG:(g_ + 1) * TG],
                    in_=hh[:, sub_, :]),
                    reads=[b_hh[sub]], dma=True, tag=f"ff{g}h{hc}sth")
        ms = rmsrow[0:1, 0, :]
        tr = rmsrow[0:1, 1, :]
        p.emit("vector", lambda e: e.tensor_scalar(
            ms, ps_rms[0:1, 0:TG], 1.0 / 8192.0, None, op0=ALU.mult),
            reads=[b_ps_rms], writes=[b_rmsrow], tag=f"ff{g}rms_ms")
        p.emit("scalar", lambda e: e.activation(tr, ms, AF.Ln,
               bias=np.float32(RMS_EPS)), reads=[b_rmsrow],
               writes=[b_rmsrow], tag=f"ff{g}rms_ln")
        p.emit("scalar", lambda e: e.activation(ms, tr, AF.Exp,
               scale=np.float32(-0.5)), writes=[b_rmsrow],
               tag=f"ff{g}rms_exp")
        ps_rb, b_ps_rb = p.bank(0)
        p.emit("tensor", lambda e: e.matmul(ps_rb[:, 0:TG], C.ones_row_f[:],
               ms, start=True, stop=True),
               reads=[b_rmsrow, C.b_ones_row_f], writes=[b_ps_rb],
               tag=f"ff{g}rms_b")
        p.emit("vector", lambda e: e.tensor_copy(rbf[:], ps_rb[:, 0:TG]),
               reads=[b_ps_rb], writes=[b_rbf], tag=f"ff{g}rms_e")


def _l0_ffn_proj(p, C, ln, io, xacc, g, rbf, b_rbf):
    with p.scope():
        hin = p.sbuf(f"ff{g}_hin", [128, 2, TG], BF16)
        b_hin = [p.buf(f"ff{g}_hin{i}") for i in range(2)]
        wpr_sb = p.sbuf(f"ff{g}_wpr", [128, 2, 1024], BF16)
        b_wpr = [p.buf(f"ff{g}_wpr{i}") for i in range(2)]
        ysb = p.sbuf(f"ff{g}_y", [128, DC, TG], F32)
        b_ysb = [p.buf(f"ff{g}_y{c}") for c in range(DC)]

        for ohalf in range(2):
            pys = [p.bank(i) for i in range(8)]
            for k in range(64):
                r = k % 2
                p.emit("sync", lambda e, k_=k, g_=g, r_=r: e.dma_start(
                    out=hin[:, r_, :],
                    in_=io["scr_h"][k_, :, g_ * TG:(g_ + 1) * TG]),
                    writes=[b_hin[r]], dma=True, tag=f"ff{g}o{ohalf}k{k}ldh")
                p.emit("sync", lambda e, k_=k, oh_=ohalf, r_=r: e.dma_start(
                    out=wpr_sb[:, r_, :],
                    in_=io["scr_pr"][k_, :, oh_ * 1024:(oh_ + 1) * 1024]),
                    writes=[b_wpr[r]], dma=True, tag=f"ff{g}o{ohalf}k{k}ldw")
                for oc in range(8):
                    p.emit("tensor", lambda e, k_=k, oc_=oc, r_=r: e.matmul(
                        pys[oc_][0][:, 0:TG],
                        wpr_sb[:, r_, oc_ * 128:(oc_ + 1) * 128],
                        hin[:, r_, :],
                        start=(k_ == 0), stop=(k_ == 63)),
                        reads=[b_wpr[r], b_hin[r]], writes=[pys[oc][1]],
                        tag=f"ff{g}o{ohalf}k{k}m{oc}")
            for oc in range(8):
                c = ohalf * 8 + oc
                p.emit("vector", lambda e, oc_=oc, c_=c: e.tensor_tensor(
                    ysb[:, c_, :], pys[oc_][0][:, 0:TG], rbf[:],
                    op=ALU.mult), reads=[pys[oc][1], b_rbf],
                    writes=[b_ysb[c]], tag=f"ff{g}o{ohalf}ev{oc}")
        ln.emit(f"ff{g}ln2", TG,
                src=lambda c: (ysb[:, c, :], b_ysb[c]),
                dst=None,
                resid=lambda c: (xacc(c)[0][:, g * TG:(g + 1) * TG],
                                 xacc(c)[1]))



NHEAD = 32


K_AFM = [None, None]


def emit_l0_attn(p, C, ln, io, xacc, ctxacc):
    lam = LOG_DECAY
    afm = p.sbuf("at_afm", [128, 2, T_OWN], BF16)
    b_afm = [p.buf(f"at_afm{c}") for c in range(2)]
    K_AFM[0], K_AFM[1] = afm, b_afm
    tri = p.sbuf("at_tri", [128, 128], BF16)
    b_tri = p.buf("at_tri")
    biasc = p.sbuf("at_biasc", [128, 2], F32)
    b_biasc = p.buf("at_biasc")
    beta = p.sbuf("at_beta", [128, 16], F32)
    b_beta = p.buf("at_beta")
    dv = p.sbuf("at_dv", [128, 2, 2], F32)
    b_dv = p.buf("at_dv")
    dvb = p.sbuf("at_dvb", [128, 2, 1], BF16)
    b_dvb = p.buf("at_dvb")

    p.emit("gpsimd", lambda g: g.memset(tri[:], 0.0), writes=[b_tri], tag="tri0")
    p.emit("gpsimd", lambda g: g.affine_select(
        out=tri[:], in_=tri[:], compare_op=ALU.is_ge, fill=NEG_BIG,
        base=0, pattern=[[1, 128]], channel_multiplier=-1),
        writes=[b_tri], tag="tri1")
    p.emit("gpsimd", lambda g: g.iota(
        biasc[:, 0:1], pattern=[[0, 1]], base=0, channel_multiplier=1,
        allow_small_or_imprecise_dtypes=True), writes=[b_biasc], tag="bc0")
    p.emit("vector", lambda e: e.tensor_scalar(
        biasc[:, 1:2], biasc[:, 0:1], -lam, 192.0 * lam, op0=ALU.mult,
        op1=ALU.add), reads=[b_biasc], writes=[b_biasc], tag="bc1")
    p.emit("vector", lambda e: e.tensor_scalar(
        biasc[:, 0:1], biasc[:, 0:1], -lam, 64.0 * lam, op0=ALU.mult,
        op1=ALU.add), writes=[b_biasc], tag="bc2")
    for c in range(2):
        p.emit("gpsimd", lambda g, c_=c: g.dma_start(
            out=dv[:, c_, 0:1],
            in_=io["vemb"][0, c_ * 128:(c_ + 1) * 128].rearrange(
                "(p o) -> p o", o=1)), writes=[b_dv], dma=True, tag=f"ve0{c}")
        p.emit("gpsimd", lambda g, c_=c: g.dma_start(
            out=dv[:, c_, 1:2],
            in_=io["vemb"][1, c_ * 128:(c_ + 1) * 128].rearrange(
                "(p o) -> p o", o=1)), writes=[b_dv], dma=True, tag=f"ve1{c}")
    for c in range(2):
        p.emit("vector", lambda e, c_=c: e.tensor_copy(
            dvb[:, c_, :], dv[:, c_, 0:1]), reads=[b_dv], writes=[b_dvb],
            tag=f"dvb{c}")
        p.emit("vector", lambda e, c_=c: e.tensor_tensor(
            dv[:, c_, 1:2], dv[:, c_, 1:2], dv[:, c_, 0:1], op=ALU.subtract),
            reads=[b_dv], writes=[b_dv], tag=f"dv{c}")

    _l0_attn_phase_a(p, C, ln, io, xacc, ctxacc, afm, b_afm, biasc, b_biasc,
                     tri, b_tri)
    _l0_attn_phase_b(p, C, ln, io, xacc, afm, b_afm, beta, b_beta, dv, b_dv,
                     dvb, b_dvb)


def _l0_attn_phase_a(p, C, ln, io, xacc, ctxacc, afm, b_afm, biasc, b_biasc,
                     tri, b_tri):
    with p.scope():
        xln = p.sbuf("atA_xln", [128, DC, 256], BF16)
        b_xln = [p.buf(f"atA_xln{c}") for c in range(DC)]
        wqp = p.sbuf("atA_wqp", [128, 2, DC, 128], BF16)
        b_wqp = [p.buf(f"atA_wqp{i}") for i in range(2)]
        wv = p.sbuf("atA_wv", [128, DC, 256], BF16)
        b_wv = p.buf("atA_wv")
        qtp = p.sbuf("atA_qtp", [128, 8, 2, 128], BF16)
        b_qtp = [p.buf(f"atA_qtp{i}") for i in range(2)]
        ktp = p.sbuf("atA_ktp", [128, 8, 3, 128], BF16)
        b_ktp = [p.buf(f"atA_ktp{i}") for i in range(3)]
        vtok = p.sbuf("atA_vtok", [128, 3, 9 * NHEAD], BF16)
        b_vtok = [p.buf(f"atA_vtok{i}") for i in range(3)]
        probs = p.sbuf("atA_probs", [128, 4, 2, 128], BF16)
        b_probs = [p.buf(f"atA_probs{i}") for i in range(4)]
        atok = p.sbuf("atA_atok", [128, 2, 256], BF16)
        b_atok = [p.buf(f"atA_atok{i}") for i in range(2)]
        recip = p.sbuf("atA_recip", [128, 2, NHEAD], F32)
        b_recip = [p.buf(f"atA_recip{i}") for i in range(2)]

        p.emit("gpsimd", lambda g: g.memset(
            vtok[:].rearrange("p t (h n) -> p t h n", n=9)[:, :, :, 8:9],
            1.0), writes=[b_vtok[0], b_vtok[1], b_vtok[2]], tag="atA_vones")
        p.emit("gpsimd", lambda g: g.dma_start(
            out=wv[:],
            in_=io["wv_t"][:, :].rearrange("(c p) n -> p c n", p=128)),
            writes=[b_wv], dma=True, tag="atA_ldwv")

        def ln_src(c, grp):
            if grp == 0:
                a, b = ctxacc(c)
                return a[:, 0:CTX], b
            a, b = xacc(c)
            return a[:, (grp - 1) * 256:grp * 256], b

        for grp in range(9):
            TT = 128 if grp == 0 else 256
            ln.emit(f"atA{grp}ln", TT,
                    src=lambda c: ln_src(c, grp),
                    dst=lambda c: (xln[:, c, 0:TT], b_xln[c]))
            ext_tiles = [0] if grp == 0 else [2 * grp - 1, 2 * grp]
            for (which, dstring, b_ring, nring, wsrc) in (
                    ("q", qtp, b_qtp, 2, io["wqp"]),
                    ("k", ktp, b_ktp, 3, io["wkp"])):
                if which == "q" and grp == 0:
                    continue
                for oc in range(8):
                    r = oc % 2
                    p.emit("gpsimd", lambda g, oc_=oc, r_=r, w_=wsrc: g.dma_start(
                        out=wqp[:, r_, :, :],
                        in_=w_[:, oc_ * 128:(oc_ + 1) * 128]
                        .rearrange("(c p) n -> p c n", p=128)),
                        writes=[b_wqp[r]], dma=True,
                        tag=f"atA{grp}{which}w{oc}")
                    pq, b_pq = p.bank(oc % 2)
                    for k in range(DC):
                        p.emit("tensor", lambda e, k_=k, r_=r, TT_=TT, pq_=pq:
                               e.matmul(pq_[:, 0:TT_], wqp[:, r_, k_, :],
                                        xln[:, k_, 0:TT_],
                                        start=(k_ == 0), stop=(k_ == DC - 1)),
                               reads=[b_wqp[r], b_xln[k]], writes=[b_pq],
                               tag=f"atA{grp}{which}m{oc}_{k}")
                    for ti, te in enumerate(ext_tiles):
                        sl = te % nring
                        p.emit("scalar", lambda e, oc_=oc, ti_=ti, sl_=sl,
                               d_=dstring, pq_=pq: e.activation(
                                   d_[:, oc_, sl_, :],
                                   pq_[:, ti_ * 128:(ti_ + 1) * 128], AF.Tanh),
                               reads=[b_pq], writes=[b_ring[sl]],
                               tag=f"atA{grp}{which}e{oc}_{ti}")
            for vc in range(2):
                pv, b_pv = p.bank(vc)
                for k in range(DC):
                    p.emit("tensor", lambda e, k_=k, vc_=vc, TT_=TT, pv_=pv:
                           e.matmul(pv_[:, 0:TT_], wv[:, k_, vc_ * 128:(vc_ + 1) * 128],
                                    xln[:, k_, 0:TT_],
                                    start=(k_ == 0), stop=(k_ == DC - 1)),
                           reads=[b_wv, b_xln[k]], writes=[b_pv],
                           tag=f"atA{grp}vm{vc}_{k}")
                p.emit("scalar", lambda e, vc_=vc, TT_=TT, pv_=pv: e.activation(
                    atok[:, vc_, 0:TT_], pv_[:, 0:TT_], AF.Sigmoid),
                    reads=[b_pv], writes=[b_atok[vc]], tag=f"atA{grp}vs{vc}")
            for ti, te in enumerate(ext_tiles):
                sl = te % 3
                pt, b_pt = p.bank(2)
                for vc in range(2):
                    p.emit("tensor", lambda e, vc_=vc, ti_=ti, pt_=pt: e.transpose(
                        pt_.bitcast(BF16)[:, vc_ * 128:(vc_ + 1) * 128],
                        atok[:, vc_, ti_ * 128:(ti_ + 1) * 128], C.ident[:]),
                        reads=[b_atok[vc], C.b_ident], writes=[b_pt],
                        tag=f"atA{grp}vt{te}_{vc}")
                p.emit("vector", lambda e, sl_=sl, pt_=pt: e.tensor_copy(
                    vtok[:, sl_, :].rearrange("p (h n) -> p h n", n=9)[:, :, 0:8],
                    pt_.bitcast(BF16)[:, 0:256].rearrange(
                        "p (h n) -> p h n", n=8)),
                    reads=[b_pt], writes=[b_vtok[sl]], tag=f"atA{grp}ve{te}")
            if grp == 0:
                continue
            for ti, te in enumerate(ext_tiles):
                tq = te - 1
                qsl = te % 2
                avb, b_avb = p.bank(6 + te % 2)
                for h in range(NHEAD):
                    hb = 32 * (h % 4)
                    htile = h // 4
                    psc, b_psc = p.bank(2 + h % 4)
                    p.emit("tensor", lambda e, h_=h, hb_=hb, ht_=htile,
                           qsl_=qsl, te_=te, ps_=psc: e.matmul(
                               ps_[:, 0:128],
                               ktp[hb_:hb_ + 32, ht_, te_ % 3, :],
                               qtp[hb_:hb_ + 32, ht_, qsl_, :],
                               start=True, stop=False,
                               tile_position=(hb_, 0)),
                           reads=[b_ktp[te % 3], b_qtp[qsl]], writes=[b_psc],
                           tag=f"atA_s{te}_{h}")
                    p.emit("tensor", lambda e, ps_=psc: e.matmul(
                        ps_[:, 0:128], C.ident[:], tri[:],
                        start=False, stop=True),
                        reads=[C.b_ident, b_tri], writes=[b_psc],
                        tag=f"atA_m{te}_{h}")
                    p.emit("tensor", lambda e, h_=h, hb_=hb, ht_=htile,
                           qsl_=qsl, te_=te, ps_=psc: e.matmul(
                               ps_[:, 128:256],
                               ktp[hb_:hb_ + 32, ht_, (te_ - 1) % 3, :],
                               qtp[hb_:hb_ + 32, ht_, qsl_, :],
                               start=True, stop=True,
                               tile_position=(hb_, 0)),
                           reads=[b_ktp[(te - 1) % 3], b_qtp[qsl]],
                           writes=[b_psc], tag=f"atA_l{te}_{h}")
                    for kt in range(2):
                        p.emit("scalar", lambda e, h_=h, kt_=kt, ps_=psc:
                               e.activation(
                                   probs[:, h_ % 4, kt_, :],
                                   ps_[:, kt_ * 128:(kt_ + 1) * 128], AF.Exp,
                                   bias=biasc[:, kt_:kt_ + 1],
                                   scale=np.float32(0.125)),
                               reads=[b_psc], slots=[b_biasc],
                               writes=[b_probs[h % 4]],
                               tag=f"atA_e{te}_{h}_{kt}")
                    for kt in range(2):
                        ksl = (te - kt) % 3
                        p.emit("tensor", lambda e, h_=h, kt_=kt, ksl_=ksl,
                               av_=avb: e.matmul(
                                   av_[:, 9 * h_:9 * h_ + 9],
                                   probs[:, h_ % 4, kt_, :],
                                   vtok[:, ksl_, 9 * h_:9 * h_ + 9],
                                   start=(kt_ == 0), stop=(kt_ == 1)),
                               reads=[b_probs[h % 4], b_vtok[ksl]],
                               writes=[b_avb], tag=f"atA_av{te}_{h}_{kt}")
                avv = avb[:, 0:288].rearrange("p (h n) -> p h n", n=9)
                p.emit("vector", lambda e, av_=avv, sl_=te % 2: e.reciprocal(
                    recip[:, sl_, :], av_[:, :, 8:9]),
                    reads=[b_avb], writes=[b_recip[te % 2]],
                    tag=f"atA_r{te}")
                p.emit("vector", lambda e, av_=avv, sl_=te % 2, tq_=tq:
                       e.tensor_tensor(
                           atok[:, tq_ % 2, :].rearrange("p (h n) -> p h n", n=8),
                           av_[:, :, 0:8],
                           recip[:, sl_, :].unsqueeze(2).broadcast_to(
                               [128, NHEAD, 8]),
                           op=ALU.mult),
                       reads=[b_avb, b_recip[te % 2]], writes=[b_atok[tq % 2]],
                       tag=f"atA_o{te}")
                ptf, b_ptf = p.bank(2 + (te % 2))
                for vc in range(2):
                    p.emit("tensor", lambda e, vc_=vc, tq_=tq, pt_=ptf:
                           e.transpose(
                               pt_.bitcast(BF16)[:, vc_ * 128:(vc_ + 1) * 128],
                               atok[:, tq_ % 2, vc_ * 128:(vc_ + 1) * 128],
                               C.ident[:]),
                           reads=[b_atok[tq % 2], C.b_ident], writes=[b_ptf],
                           tag=f"atA_t{te}_{vc}")
                for vc in range(2):
                    p.emit("vector", lambda e, vc_=vc, tq_=tq, pt_=ptf:
                           e.tensor_copy(
                               afm[:, vc_, tq_ * 128:(tq_ + 1) * 128],
                               pt_.bitcast(BF16)[:, vc_ * 128:(vc_ + 1) * 128]),
                           reads=[b_ptf], writes=[b_afm[vc]],
                           tag=f"atA_f{te}_{vc}")


def _l0_attn_phase_b(p, C, ln, io, xacc, afm, b_afm, beta, b_beta, dv, b_dv,
                     dvb, b_dvb):
    with p.scope():
        xln = p.sbuf("atB_xln", [128, DC, 256], BF16)
        b_xln = [p.buf(f"atB_xln{c}") for c in range(DC)]
        wg = p.sbuf("atB_wg", [128, 2, DC, 128], BF16)
        b_wg = [p.buf(f"atB_wg{i}") for i in range(2)]
        wo = p.sbuf("atB_wo", [128, 2, 2, 128], BF16)
        b_wo = [p.buf(f"atB_wo{i}") for i in range(2)]
        gate = p.sbuf("atB_gate", [128, 2, 256], BF16)
        b_gate = [p.buf(f"atB_gate{i}") for i in range(2)]
        ysb = p.sbuf("atB_y", [128, DC, 256], F32)
        b_ysb = [p.buf(f"atB_y{c}") for c in range(DC)]

        for oc in range(16):
            r = oc % 2
            pb, b_pb = p.bank(r)
            p.emit("gpsimd", lambda g, oc_=oc, r_=r: g.dma_start(
                out=wo[:, r_, :, :],
                in_=io["wo_t"][:, oc_ * 128:(oc_ + 1) * 128]
                .rearrange("(c p) n -> p c n", p=128)),
                writes=[b_wo[r]], dma=True, tag=f"atBbw{oc}")
            for c in range(2):
                p.emit("tensor", lambda e, c_=c, r_=r, pb_=pb: e.matmul(
                    pb_[:, 0:1], wo[:, r_, c_, :], dvb[:, c_, :],
                    start=(c_ == 0), stop=(c_ == 1)),
                    reads=[b_wo[r], b_dvb], writes=[b_pb], tag=f"atBbm{oc}_{c}")
            p.emit("vector", lambda e, oc_=oc, pb_=pb: e.tensor_copy(
                beta[:, oc_:oc_ + 1], pb_[:, 0:1]),
                reads=[b_pb], writes=[b_beta], tag=f"atBbe{oc}")

        for grp in range(8):
            ln.emit(f"atB{grp}ln", 256,
                    src=lambda c: (xacc(c)[0][:, grp * 256:(grp + 1) * 256],
                                   xacc(c)[1]),
                    dst=lambda c: (xln[:, c, :], b_xln[c]))
            for oc in range(DC):
                r = oc % 2
                pgt, b_pgt = p.bank(r)
                p.emit("gpsimd", lambda g, oc_=oc, r_=r: g.dma_start(
                    out=wg[:, r_, :, :],
                    in_=io["wg_t"][:, oc_ * 128:(oc_ + 1) * 128]
                    .rearrange("(c p) n -> p c n", p=128)),
                    writes=[b_wg[r]], dma=True, tag=f"atB{grp}gw{oc}")
                for k in range(DC):
                    p.emit("tensor", lambda e, k_=k, r_=r, pg_=pgt: e.matmul(
                        pg_[:, 0:256], wg[:, r_, k_, :], xln[:, k_, :],
                        start=(k_ == 0), stop=(k_ == DC - 1)),
                        reads=[b_wg[r], b_xln[k]], writes=[b_pgt],
                        tag=f"atB{grp}gm{oc}_{k}")
                p.emit("scalar", lambda e, r_=r, pg_=pgt: e.activation(
                    gate[:, r_, :], pg_[:, 0:256], AF.Sigmoid),
                    reads=[b_pgt], writes=[b_gate[r]], tag=f"atB{grp}gs{oc}")
                pat, b_pat = p.bank(2 + r)
                p.emit("gpsimd", lambda g, oc_=oc, r_=r: g.dma_start(
                    out=wo[:, r_, :, :],
                    in_=io["wo_t"][:, oc_ * 128:(oc_ + 1) * 128]
                    .rearrange("(c p) n -> p c n", p=128)),
                    writes=[b_wo[r]], dma=True, tag=f"atB{grp}ow{oc}")
                for c in range(2):
                    p.emit("vector", lambda e, c_=c, r_=r: e.tensor_scalar(
                        wo[:, r_, c_, :], wo[:, r_, c_, :], dv[:, c_, 1:2],
                        None, op0=ALU.mult),
                        reads=[b_dv], writes=[b_wo[r]], tag=f"atB{grp}os{oc}_{c}")
                for c in range(2):
                    p.emit("tensor", lambda e, c_=c, r_=r, grp_=grp, pa_=pat:
                           e.matmul(
                               pa_[:, 0:256], wo[:, r_, c_, :],
                               afm[:, c_, grp_ * 256:(grp_ + 1) * 256],
                               start=(c_ == 0), stop=(c_ == 1)),
                           reads=[b_wo[r], b_afm[c]], writes=[b_pat],
                           tag=f"atB{grp}om{oc}_{c}")
                p.emit("vector", lambda e, oc_=oc, r_=r, pa_=pat:
                       e.scalar_tensor_tensor(
                           ysb[:, oc_, :], pa_[:, 0:256], beta[:, oc_:oc_ + 1],
                           gate[:, r_, :], op0=ALU.add, op1=ALU.mult),
                       reads=[b_pat, b_gate[r]], slots=[b_beta],
                       writes=[b_ysb[oc]], tag=f"atB{grp}ev{oc}")
            ln.emit(f"atB{grp}ln2", 256,
                    src=lambda c: (ysb[:, c, :], b_ysb[c]),
                    dst=None,
                    resid=lambda c: (xacc(c)[0][:, grp * 256:(grp + 1) * 256],
                                     xacc(c)[1]))



LH = 16
LD = 128


def emit_l1_attn(p, C, ln, io, xacc):
    _l1_main(p, C, ln, io, xacc)
    _l1_p3(p, C, ln, io, xacc)


def _l1_main(p, C, ln, io, xacc):
    with p.scope():
        _l1_main2(p, C, ln, io, xacc)


def _l1_main2(p, C, ln, io, xacc):
    dk = p.sbuf("la_dk", [1, 4, LH], F32)
    b_dk = p.buf("la_dk")
    lndb = p.sbuf("la_lndb", [128, LH], F32)
    b_lndb = p.buf("la_lndb")
    mm = p.sbuf("la_mm", [128, 128], F32)
    b_mm = p.buf("la_mm")
    tri01 = p.sbuf("la_tri01", [128, 128], BF16)
    b_tri01 = p.buf("la_tri01")
    Ah = p.sbuf("la_A", [128, LH, 128], BF16)
    b_Ah = p.buf("la_A")
    af = p.sbuf("la_af", [128, 1, 128], F32)
    b_af = p.buf("la_af")
    dprow = p.sbuf("la_dprow", [1, LH, 128], BF16)
    b_dprow = p.buf("la_dprow")
    dprf = p.sbuf("la_dprf", [1, 2, 128], F32)
    b_dprf = p.buf("la_dprf")
    d128b = p.sbuf("la_d128b", [1, LH], BF16)
    b_d128b = p.buf("la_d128b")
    dvall = p.sbuf("la_dvall", [128, 16, LH], BF16)
    b_dvall = p.buf("la_dvall")
    dvf = p.sbuf("la_dvf", [128, 2, 16], F32)
    b_dvf = p.buf("la_dvf")

    p.emit("gpsimd", lambda g: g.dma_start(out=dk[:, 0, :], in_=io["decay"][:, :]),
           writes=[b_dk], dma=True, tag="la_lddk")
    p.emit("scalar", lambda e: e.activation(dk[:, 1, :], dk[:, 0, :], AF.Sigmoid),
           reads=[b_dk], writes=[b_dk], tag="la_sig")
    p.emit("scalar", lambda e: e.activation(dk[:, 2, :], dk[:, 1, :], AF.Ln),
           writes=[b_dk], tag="la_ln")
    p.emit("scalar", lambda e: e.activation(dk[:, 3, :], dk[:, 2, :], AF.Exp,
           scale=np.float32(128.0)), writes=[b_dk], tag="la_d128")
    p.emit("vector", lambda e: e.tensor_copy(d128b[:, :], dk[:, 3, :]),
           reads=[b_dk], writes=[b_d128b], tag="la_d128b")
    pb0, b_pb0 = p.bank(0)
    p.emit("tensor", lambda e, ps_=pb0: e.matmul(ps_[:, 0:LH], C.ones_row_f[:],
           dk[:, 2, :], start=True, stop=True),
           reads=[b_dk, C.b_ones_row_f], writes=[b_pb0], tag="la_bl")
    p.emit("vector", lambda e, ps_=pb0: e.tensor_copy(lndb[:], ps_[:, 0:LH]),
           reads=[b_pb0], writes=[b_lndb], tag="la_ble")
    p.emit("gpsimd", lambda g: g.iota(mm[:], pattern=[[1, 128]], base=0,
           channel_multiplier=-1, allow_small_or_imprecise_dtypes=True),
           writes=[b_mm], tag="la_m0")
    p.emit("vector", lambda e: e.tensor_scalar(mm[:], mm[:], 0.0, None,
           op0=ALU.max), reads=[b_mm], writes=[b_mm], tag="la_m1")
    p.emit("gpsimd", lambda g: g.memset(tri01[:], 0.0), writes=[b_tri01],
           tag="la_t0")
    p.emit("gpsimd", lambda g: g.affine_select(
        out=tri01[:], in_=tri01[:], compare_op=ALU.is_gt, fill=1.0,
        base=0, pattern=[[-1, 128]], channel_multiplier=1),
        writes=[b_tri01], tag="la_t1")
    p.emit("gpsimd", lambda g: g.iota(dprf[:, 0, :], pattern=[[1, 128]], base=1,
           channel_multiplier=0, allow_small_or_imprecise_dtypes=True),
           writes=[b_dprf], tag="la_dp0")
    p.emit("gpsimd", lambda g: g.iota(dvf[:, 0, :], pattern=[[-128, 16]],
           base=2047, channel_multiplier=-1,
           allow_small_or_imprecise_dtypes=True),
           writes=[b_dvf], tag="la_dv0")
    for h in range(LH):
        p.emit("scalar", lambda e, h_=h: e.activation(
            af[:, 0, :], mm[:], AF.Exp, scale=lndb[:, h_:h_ + 1]),
            reads=[b_mm], slots=[b_lndb], writes=[b_af], tag=f"la_Ae{h}")
        p.emit("vector", lambda e, h_=h: e.tensor_tensor(
            Ah[:, h_, :], af[:, 0, :], tri01[:], op=ALU.mult),
            reads=[b_af, b_tri01], writes=[b_Ah], tag=f"la_Am{h}")
        p.emit("scalar", lambda e, h_=h: e.activation(
            dprf[:, 1, :], dprf[:, 0, :], AF.Exp, scale=dk[:, 2, h_:h_ + 1]),
            reads=[b_dprf], slots=[b_dk], writes=[b_dprf], tag=f"la_dpe{h}")
        p.emit("vector", lambda e, h_=h: e.tensor_copy(
            dprow[:, h_, :], dprf[:, 1, :]),
            reads=[b_dprf], writes=[b_dprow], tag=f"la_dpc{h}")
        p.emit("scalar", lambda e, h_=h: e.activation(
            dvf[:, 1, :], dvf[:, 0, :], AF.Exp, scale=lndb[:, h_:h_ + 1]),
            reads=[b_dvf], slots=[b_lndb], writes=[b_dvf], tag=f"la_dve{h}")
        p.emit("vector", lambda e, h_=h: e.tensor_copy(
            dvall[:, :, h_], dvf[:, 1, :]),
            reads=[b_dvf], writes=[b_dvall], tag=f"la_dvc{h}")

    gkv = p.sbuf("la_gkv", [1, 2048], F32)
    b_gkv = p.buf("la_gkv")
    gks = p.sbuf("la_gks", [1, 2048], F32)
    b_gks = p.buf("la_gks")
    p.emit("gpsimd", lambda g: g.memset(gkv[:], 0.0), writes=[b_gkv], tag="la_g0")
    p.emit("gpsimd", lambda g: g.memset(gks[:], 0.0), writes=[b_gks], tag="la_g1")

    _l1_p1(p, C, ln, io, xacc, dvall, b_dvall, gkv, b_gkv, gks, b_gks)
    Tkv, b_Tkv = gkv, b_gkv
    Tks, b_Tks = gks, b_gks
    _l1_exchange(p, C, io, gkv, b_gkv, gks, b_gks, Tkv, b_Tkv, Tks, b_Tks)
    _l1_p2(p, C, io, Ah, b_Ah, dprow, b_dprow, d128b, b_d128b,
           Tkv, b_Tkv, Tks, b_Tks)


def _l1_p1(p, C, ln, io, xacc, dvall, b_dvall, gkv, b_gkv, gks, b_gks):
    NBORDER = [0, 1, 2, 3, 4, 8, 5, 9, 6, 10, 7, 11]
    with p.scope():
        xln = p.sbuf("l1a_xln", [128, DC, TG], BF16)
        b_xln = [p.buf(f"l1a_xln{c}") for c in range(DC)]
        wt = p.sbuf("l1a_wt", [128, 2, 512], BF16)
        b_wt = [p.buf(f"l1a_wt{i}") for i in range(2)]
        s1 = p.sbuf("l1a_s1", [128, 1, 512], F32)
        b_s1 = [p.buf("l1a_s10")]
        kb = p.sbuf("l1a_kb", [128, 4, 512], BF16)
        b_kb = [p.buf(f"l1a_kb{i}") for i in range(4)]
        vb = p.sbuf("l1a_vb", [128, 2, 512], BF16)
        b_vb = [p.buf(f"l1a_vb{i}") for i in range(2)]
        kvb = p.sbuf("l1a_kvb", [128, 2, 512], BF16)
        b_kvb = [p.buf(f"l1a_kvb{i}") for i in range(2)]

        for g in range(NG):
            ln.emit(f"l1a{g}ln", TG,
                    src=lambda c: (xacc(c)[0][:, g * TG:(g + 1) * TG], xacc(c)[1]),
                    dst=lambda c: (xln[:, c, :], b_xln[c]))
            for nb in NBORDER:
                kind = nb // 4
                co = nb % 4
                pqs = [p.bank(t) for t in range(4)]
                for kc in range(DC):
                    r = kc % 2
                    p.emit("gpsimd", lambda e, kc_=kc, nb_=nb, r_=r:
                           e.dma_start(
                               out=wt[:, r_, :],
                               in_=io["wqkv"][kc_ * 128:(kc_ + 1) * 128,
                                              nb_ * 512:(nb_ + 1) * 512]),
                           writes=[b_wt[r]], dma=True,
                           tag=f"l1a{g}w{nb}_{kc}")
                    for tt in range(4):
                        p.emit("tensor", lambda e, kc_=kc, tt_=tt, r_=r,
                               pq_=pqs[tt][0]: e.matmul(
                                   pq_[:, 0:512],
                                   xln[:, kc_, tt_ * 128:(tt_ + 1) * 128],
                                   wt[:, r_, :],
                                   start=(kc_ == 0), stop=(kc_ == DC - 1)),
                               reads=[b_wt[r], b_xln[kc]], writes=[pqs[tt][1]],
                               tag=f"l1a{g}m{nb}_{tt}_{kc}")
                for tt in range(4):
                    cc = 4 * g + tt
                    pq, b_pq = pqs[tt]
                    rr = tt % 2
                    if kind == 0:
                        _phi_evac(p, pq, b_pq, s1, b_s1, kvb, b_kvb, rr,
                                  f"l1a{g}q{nb}_{tt}")
                        p.emit("gpsimd", lambda e, cc_=cc, co_=co, r_=rr:
                               e.dma_start(
                                   out=io["scr_q"][cc_, :,
                                                   co_ * 512:(co_ + 1) * 512],
                                   in_=kvb[:, r_, :]),
                               reads=[b_kvb[rr]], dma=True,
                               tag=f"l1a{g}qs{nb}_{tt}")
                    elif kind == 1:
                        _phi_evac(p, pq, b_pq, s1, b_s1, kb, b_kb, tt,
                                  f"l1a{g}k{nb}_{tt}")
                        p.emit("gpsimd", lambda e, cc_=cc, co_=co, tt_=tt:
                               e.dma_start(
                                   out=io["scr_k"][cc_, :,
                                                   co_ * 512:(co_ + 1) * 512],
                                   in_=kb[:, tt_, :]),
                               reads=[b_kb[tt]], dma=True,
                               tag=f"l1a{g}ks{nb}_{tt}")
                    else:
                        p.emit("vector", lambda e, tt_=tt, r_=rr, pq_=pq:
                               e.tensor_copy(vb[:, r_, :], pq_[:, 0:512]),
                               reads=[b_pq], writes=[b_vb[rr]],
                               tag=f"l1a{g}v{nb}_{tt}")
                        p.emit("vector", lambda e, tt_=tt, r_=rr:
                               e.tensor_tensor(
                                   kvb[:, r_, :], kb[:, tt_, :], vb[:, r_, :],
                                   op=ALU.mult),
                               reads=[b_kb[tt], b_vb[rr]], writes=[b_kvb[rr]],
                               tag=f"l1a{g}kv{nb}_{tt}")
                        p.emit("gpsimd", lambda e, cc_=cc, co_=co, r_=rr:
                               e.dma_start(
                                   out=io["scr_kv"][cc_, :,
                                                    co_ * 512:(co_ + 1) * 512],
                                   in_=kvb[:, r_, :]),
                               reads=[b_kvb[rr]], dma=True,
                               tag=f"l1a{g}kvs{nb}_{tt}")
                        p4, b_p4 = p.bank(4)
                        p5, b_p5 = p.bank(5)
                        for hh in range(4):
                            h = co * 4 + hh
                            p.emit("tensor", lambda e, h_=h, hh_=hh, cc_=cc,
                                   r_=rr, ps_=p4: e.matmul(
                                       ps_[0:1, 128 * hh_:128 * hh_ + 128],
                                       dvall[:, cc_, h_:h_ + 1],
                                       kvb[:, r_, 128 * hh_:128 * hh_ + 128],
                                       start=True, stop=True),
                                   reads=[b_kvb[rr], b_dvall],
                                   writes=[b_p4], tag=f"l1a{g}st{nb}_{tt}_{hh}")
                            p.emit("tensor", lambda e, h_=h, hh_=hh, cc_=cc,
                                   tt_=tt, ps_=p5: e.matmul(
                                       ps_[0:1, 128 * hh_:128 * hh_ + 128],
                                       dvall[:, cc_, h_:h_ + 1],
                                       kb[:, tt_, 128 * hh_:128 * hh_ + 128],
                                       start=True, stop=True),
                                   reads=[b_kb[tt], b_dvall],
                                   writes=[b_p5], tag=f"l1a{g}su{nb}_{tt}_{hh}")
                        cols = slice(co * 512, co * 512 + 512)
                        p.emit("vector", lambda e, cols_=cols, ps_=p4:
                               e.tensor_tensor(gkv[0:1, cols_],
                                               gkv[0:1, cols_],
                                               ps_[0:1, 0:512], op=ALU.add),
                               reads=[b_p4], writes=[b_gkv],
                               tag=f"l1a{g}ga{nb}_{tt}")
                        p.emit("vector", lambda e, cols_=cols, ps_=p5:
                               e.tensor_tensor(gks[0:1, cols_],
                                               gks[0:1, cols_],
                                               ps_[0:1, 0:512], op=ALU.add),
                               reads=[b_p5], writes=[b_gks],
                               tag=f"l1a{g}gb{nb}_{tt}")


def _phi_evac(p, pq, b_pq, s1, b_s1, dst, b_dst, slot, tag):
    r = 0
    p.emit("vector", lambda e, r_=r, pq_=pq: e.tensor_scalar(
        s1[:, r_, :], pq_[:, 0:512], 0.0, None, op0=ALU.min),
        reads=[b_pq], writes=[b_s1[r]], tag=tag + "min")
    p.emit("scalar", lambda e, r_=r: e.activation(
        s1[:, r_, :], s1[:, r_, :], AF.Exp),
        reads=[b_s1[r]], writes=[b_s1[r]], tag=tag + "exp")
    p.emit("vector", lambda e, r_=r, s_=slot, pq_=pq: e.scalar_tensor_tensor(
        dst[:, s_, :], pq_[:, 0:512], 0.0, s1[:, r_, :],
        op0=ALU.max, op1=ALU.add),
        reads=[b_pq, b_s1[r]], writes=[b_dst[slot]], tag=tag + "stt")


def _l1_exchange(p, C, io, gkv, b_gkv, gks, b_gks, Tkv, b_Tkv, Tks, b_Tks):
    with p.scope():
        _l1_exchange2(p, C, io, gkv, b_gkv, gks, b_gks, Tkv, b_Tkv, Tks, b_Tks)


def _l1_exchange2(p, C, io, gkv, b_gkv, gks, b_gks, Tkv, b_Tkv, Tks, b_Tks):
    selb = p.sbuf("la_sel", [16, 2], F32)
    b_selb = p.buf("la_sel")
    gath = p.sbuf("la_gath", [16, 2048], F32)
    b_gath = p.buf("la_gath")
    p.emit("gpsimd", lambda g: g.dma_start(out=selb[:], in_=io["sel"][:, :]),
           writes=[b_selb], dma=True, tag="la_ldsel")
    w0 = p.emit("gpsimd", lambda g: g.dma_start(out=io["cc_in"][0:1, :],
                in_=gkv[:]), reads=[b_gkv], dma=True, tag="la_ccw0")
    w1 = p.emit("gpsimd", lambda g: g.dma_start(out=io["cc_in"][1:2, :],
                in_=gks[:]), reads=[b_gks], dma=True, tag="la_ccw1")
    cc_op = p.emit("gpsimd", lambda g: g.collective_compute(
        "AllGather", ALU.bypass, replica_groups=[list(range(NCORES))],
        ins=[io["cc_in"].opt()], outs=[io["cc_out"].opt()]),
        deps=[w0, w1], tag="la_cc")
    p.emit("gpsimd", lambda g: g.dma_start(out=gath[:], in_=io["cc_out"][:, :]),
           deps=[cc_op], writes=[b_gath], dma=True, tag="la_ccr")
    for j in range(4):
        pr, b_pr = p.bank(j % 2)
        pr2, b_pr2 = p.bank(2 + j % 2)
        p.emit("tensor", lambda e, j_=j, pr_=pr: e.matmul(
            pr_[0:1, 0:512], selb[:, 0:1], gath[:, j_ * 512:(j_ + 1) * 512],
            start=True, stop=True),
            reads=[b_selb, b_gath], writes=[b_pr], tag=f"la_sm{j}")
        p.emit("tensor", lambda e, j_=j, pr_=pr2: e.matmul(
            pr_[0:1, 0:512], selb[:, 1:2], gath[:, j_ * 512:(j_ + 1) * 512],
            start=True, stop=True),
            reads=[b_selb, b_gath], writes=[b_pr2], tag=f"la_sn{j}")
        p.emit("vector", lambda e, j_=j, pr_=pr: e.tensor_copy(
            Tkv[0:1, 512 * j_:512 * (j_ + 1)], pr_[0:1, 0:512]),
            reads=[b_pr], writes=[b_Tkv], tag=f"la_se0{j}")
        p.emit("vector", lambda e, j_=j, pr_=pr2: e.tensor_copy(
            Tks[0:1, 512 * j_:512 * (j_ + 1)], pr_[0:1, 0:512]),
            reads=[b_pr2], writes=[b_Tks], tag=f"la_se1{j}")


def _l1_p2(p, C, io, Ah, b_Ah, dprow, b_dprow, d128b, b_d128b,
           Tkv, b_Tkv, Tks, b_Tks):
    with p.scope():
        kvt = p.sbuf("l1b_kvt", [128, 1, 2048], BF16)
        b_kvt = [p.buf("l1b_kvt0")]
        kt = p.sbuf("l1b_kt", [128, 1, 2048], BF16)
        b_kt = [p.buf("l1b_kt0")]
        qt = p.sbuf("l1b_qt", [128, 2048], BF16)
        b_qt = p.buf("l1b_qt")
        Tbf = p.sbuf("l1b_Tbf", [1, 2, 2048], BF16)
        b_Tbf = p.buf("l1b_Tbf")
        tmp = p.sbuf("l1b_tmp", [128, 512], F32)
        b_tmp = p.buf("l1b_tmp")
        den = p.sbuf("l1b_den", [128, 2, LH], F32)
        b_den = p.buf("l1b_den")
        atk = p.sbuf("l1b_atk", [128, 2, 512], BF16)
        b_atk = [p.buf(f"l1b_atk{i}") for i in range(2)]

        for cc in range(16):
            r = 0
            p.emit("sync", lambda e, cc_=cc, r_=r: e.dma_start(
                out=kvt[:, r_, :], in_=io["scr_kv"][cc_, :, :]),
                writes=[b_kvt[r]], dma=True, tag=f"l1b{cc}lkv")
            p.emit("sync", lambda e, cc_=cc, r_=r: e.dma_start(
                out=kt[:, r_, :], in_=io["scr_k"][cc_, :, :]),
                writes=[b_kt[r]], dma=True, tag=f"l1b{cc}lk")
            p.emit("sync", lambda e, cc_=cc: e.dma_start(
                out=qt[:], in_=io["scr_q"][cc_, :, :]),
                writes=[b_qt], dma=True, tag=f"l1b{cc}lq")
            p.emit("gpsimd", lambda g: g.dma_start(out=Tbf[0:1, 0, :],
                   in_=Tkv[:]), reads=[b_Tkv], writes=[b_Tbf], dma=True,
                   tag=f"l1b{cc}cb0")
            p.emit("gpsimd", lambda g: g.dma_start(out=Tbf[0:1, 1, :],
                   in_=Tks[:]), reads=[b_Tks], writes=[b_Tbf], dma=True,
                   tag=f"l1b{cc}cb1")
            for sc in range(2):
                srcb = (b_kvt if sc == 0 else b_kt)[r]
                srct = kvt if sc == 0 else kt
                for h in range(LH):
                    bk, b_bk = p.bank(4 * sc + h // 4)
                    colr = slice(128 * (h % 4), 128 * (h % 4) + 128)
                    p.emit("tensor", lambda e, h_=h, colr_=colr, bk_=bk,
                           st_=srct, r_=r: e.matmul(
                               bk_[:, colr_], Ah[:, h_, :],
                               st_[:, r_, 128 * h_:128 * h_ + 128],
                               start=True, stop=False),
                           reads=[srcb, b_Ah], writes=[b_bk],
                           tag=f"l1b{cc}A{sc}_{h}")
                    p.emit("tensor", lambda e, h_=h, sc_=sc, colr_=colr,
                           bk_=bk: e.matmul(
                               bk_[:, colr_], dprow[:, h_, :],
                               Tbf[0:1, sc_, 128 * h_:128 * h_ + 128],
                               start=False, stop=True),
                           reads=[b_dprow, b_Tbf], writes=[b_bk],
                           tag=f"l1b{cc}C{sc}_{h}")
            for j in range(4):
                bk, b_bk = p.bank(4 + j)
                p.emit("vector", lambda e, j_=j, bk_=bk: e.tensor_tensor(
                    tmp[:], qt[:, 512 * j_:512 * (j_ + 1)], bk_[:, :],
                    op=ALU.mult), reads=[b_qt, b_bk], writes=[b_tmp],
                    tag=f"l1b{cc}dm{j}")
                p.emit("vector", lambda e, j_=j: e.tensor_reduce(
                    den[:, 0, 4 * j_:4 * j_ + 4],
                    tmp[:].rearrange("p (h d) -> p h d", d=128),
                    axis=AX.X, op=ALU.add),
                    reads=[b_tmp], writes=[b_den], tag=f"l1b{cc}dr{j}")
            p.emit("vector", lambda e: e.tensor_scalar(
                den[:, 0, :], den[:, 0, :], 1e-6, None, op0=ALU.max),
                writes=[b_den], tag=f"l1b{cc}dc")
            p.emit("vector", lambda e: e.reciprocal(den[:, 1, :], den[:, 0, :]),
                   reads=[b_den], writes=[b_den], tag=f"l1b{cc}di")
            for j in range(4):
                bk, b_bk = p.bank(j)
                p.emit("vector", lambda e, j_=j, bk_=bk: e.tensor_tensor(
                    tmp[:], qt[:, 512 * j_:512 * (j_ + 1)], bk_[:, :],
                    op=ALU.mult), reads=[b_qt, b_bk], writes=[b_tmp],
                    tag=f"l1b{cc}om{j}")
                p.emit("vector", lambda e, j_=j: e.tensor_tensor(
                    atk[:, j_ % 2, :].rearrange("p (h d) -> p h d", d=128),
                    tmp[:].rearrange("p (h d) -> p h d", d=128),
                    den[:, 1, 4 * j_:4 * j_ + 4].unsqueeze(2)
                    .broadcast_to([128, 4, 128]),
                    op=ALU.mult),
                    reads=[b_tmp, b_den], writes=[b_atk[j % 2]],
                    tag=f"l1b{cc}oo{j}")
                p.emit("gpsimd", lambda e, cc_=cc, j_=j: e.dma_start(
                    out=io["scr_at"][cc_, :, 512 * j_:512 * (j_ + 1)],
                    in_=atk[:, j_ % 2, :]),
                    reads=[b_atk[j % 2]], dma=True, tag=f"l1b{cc}os{j}")
            if cc < 15:
                for sc in range(2):
                    srcb = (b_kvt if sc == 0 else b_kt)[r]
                    srct = kvt if sc == 0 else kt
                    for h in range(LH):
                        bk, b_bk = p.bank(4 * sc + h // 4)
                        colr = slice(128 * (h % 4), 128 * (h % 4) + 128)
                        p.emit("tensor", lambda e, h_=h, colr_=colr, bk_=bk,
                               st_=srct, r_=r: e.matmul(
                                   bk_[0:1, colr_], Ah[:, h_, 127:128],
                                   st_[:, r_, 128 * h_:128 * h_ + 128],
                                   start=True, stop=False),
                               reads=[srcb, b_Ah], writes=[b_bk],
                               tag=f"l1b{cc}sv{sc}_{h}")
                        p.emit("tensor", lambda e, h_=h, sc_=sc, colr_=colr,
                               bk_=bk: e.matmul(
                                   bk_[0:1, colr_], d128b[:, h_:h_ + 1],
                                   Tbf[0:1, sc_, 128 * h_:128 * h_ + 128],
                                   start=False, stop=True),
                               reads=[b_d128b, b_Tbf], writes=[b_bk],
                               tag=f"l1b{cc}sw{sc}_{h}")
                for j in range(4):
                    b0, bb0 = p.bank(j)
                    b1, bb1 = p.bank(4 + j)
                    p.emit("vector", lambda e, j_=j, b_=b0: e.tensor_copy(
                        Tkv[0:1, 512 * j_:512 * (j_ + 1)], b_[0:1, 0:512]),
                        reads=[bb0], writes=[b_Tkv], tag=f"l1b{cc}te0{j}")
                    p.emit("vector", lambda e, j_=j, b_=b1: e.tensor_copy(
                        Tks[0:1, 512 * j_:512 * (j_ + 1)], b_[0:1, 0:512]),
                        reads=[bb1], writes=[b_Tks], tag=f"l1b{cc}te1{j}")


def _l1_p3(p, C, ln, io, xacc):
    with p.scope():
        ain = p.sbuf("l1c_ain", [128, 1, 2048], BF16)
        b_ain = [p.buf("l1c_ain0")]
        afm = p.sbuf("l1c_afm", [128, DC, 256], BF16)
        b_afm = [p.buf(f"l1c_afm{i}") for i in range(DC)]
        wout = p.sbuf("l1c_wout", [128, 2, DC, 128], BF16)
        b_wout = [p.buf(f"l1c_wout{i}") for i in range(2)]
        ysb = p.sbuf("l1c_y", [128, DC, 256], F32)
        b_ysb = [p.buf(f"l1c_y{c}") for c in range(DC)]

        for grp in range(8):
            for ti in range(2):
                cc = 2 * grp + ti
                r = 0
                p.emit("sync", lambda e, cc_=cc, r_=r: e.dma_start(
                    out=ain[:, r_, :], in_=io["scr_at"][cc_, :, :]),
                    writes=[b_ain[r]], dma=True, tag=f"l1c{cc}l")
                for c in range(DC):
                    pt, b_pt = p.bank(c % 2)
                    p.emit("tensor", lambda e, c_=c, r_=r, pt_=pt: e.transpose(
                        pt_.bitcast(BF16)[:, 0:128],
                        ain[:, r_, c_ * 128:(c_ + 1) * 128], C.ident[:]),
                        reads=[b_ain[r], C.b_ident], writes=[b_pt],
                        tag=f"l1c{cc}t{c}")
                    p.emit("vector", lambda e, c_=c, ti_=ti, pt_=pt:
                           e.tensor_copy(
                               afm[:, c_, ti_ * 128:(ti_ + 1) * 128],
                               pt_.bitcast(BF16)[:, 0:128]),
                           reads=[b_pt], writes=[b_afm[c]], tag=f"l1c{cc}e{c}")
            for oc in range(DC):
                r = oc % 2
                po, b_po = p.bank(2 + r)
                p.emit("gpsimd", lambda e, oc_=oc, r_=r: e.dma_start(
                    out=wout[:, r_, :, :],
                    in_=io["wout"][:, oc_ * 128:(oc_ + 1) * 128]
                    .rearrange("(c p) n -> p c n", p=128)),
                    writes=[b_wout[r]], dma=True, tag=f"l1c{grp}w{oc}")
                for k in range(DC):
                    p.emit("tensor", lambda e, k_=k, r_=r, po_=po: e.matmul(
                        po_[:, 0:256], wout[:, r_, k_, :], afm[:, k_, :],
                        start=(k_ == 0), stop=(k_ == DC - 1)),
                        reads=[b_wout[r], b_afm[k]], writes=[b_po],
                        tag=f"l1c{grp}m{oc}_{k}")
                p.emit("vector", lambda e, oc_=oc, po_=po: e.tensor_copy(
                    ysb[:, oc_, :], po_[:, 0:256]),
                    reads=[b_po], writes=[b_ysb[oc]], tag=f"l1c{grp}ev{oc}")
            ln.emit(f"l1c{grp}ln2", 256,
                    src=lambda c: (ysb[:, c, :], b_ysb[c]),
                    dst=None,
                    resid=lambda c: (xacc(c)[0][:, grp * 256:(grp + 1) * 256],
                                     xacc(c)[1]))



MTG = 1024
NIC = {0: 11, 1: 11, 2: 11, 3: 11, 4: 11, 5: 11, 6: 11, 7: 11, 8: 22}


def emit_l1_ffn(p, C, ln, io, xacc):
    b_scrw = p.buf("mf_scrw")
    b_scrw3 = p.buf("mf_scrw3")
    p.b_scrw, p.b_scrw3 = b_scrw, b_scrw3
    for w in range(2):
        for e in range(8):
            p.emit("gpsimd", lambda g, w_=w, e_=e: g.dma_start(
                out=io["scr_w12"][w_, e_],
                in_=io["w12t"][w_, e_].rearrange("a p k n -> a p (k n)")),
                writes=[b_scrw], dma=True, tag=f"mfW{w}_{e}")
        for hf in range(2):
            p.emit("gpsimd", lambda g, w_=w, hf_=hf: g.dma_start(
                out=io["scr_sw12"][w_, hf_ * 11:(hf_ + 1) * 11],
                in_=io["sw12t"][w_, hf_ * 11:(hf_ + 1) * 11]
                .rearrange("a p k n -> a p (k n)")),
                writes=[b_scrw], dma=True, tag=f"mfWs{w}_{hf}")
    for j in range(10):
        p.emit("gpsimd", lambda g, j_=j: g.dma_start(
            out=io["scr_w3"][j_ * 11:(j_ + 1) * 11],
            in_=io["w3catt"][j_ * 11:(j_ + 1) * 11]
            .rearrange("a p k n -> a p (k n)")),
            writes=[b_scrw3], dma=True, tag=f"mfW3_{j}")

    combT = p.sbuf("mf_combT", [8, 2048], BF16)
    b_combT = p.buf("mf_combT")
    eye8 = p.sbuf("mf_eye8", [8, 8, 128], BF16)
    b_eye8 = p.buf("mf_eye8")
    p.emit("gpsimd", lambda g: g.memset(eye8[:], 0.0), writes=[b_eye8],
           tag="mf_eye0")
    p.emit("gpsimd", lambda g: g.affine_select(
        out=eye8[:], in_=eye8[:], compare_op=ALU.not_equal, fill=1.0,
        base=0, pattern=[[-1, 8], [0, 128]], channel_multiplier=1),
        writes=[b_eye8], tag="mf_eye1")
    for g in range(2):
        _l1f_group(p, C, ln, io, xacc, g, combT, b_combT, eye8, b_eye8)
    _l1f_y(p, C, ln, io, xacc)


def _l1f_group(p, C, ln, io, xacc, g, combT, b_combT, eye8, b_eye8):
    with p.scope():
        xln = p.sbuf(f"mf{g}_xln", [128, DC, MTG], BF16)
        b_xln = [p.buf(f"mf{g}_xln{c}") for c in range(DC)]
        gw = p.sbuf(f"mf{g}_gw", [128, DC, 8], BF16)
        b_gw = p.buf(f"mf{g}_gw")
        lg = p.sbuf(f"mf{g}_lg", [128, 6, 8], F32)
        b_lg = p.buf(f"mf{g}_lg")
        lrow = p.sbuf(f"mf{g}_lrow", [128, 4], F32)
        b_lrow = p.buf(f"mf{g}_lrow")
        lgb = p.sbuf(f"mf{g}_lgb", [128, 8], BF16)
        b_lgb = p.buf(f"mf{g}_lgb")
        wt = p.sbuf(f"mf{g}_wt", [128, 2, DC, 128], BF16)
        b_wt = [p.buf(f"mf{g}_wt{i}") for i in range(2)]
        cb = p.sbuf(f"mf{g}_cb", [128, 2, 512], BF16)
        b_cb = [p.buf(f"mf{g}_cb{i}") for i in range(2)]
        sv = p.sbuf(f"mf{g}_sv", [128, 512], F32)
        b_sv = p.buf(f"mf{g}_sv")
        uv = p.sbuf(f"mf{g}_uv", [128, 1, 512], F32)
        b_uv = [p.buf(f"mf{g}_uv0"), None]
        hv = p.sbuf(f"mf{g}_hv", [128, 2, 512], BF16)
        b_hv = [p.buf(f"mf{g}_hv{i}") for i in range(2)]

        p.emit("gpsimd", lambda e: e.dma_start(
            out=gw[:],
            in_=io["gw_t"][:, :].rearrange("(c p) n -> p c n", p=128)),
            writes=[b_gw], dma=True, tag=f"mf{g}gw")
        for hf in range(2):
            ln.emit(f"mf{g}ln{hf}", 512,
                    src=lambda c: (xacc(c)[0][:, g * MTG + hf * 512:
                                              g * MTG + (hf + 1) * 512],
                                   xacc(c)[1]),
                    dst=lambda c: (xln[:, c, hf * 512:(hf + 1) * 512],
                                   b_xln[c]))
        for tt in range(8):
            pgt, b_pgt = p.bank(tt % 2)
            for k in range(DC):
                p.emit("tensor", lambda e, k_=k, tt_=tt, pg_=pgt: e.matmul(
                    pg_[:, 0:8], xln[:, k_, tt_ * 128:(tt_ + 1) * 128],
                    gw[:, k_, :], start=(k_ == 0), stop=(k_ == DC - 1)),
                    reads=[b_gw, b_xln[k]], writes=[b_pgt],
                    tag=f"mf{g}g{tt}_{k}")
            p.emit("vector", lambda e, pg_=pgt: e.tensor_copy(
                lg[:, 0, :], pg_[:, 0:8]), reads=[b_pgt], writes=[b_lg],
                tag=f"mf{g}s0_{tt}")
            p.emit("vector", lambda e: e.tensor_reduce(
                lrow[:, 0:1], lg[:, 0, :], axis=AX.X, op=ALU.max, negate=True),
                reads=[b_lg], writes=[b_lrow], tag=f"mf{g}s1_{tt}")
            p.emit("scalar", lambda e: e.activation(
                lg[:, 1, :], lg[:, 0, :], AF.Exp, bias=lrow[:, 0:1]),
                reads=[b_lg], slots=[b_lrow], writes=[b_lg],
                tag=f"mf{g}s2_{tt}")
            p.emit("vector", lambda e: e.tensor_reduce(
                lrow[:, 1:2], lg[:, 1, :], axis=AX.X, op=ALU.add),
                reads=[b_lg], writes=[b_lrow], tag=f"mf{g}s3_{tt}")
            p.emit("vector", lambda e: e.reciprocal(
                lrow[:, 3:4], lrow[:, 1:2]), reads=[b_lrow], writes=[b_lrow],
                tag=f"mf{g}s4_{tt}")
            p.emit("vector", lambda e: e.tensor_scalar(
                lg[:, 2, :], lg[:, 1, :], lrow[:, 3:4], None, op0=ALU.mult),
                reads=[b_lg], slots=[b_lrow], writes=[b_lg],
                tag=f"mf{g}s5_{tt}")
            p.emit("vector", lambda e: e.tensor_reduce(
                lrow[:, 0:1], lg[:, 2, :], axis=AX.X, op=ALU.max),
                reads=[b_lg], writes=[b_lrow], tag=f"mf{g}t0_{tt}")
            p.emit("vector", lambda e: e.tensor_scalar(
                lg[:, 3, :], lg[:, 2, :], lrow[:, 0:1], None, op0=ALU.is_lt),
                reads=[b_lg], slots=[b_lrow], writes=[b_lg],
                tag=f"mf{g}t1_{tt}")
            p.emit("vector", lambda e: e.tensor_tensor(
                lg[:, 4, :], lg[:, 2, :], lg[:, 3, :], op=ALU.mult),
                writes=[b_lg], tag=f"mf{g}t2_{tt}")
            p.emit("vector", lambda e: e.tensor_reduce(
                lrow[:, 2:3], lg[:, 4, :], axis=AX.X, op=ALU.max),
                reads=[b_lg], writes=[b_lrow], tag=f"mf{g}t3_{tt}")
            p.emit("vector", lambda e: e.tensor_scalar(
                lg[:, 5, :], lg[:, 2, :], lrow[:, 2:3], None, op0=ALU.is_ge),
                reads=[b_lg], slots=[b_lrow], writes=[b_lg],
                tag=f"mf{g}t4_{tt}")
            p.emit("vector", lambda e: e.tensor_tensor(
                lg[:, 5, :], lg[:, 5, :], lg[:, 2, :], op=ALU.mult),
                writes=[b_lg], tag=f"mf{g}t5_{tt}")
            p.emit("vector", lambda e: e.tensor_copy(
                lg[:, 5, :].bitcast(U16)[:, 0:8].bitcast(BF16) if False else
                lgb[:, :], lg[:, 5, :]),
                reads=[b_lg], writes=[b_lgb], tag=f"mf{g}tb{tt}")
            ptc, b_ptc = p.bank(2 + tt % 2)
            p.emit("tensor", lambda e, pt_=ptc: e.transpose(
                pt_.bitcast(BF16)[0:8, 0:128], lgb[:, :], C.ident[:]),
                reads=[b_lgb, C.b_ident], writes=[b_ptc],
                tag=f"mf{g}tc{tt}")
            p.emit("vector", lambda e, g_=g, tt_=tt, pt_=ptc: e.tensor_copy(
                combT[:, g_ * MTG + tt_ * 128:g_ * MTG + (tt_ + 1) * 128],
                pt_.bitcast(BF16)[0:8, 0:128]),
                reads=[b_ptc], writes=[b_combT], tag=f"mf{g}te{tt}")
        for e9 in range(9):
            for tt in range(2):
                if e9 < 8:
                    pcb, b_pcb = p.bank(4 + tt % 2)
                    p.emit("tensor", lambda e, e_=e9, tt_=tt, g_=g, ps_=pcb:
                           e.matmul(ps_[:, 0:512], eye8[:, e_, :],
                                    combT[:, g_ * MTG + tt_ * 512:
                                          g_ * MTG + (tt_ + 1) * 512],
                                    start=True, stop=True),
                           reads=[b_combT, b_eye8], writes=[b_pcb],
                           tag=f"mf{g}cb{e9}_{tt}")
                    p.emit("vector", lambda e, tt_=tt, ps_=pcb: e.tensor_copy(
                        cb[:, tt_, :], ps_[:, 0:512]),
                        reads=[b_pcb], writes=[b_cb[tt]],
                        tag=f"mf{g}cbe{e9}_{tt}")
            for ic in range(NIC[e9]):
                row = e9 * 11 + ic if e9 < 8 else 88 + ic
                for w in range(2):
                    r = w
                    src = (io["scr_w12"][w, e9, ic] if e9 < 8 else
                           io["scr_sw12"][w, ic])
                    p.emit("sync", lambda e, r_=r, src_=src: e.dma_start(
                        out=wt[:, r_, :, :].rearrange("p c n -> p (c n)"),
                        in_=src_),
                        reads=[p.b_scrw], writes=[b_wt[r]], dma=True,
                        tag=f"mf{g}wl{e9}_{ic}_{w}")
                    for tt in range(2):
                        ph, b_ph = p.bank(w * 2 + tt)
                        for k in range(DC):
                            p.emit("tensor", lambda e, k_=k, tt_=tt, r_=r,
                                   ph_=ph: e.matmul(
                                       ph_[:, 0:512], wt[:, r_, k_, :],
                                       xln[:, k_, tt_ * 512:(tt_ + 1) * 512],
                                       start=(k_ == 0), stop=(k_ == DC - 1)),
                                   reads=[b_wt[r], b_xln[k]], writes=[b_ph],
                                   tag=f"mf{g}hm{e9}_{ic}_{w}_{tt}_{k}")
                for tt in range(2):
                    p1, b_p1 = p.bank(tt)
                    p2, b_p2 = p.bank(2 + tt)
                    p.emit("scalar", lambda e, ps_=p1: e.activation(
                        sv[:, :], ps_[:, 0:512], AF.Silu),
                        reads=[b_p1], writes=[b_sv], tag=f"mf{g}si{e9}_{ic}_{tt}")
                    p.emit("vector", lambda e, ps_=p2: e.tensor_tensor(
                        uv[:, 0, :], sv[:, :], ps_[:, 0:512], op=ALU.mult),
                        reads=[b_sv, b_p2], writes=[b_uv[0]],
                        tag=f"mf{g}u{e9}_{ic}_{tt}")
                    if e9 < 8:
                        p.emit("gpsimd", lambda e, tt_=tt: e.tensor_tensor(
                            hv[:, tt_, :], uv[:, 0, :], cb[:, tt_, :],
                            op=ALU.mult),
                            reads=[b_uv[0], b_cb[tt]], writes=[b_hv[tt]],
                            tag=f"mf{g}hc{e9}_{ic}_{tt}")
                    else:
                        p.emit("gpsimd", lambda e, tt_=tt: e.tensor_copy(
                            hv[:, tt_, :], uv[:, 0, :]),
                            reads=[b_uv[0]], writes=[b_hv[tt]],
                            tag=f"mf{g}hc{e9}_{ic}_{tt}")
                    p.emit("gpsimd", lambda e, row_=row, g_=g, tt_=tt:
                           e.dma_start(
                               out=io["scr_H"][row_, :,
                                               g_ * MTG + tt_ * 512:
                                               g_ * MTG + (tt_ + 1) * 512],
                               in_=hv[:, tt_, :]),
                           reads=[b_hv[tt]], dma=True,
                           tag=f"mf{g}hs{e9}_{ic}_{tt}")


def _l1f_y(p, C, ln, io, xacc):
    with p.scope():
        hrow = p.sbuf("mfy_h", [128, 2, 512], BF16)
        b_hrow = [p.buf(f"mfy_h{i}") for i in range(2)]
        w3 = p.sbuf("mfy_w3", [128, 2, 8, 128], BF16)
        b_w3 = [p.buf(f"mfy_w3{i}") for i in range(2)]
        ysb = p.sbuf("mfy_y", [128, DC, 512], F32)
        b_ysb = [p.buf(f"mfy_y{c}") for c in range(DC)]

        for tt in range(4):
            for oh in range(2):
                pys = [p.bank(i) for i in range(8)]
                for kc in range(110):
                    r = kc % 2
                    p.emit("sync", lambda e, kc_=kc, tt_=tt, r_=r: e.dma_start(
                        out=hrow[:, r_, :],
                        in_=io["scr_H"][kc_, :, tt_ * 512:(tt_ + 1) * 512]),
                        writes=[b_hrow[r]], dma=True, tag=f"mfy{tt}_{oh}l{kc}")
                    p.emit("sync", lambda e, kc_=kc, oh_=oh, r_=r: e.dma_start(
                        out=w3[:, r_, :, :].rearrange("p c n -> p (c n)"),
                        in_=io["scr_w3"][kc_, :, oh_ * 1024:(oh_ + 1) * 1024]),
                        reads=[p.b_scrw3], writes=[b_w3[r]], dma=True,
                        tag=f"mfy{tt}_{oh}w{kc}")
                    for oc in range(8):
                        p.emit("tensor", lambda e, kc_=kc, oc_=oc, r_=r,
                               ps_=pys[oc][0]: e.matmul(
                                   ps_[:, 0:512], w3[:, r_, oc_, :],
                                   hrow[:, r_, :],
                                   start=(kc_ == 0), stop=(kc_ == 109)),
                               reads=[b_w3[r], b_hrow[r]], writes=[pys[oc][1]],
                               tag=f"mfy{tt}_{oh}m{kc}_{oc}")
                for oc in range(8):
                    c = oh * 8 + oc
                    p.emit("vector", lambda e, c_=c, ps_=pys[oc][0]:
                           e.tensor_copy(ysb[:, c_, :], ps_[:, 0:512]),
                           reads=[pys[oc][1]], writes=[b_ysb[c]],
                           tag=f"mfy{tt}_{oh}e{oc}")
            ln.emit(f"mfy{tt}ln", 512,
                    src=lambda c: (ysb[:, c, :], b_ysb[c]),
                    dst=None,
                    resid=lambda c: (xacc(c)[0][:, tt * 512:(tt + 1) * 512],
                                     xacc(c)[1]))



def _t(a):
    return np.ascontiguousarray(np.transpose(a))


def _tile_w12(w1, w2):
    out = []
    for w in (w1, w2):
        wt = np.transpose(np.asarray(w), (0, 2, 1))
        out.append(wt.reshape(8, 16, 128, 11, 128).transpose(0, 3, 2, 1, 4))
    return np.ascontiguousarray(np.stack(out), dtype=np.float32)


def _tile_sw12(sw1, sw2):
    out = []
    for w in (sw1, sw2):
        wt = _t(np.asarray(w))
        out.append(wt.reshape(16, 128, 22, 128).transpose(2, 1, 0, 3))
    return np.ascontiguousarray(np.stack(out), dtype=np.float32)


def _tile_w3(w3, sw3):
    w3t = np.transpose(np.asarray(w3), (0, 2, 1)).reshape(11264, 2048)
    cat = np.concatenate([w3t, _t(np.asarray(sw3))], axis=0)
    return np.ascontiguousarray(cat.reshape(110, 128, 16, 128),
                                dtype=np.float32)


def _pad_qk(w_t):
    w = np.zeros((D, 1024), dtype=np.float32)
    for h in range(NHEAD):
        base = 128 * (h // 4) + 32 * (h % 4)
        w[:, base:base + 8] = w_t[:, 8 * h:8 * h + 8]
    return w


def shard_inputs(inputs):
    x = inputs["L0_attn_hidden_states"]
    w_shared = {
        "wqp": _pad_qk(_t(inputs["L0_attn_Wq"])),
        "wkp": _pad_qk(_t(inputs["L0_attn_Wk"])),
        "wv_t": _t(inputs["L0_attn_Wv"]),
        "wo_t": _t(inputs["L0_attn_Wo"]),
        "wg_t": _t(inputs["L0_attn_Wg"]),
        "vemb": np.stack([np.asarray(inputs["L0_attn_v_emb0"]),
                          np.asarray(inputs["L0_attn_v_emb1"])]),
        "wgu_t": _t(inputs["L0_ffn_w_gate_up"]),
        "wpr_t": _t(inputs["L0_ffn_w_proj"]),
        "wqkv": np.ascontiguousarray(inputs["L1_attn_w_qkv"]),
        "wout": np.ascontiguousarray(inputs["L1_attn_w_out"]),
        "decay": np.asarray(inputs["L1_attn_decay_param"]).reshape(1, 16),
        "gw_t": _t(inputs["L1_ffn_gate_w"]),
        "w12t": _tile_w12(inputs["L1_ffn_w1"], inputs["L1_ffn_w2"]),
        "sw12t": _tile_sw12(inputs["L1_ffn_sw1"], inputs["L1_ffn_sw2"]),
        "w3catt": _tile_w3(inputs["L1_ffn_w3"], inputs["L1_ffn_sw3"]),
    }
    w_shared = {k: np.ascontiguousarray(v, dtype=np.float32)
                for k, v in w_shared.items()}
    maps = []
    for c in range(NCORES):
        b, half = c // 2, c % 2
        t0 = half * T_OWN
        xs = np.asarray(x[b], dtype=np.float32)
        own = xs[t0:t0 + T_OWN]
        if half == 0:
            ctx = np.zeros((CTX, D), dtype=np.float32)
        else:
            ctx = xs[t0 - CTX:t0]
        ext = np.concatenate([ctx, own], axis=0)
        xin = np.ascontiguousarray(ext.T)
        sel = np.zeros((16, 2), dtype=np.float32)
        if half == 1:
            prev = c - 1
            sel[2 * prev, 0] = 1.0
            sel[2 * prev + 1, 1] = 1.0
        m = {"xin": xin, "sel": sel}
        m.update(w_shared)
        maps.append(m)
    return maps


def unshard_output(results):
    out = np.empty((4, 4096, D), dtype=np.float32)
    for c in range(NCORES):
        b, half = c // 2, c % 2
        xo = results[c]["xout"]
        out[b, half * T_OWN:(half + 1) * T_OWN] = xo.T
    return out



def build_block_l0_ffn():
    nc = bass.Bass("TRN2", target_bir_lowering=False)
    xin = nc.dram_tensor("xin", [D, T_OWN], F32, kind="ExternalInput").ap()
    wgu = nc.dram_tensor("wgu_t", [D, 16384], F32, kind="ExternalInput").ap()
    wpr = nc.dram_tensor("wpr_t", [8192, D], F32, kind="ExternalInput").ap()
    xout = nc.dram_tensor("xout", [D, T_OWN], F32, kind="ExternalOutput").ap()
    scr_gu = nc.dram_tensor("scr_gu", [DC, 128, 16384], BF16).ap()
    scr_pr = nc.dram_tensor("scr_pr", [64, 128, 2048], BF16).ap()
    scr_h = nc.dram_tensor("scr_h", [64, 128, T_OWN], BF16).ap()
    p = Prog(nc)
    C = Consts(p)
    ln = LNKit(p, C)

    xres = p.sbuf("xres", [128, DC, T_OWN], F32)
    b_xres = [p.buf(f"xres{c}") for c in range(DC)]
    for c in range(DC):
        p.emit("sync", lambda e, c_=c: e.dma_start(
            out=xres[:, c_, :], in_=xin[c_ * 128:(c_ + 1) * 128, :]),
            writes=[b_xres[c]], dma=True, tag=f"ldx{c}")

    io = {"wgu": wgu, "wpr": wpr, "scr_gu": scr_gu, "scr_pr": scr_pr,
          "scr_h": scr_h}
    emit_l0_ffn(p, C, ln, io, lambda c: (xres[:, c, :], b_xres[c]))

    for c in range(DC):
        p.out_dma("gpsimd", lambda e, c_=c: e.dma_start(
            out=xout[c_ * 128:(c_ + 1) * 128, :], in_=xres[:, c_, :]),
            reads=[b_xres[c]], tag=f"stx{c}")
    p.realize()
    return nc


def build_block_l0_attn(debug=False):
    nc = bass.Bass("TRN2", target_bir_lowering=False)
    xin = nc.dram_tensor("xin", [D, XT], F32, kind="ExternalInput").ap()
    wqp = nc.dram_tensor("wqp", [D, 1024], F32, kind="ExternalInput").ap()
    wkp = nc.dram_tensor("wkp", [D, 1024], F32, kind="ExternalInput").ap()
    wv_t = nc.dram_tensor("wv_t", [D, 256], F32, kind="ExternalInput").ap()
    wo_t = nc.dram_tensor("wo_t", [256, D], F32, kind="ExternalInput").ap()
    wg_t = nc.dram_tensor("wg_t", [D, D], F32, kind="ExternalInput").ap()
    vemb = nc.dram_tensor("vemb", [2, 256], F32, kind="ExternalInput").ap()
    xout = nc.dram_tensor("xout", [D, T_OWN], F32, kind="ExternalOutput").ap()
    p = Prog(nc)
    C = Consts(p)
    ln = LNKit(p, C)

    xres = p.sbuf("xres", [128, DC, T_OWN], F32)
    b_xres = [p.buf(f"xres{c}") for c in range(DC)]
    xctx = p.sbuf("xctx", [128, DC, CTX], BF16)
    b_xctx = [p.buf(f"xctx{c}") for c in range(DC)]
    for c in range(DC):
        p.emit("gpsimd", lambda e, c_=c: e.dma_start(
            out=xctx[:, c_, :], in_=xin[c_ * 128:(c_ + 1) * 128, 0:CTX]),
            writes=[b_xctx[c]], dma=True, tag=f"ldctx{c}")
        p.emit("sync", lambda e, c_=c: e.dma_start(
            out=xres[:, c_, :], in_=xin[c_ * 128:(c_ + 1) * 128, CTX:XT]),
            writes=[b_xres[c]], dma=True, tag=f"ldx{c}")

    io = {"wqp": wqp, "wkp": wkp, "wv_t": wv_t, "wo_t": wo_t, "wg_t": wg_t,
          "vemb": vemb}
    emit_l0_attn(p, C, ln, io,
                 lambda c: (xres[:, c, :], b_xres[c]),
                 lambda c: (xctx[:, c, :], b_xctx[c]))

    if debug:
        dbg = nc.dram_tensor("dbg", [256, T_OWN], F32, kind="ExternalOutput").ap()
        for c in range(2):
            for t in range(4):
                p.emit("vector", lambda e, c_=c, t_=t: e.tensor_copy(
                    ln.tmp[:, 0, :], K_AFM[0][:, c_, t_ * 512:(t_ + 1) * 512]),
                    reads=[K_AFM[1][c]], writes=[ln.b_tmp[0]],
                    tag=f"dbgu{c}_{t}")
                p.out_dma("gpsimd", lambda e, c_=c, t_=t: e.dma_start(
                    out=dbg[c_ * 128:(c_ + 1) * 128, t_ * 512:(t_ + 1) * 512],
                    in_=ln.tmp[:, 0, :]),
                    reads=[ln.b_tmp[0]], tag=f"dbgs{c}_{t}")
    for c in range(DC):
        p.out_dma("gpsimd", lambda e, c_=c: e.dma_start(
            out=xout[c_ * 128:(c_ + 1) * 128, :], in_=xres[:, c_, :]),
            reads=[b_xres[c]], tag=f"stx{c}")
    p.realize()
    return nc


def build_block_l1_attn():
    nc = bass.Bass("TRN2", target_bir_lowering=False)
    xin = nc.dram_tensor("xin", [D, T_OWN], F32, kind="ExternalInput").ap()
    wqkv = nc.dram_tensor("wqkv", [D, 6144], F32, kind="ExternalInput").ap()
    wout = nc.dram_tensor("wout", [D, D], F32, kind="ExternalInput").ap()
    decay = nc.dram_tensor("decay", [1, 16], F32, kind="ExternalInput").ap()
    sel = nc.dram_tensor("sel", [16, 2], F32, kind="ExternalInput").ap()
    xout = nc.dram_tensor("xout", [D, T_OWN], F32, kind="ExternalOutput").ap()
    cc_in = nc.dram_tensor("cc_in", [2, 2048], F32).ap()
    cc_out = nc.dram_tensor("cc_out", [16, 2048], F32).ap()
    scr_q = nc.dram_tensor("scr_q", [16, 128, 2048], BF16).ap()
    scr_k = nc.dram_tensor("scr_k", [16, 128, 2048], BF16).ap()
    scr_kv = nc.dram_tensor("scr_kv", [16, 128, 2048], BF16).ap()
    scr_at = nc.dram_tensor("scr_at", [16, 128, 2048], BF16).ap()
    p = Prog(nc)
    C = Consts(p)
    ln = LNKit(p, C)

    xres = p.sbuf("xres", [128, DC, T_OWN], F32)
    b_xres = [p.buf(f"xres{c}") for c in range(DC)]
    for c in range(DC):
        p.emit("sync", lambda e, c_=c: e.dma_start(
            out=xres[:, c_, :], in_=xin[c_ * 128:(c_ + 1) * 128, :]),
            writes=[b_xres[c]], dma=True, tag=f"ldx{c}")

    io = {"wqkv": wqkv, "wout": wout, "decay": decay, "sel": sel,
          "cc_in": cc_in, "cc_out": cc_out, "scr_q": scr_q, "scr_k": scr_k,
          "scr_kv": scr_kv, "scr_at": scr_at}
    emit_l1_attn(p, C, ln, io, lambda c: (xres[:, c, :], b_xres[c]))

    for c in range(DC):
        p.out_dma("gpsimd", lambda e, c_=c: e.dma_start(
            out=xout[c_ * 128:(c_ + 1) * 128, :], in_=xres[:, c_, :]),
            reads=[b_xres[c]], tag=f"stx{c}")
    p.realize()
    return nc


def build_block_l1_ffn():
    nc = bass.Bass("TRN2", target_bir_lowering=False)
    xin = nc.dram_tensor("xin", [D, T_OWN], F32, kind="ExternalInput").ap()
    gw_t = nc.dram_tensor("gw_t", [D, 8], F32, kind="ExternalInput").ap()
    w12t = nc.dram_tensor("w12t", [2, 8, 11, 128, 16, 128], F32,
                          kind="ExternalInput").ap()
    sw12t = nc.dram_tensor("sw12t", [2, 22, 128, 16, 128], F32,
                           kind="ExternalInput").ap()
    w3catt = nc.dram_tensor("w3catt", [110, 128, 16, 128], F32,
                            kind="ExternalInput").ap()
    xout = nc.dram_tensor("xout", [D, T_OWN], F32, kind="ExternalOutput").ap()
    scr_w12 = nc.dram_tensor("scr_w12", [2, 8, 11, 128, 2048], BF16).ap()
    scr_sw12 = nc.dram_tensor("scr_sw12", [2, 22, 128, 2048], BF16).ap()
    scr_w3 = nc.dram_tensor("scr_w3", [110, 128, 2048], BF16).ap()
    scr_H = nc.dram_tensor("scr_H", [110, 128, 2048], BF16).ap()
    p = Prog(nc)
    C = Consts(p)
    ln = LNKit(p, C)

    xres = p.sbuf("xres", [128, DC, T_OWN], F32)
    b_xres = [p.buf(f"xres{c}") for c in range(DC)]
    for c in range(DC):
        p.emit("sync", lambda e, c_=c: e.dma_start(
            out=xres[:, c_, :], in_=xin[c_ * 128:(c_ + 1) * 128, :]),
            writes=[b_xres[c]], dma=True, tag=f"ldx{c}")

    io = {"gw_t": gw_t, "w12t": w12t, "sw12t": sw12t, "w3catt": w3catt,
          "scr_w12": scr_w12, "scr_sw12": scr_sw12, "scr_w3": scr_w3,
          "scr_H": scr_H}
    emit_l1_ffn(p, C, ln, io, lambda c: (xres[:, c, :], b_xres[c]))

    for c in range(DC):
        p.out_dma("gpsimd", lambda e, c_=c: e.dma_start(
            out=xout[c_ * 128:(c_ + 1) * 128, :], in_=xres[:, c_, :]),
            reads=[b_xres[c]], tag=f"stx{c}")
    p.realize()
    return nc



def _xres_io(nc, p, with_ctx):
    xin = nc.dram_tensor("xin", [D, XT if with_ctx else T_OWN], F32,
                         kind="ExternalInput").ap()
    xout = nc.dram_tensor("xout", [D, T_OWN], F32, kind="ExternalOutput").ap()
    xres = p.sbuf("xres", [128, DC, T_OWN], F32)
    b_xres = [p.buf(f"xres{c}") for c in range(DC)]
    off = CTX if with_ctx else 0
    for c in range(DC):
        p.emit("sync", lambda e, c_=c: e.dma_start(
            out=xres[:, c_, :], in_=xin[c_ * 128:(c_ + 1) * 128, off:off + T_OWN]),
            writes=[b_xres[c]], dma=True, tag=f"ldx{c}")
    xctx = xctx_b = None
    if with_ctx:
        xctx = p.sbuf("xctx", [128, DC, CTX], BF16)
        xctx_b = [p.buf(f"xctx{c}") for c in range(DC)]
        for c in range(DC):
            p.emit("gpsimd", lambda e, c_=c: e.dma_start(
                out=xctx[:, c_, :], in_=xin[c_ * 128:(c_ + 1) * 128, 0:CTX]),
                writes=[xctx_b[c]], dma=True, tag=f"ldctx{c}")

    def store(p):
        for c in range(DC):
            p.out_dma("gpsimd", lambda e, c_=c: e.dma_start(
                out=xout[c_ * 128:(c_ + 1) * 128, :], in_=xres[:, c_, :]),
                reads=[b_xres[c]], tag=f"stx{c}")
    return (xres, b_xres, xctx, xctx_b, store)


def build_launch_l0_attn():
    nc = bass.Bass("TRN2", target_bir_lowering=False)
    p = Prog(nc)
    C = Consts(p)
    ln = LNKit(p, C)
    xres, b_xres, xctx, b_xctx, store = _xres_io(nc, p, True)
    io = {k: nc.dram_tensor(k, shp, F32, kind="ExternalInput").ap()
          for k, shp in [("wqp", [D, 1024]), ("wkp", [D, 1024]),
                         ("wv_t", [D, 256]), ("wo_t", [256, D]),
                         ("wg_t", [D, D]), ("vemb", [2, 256])]}
    emit_l0_attn(p, C, ln, io,
                 lambda c: (xres[:, c, :], b_xres[c]),
                 lambda c: (xctx[:, c, :], b_xctx[c]))
    store(p)
    p.realize()
    return nc


def build_launch_l0_ffn():
    nc = bass.Bass("TRN2", target_bir_lowering=False)
    p = Prog(nc)
    C = Consts(p)
    ln = LNKit(p, C)
    xres, b_xres, _, _, store = _xres_io(nc, p, False)
    io = {"wgu": nc.dram_tensor("wgu_t", [D, 16384], F32,
                                kind="ExternalInput").ap(),
          "wpr": nc.dram_tensor("wpr_t", [8192, D], F32,
                                kind="ExternalInput").ap(),
          "scr_gu": nc.dram_tensor("scr_gu", [DC, 128, 16384], BF16).ap(),
          "scr_pr": nc.dram_tensor("scr_pr", [64, 128, 2048], BF16).ap(),
          "scr_h": nc.dram_tensor("scr_h", [64, 128, T_OWN], BF16).ap()}
    emit_l0_ffn(p, C, ln, io, lambda c: (xres[:, c, :], b_xres[c]))
    store(p)
    p.realize()
    return nc


def build_launch_l1_attn():
    nc = bass.Bass("TRN2", target_bir_lowering=False)
    p = Prog(nc)
    C = Consts(p)
    ln = LNKit(p, C)
    xres, b_xres, _, _, store = _xres_io(nc, p, False)
    io = {"wqkv": nc.dram_tensor("wqkv", [D, 6144], F32,
                                 kind="ExternalInput").ap(),
          "wout": nc.dram_tensor("wout", [D, D], F32,
                                 kind="ExternalInput").ap(),
          "decay": nc.dram_tensor("decay", [1, 16], F32,
                                  kind="ExternalInput").ap(),
          "sel": nc.dram_tensor("sel", [16, 2], F32,
                                kind="ExternalInput").ap(),
          "cc_in": nc.dram_tensor("cc_in", [2, 2048], F32).ap(),
          "cc_out": nc.dram_tensor("cc_out", [16, 2048], F32).ap(),
          "scr_q": nc.dram_tensor("scr_q", [16, 128, 2048], BF16).ap(),
          "scr_k": nc.dram_tensor("scr_k", [16, 128, 2048], BF16).ap(),
          "scr_kv": nc.dram_tensor("scr_kv", [16, 128, 2048], BF16).ap(),
          "scr_at": nc.dram_tensor("scr_at", [16, 128, 2048], BF16).ap()}
    emit_l1_attn(p, C, ln, io, lambda c: (xres[:, c, :], b_xres[c]))
    store(p)
    p.realize()
    return nc


def build_launch_l1_ffn():
    nc = bass.Bass("TRN2", target_bir_lowering=False)
    p = Prog(nc)
    C = Consts(p)
    ln = LNKit(p, C)
    xres, b_xres, _, _, store = _xres_io(nc, p, False)
    io = {"gw_t": nc.dram_tensor("gw_t", [D, 8], F32,
                                 kind="ExternalInput").ap(),
          "w12t": nc.dram_tensor("w12t", [2, 8, 11, 128, 16, 128], F32,
                                 kind="ExternalInput").ap(),
          "sw12t": nc.dram_tensor("sw12t", [2, 22, 128, 16, 128], F32,
                                  kind="ExternalInput").ap(),
          "w3catt": nc.dram_tensor("w3catt", [110, 128, 16, 128], F32,
                                   kind="ExternalInput").ap(),
          "scr_w12": nc.dram_tensor("scr_w12", [2, 8, 11, 128, 2048],
                                    BF16).ap(),
          "scr_sw12": nc.dram_tensor("scr_sw12", [2, 22, 128, 2048],
                                     BF16).ap(),
          "scr_w3": nc.dram_tensor("scr_w3", [110, 128, 2048], BF16).ap(),
          "scr_H": nc.dram_tensor("scr_H", [110, 128, 2048], BF16).ap()}
    emit_l1_ffn(p, C, ln, io, lambda c: (xres[:, c, :], b_xres[c]))
    store(p)
    p.realize()
    return nc


_L0A_KEYS = ["wqp", "wkp", "wv_t", "wo_t", "wg_t", "vemb"]
_L0F_KEYS = ["wgu_t", "wpr_t"]
_L1A_KEYS = ["wqkv", "wout", "decay", "sel"]
_L1F_KEYS = ["gw_t", "w12t", "sw12t", "w3catt"]


def kernel(**inputs):
    maps = shard_inputs(inputs)
    nc = build_launch_l0_attn()
    ms = [{"xin": m["xin"], **{k: m[k] for k in _L0A_KEYS}} for m in maps]
    res = run_bass_kernel_spmd(nc, ms, core_ids=list(range(NCORES)))
    xcur = [r["xout"] for r in res.results]
    nc = build_launch_l0_ffn()
    ms = [{"xin": xcur[c], **{k: maps[c][k] for k in _L0F_KEYS}}
          for c in range(NCORES)]
    res = run_bass_kernel_spmd(nc, ms, core_ids=list(range(NCORES)))
    xcur = [r["xout"] for r in res.results]
    nc = build_launch_l1_attn()
    ms = [{"xin": xcur[c], **{k: maps[c][k] for k in _L1A_KEYS}}
          for c in range(NCORES)]
    res = run_bass_kernel_spmd(nc, ms, core_ids=list(range(NCORES)))
    xcur = [r["xout"] for r in res.results]
    nc = build_launch_l1_ffn()
    ms = [{"xin": xcur[c], **{k: maps[c][k] for k in _L1F_KEYS}}
          for c in range(NCORES)]
    res = run_bass_kernel_spmd(nc, ms, core_ids=list(range(NCORES)))
    return unshard_output(res.results)


# version 03
# speedup vs baseline: 1.0025x; 1.0025x over previous
import numpy as np
from contextlib import ExitStack, contextmanager

import concourse.bass as bass
import concourse.mybir as mybir
from concourse.bass_utils import run_bass_kernel_spmd

F32 = mybir.dt.float32
BF16 = mybir.dt.bfloat16
AF = mybir.ActivationFunctionType
ALU = mybir.AluOpType
AX = mybir.AxisListType
U16 = mybir.dt.uint16
I32 = mybir.dt.int32

ENGINES = ("tensor", "vector", "scalar", "gpsimd", "sync")

D = 2048
DC = D // 128
T_OWN = 2048
CTX = 128
XT = T_OWN + CTX
NCORES = 8
TG = 512
NG = T_OWN // TG

LN_EPS = 1e-5
RMS_EPS = float(np.finfo(np.float32).eps)
LOG_DECAY = float(np.log(0.45))
NEG_BIG = -1e30
TERN_INV_C = float(1.0 / (0.5 + 2.0 ** -9))



class _Op:
    __slots__ = ("id", "engine", "fn", "is_dma", "deps", "needs_inc",
                 "done_sem", "done_val", "tag", "pre_wait")

    def __init__(self, oid, engine, fn, is_dma, deps, tag):
        self.id = oid
        self.engine = engine
        self.fn = fn
        self.is_dma = is_dma
        self.deps = deps
        self.needs_inc = False
        self.done_sem = None
        self.done_val = None
        self.tag = tag
        self.pre_wait = None


class Buf:
    __slots__ = ("name", "last_w", "readers")

    def __init__(self, name):
        self.name = name
        self.last_w = None
        self.readers = []


class Prog:
    def __init__(self, nc, strict=("gpsimd", "vector", "scalar")):
        self.nc = nc
        self.strict = set(strict)
        self.stack = ExitStack()
        self.scopes = [self.stack]
        self.ops = {e: [] for e in ENGINES}
        self._oid = 0
        self.sbuf_bytes = 0
        self.out_dmas = []
        self._names = set()
        self.psbanks = [self.stack.enter_context(
            nc.psum_tensor(f"psb{i}", [128, 512], F32)) for i in range(8)]
        self.bankbuf = [Buf(f"bank{i}") for i in range(8)]

    def bank(self, i):
        return self.psbanks[i][:, :], self.bankbuf[i]

    def sbuf(self, name, shape, dtype):
        assert name not in self._names, name
        self._names.add(name)
        t = self.scopes[-1].enter_context(
            self.nc.sbuf_tensor(name, list(shape), dtype))
        esz = 2 if dtype == BF16 else 4
        self.sbuf_bytes += int(np.prod(shape[1:])) * esz
        assert self.sbuf_bytes < 207500, \
            f"SBUF budget blown at {name}: {self.sbuf_bytes}"
        return t

    def buf(self, name):
        return Buf(name)

    @contextmanager
    def scope(self):
        st = ExitStack()
        self.scopes.append(st)
        saved = self.sbuf_bytes
        try:
            yield
        finally:
            lasts = []
            for e in ENGINES:
                if self.ops[e]:
                    lasts.append(self.ops[e][-1])
                    lasts.extend([o for o in self.ops[e] if o.is_dma][-8:])
            joiner = self.emit("sync", lambda eng: eng.nop(), deps=lasts,
                               tag="scope_join")
            for e in ENGINES:
                if e != "sync":
                    self.emit(e, lambda eng: eng.nop(), deps=[joiner],
                              tag=f"scope_rel_{e}")
            self.scopes.pop()
            st.close()
            self.sbuf_bytes = saved

    def emit(self, engine, fn, reads=(), writes=(), slots=(), deps=(), dma=False,
             tag=""):
        d = []
        fstrict = engine in self.strict
        for b in reads:
            if b.last_w is not None:
                d.append((b.last_w, fstrict))
        for b in slots:
            if b.last_w is not None:
                d.append((b.last_w, True))
        for b in writes:
            if b.last_w is not None:
                d.append((b.last_w, fstrict))
            for r in b.readers:
                d.append((r, fstrict))
        for x in deps:
            if x is None:
                continue
            d.append(x if isinstance(x, tuple) else (x, fstrict))
        op = _Op(self._oid, engine, fn, dma, d, tag)
        self._oid += 1
        self.ops[engine].append(op)
        for b in reads:
            b.readers.append(op)
        for b in slots:
            b.readers.append(op)
        for b in writes:
            b.last_w = op
            b.readers = []
        return op

    def out_dma(self, engine, fn, reads=(), deps=(), tag="out"):
        op = self.emit(engine, fn, reads=reads, deps=deps, dma=True, tag=tag)
        self.out_dmas.append(op)
        return op

    def _verify_schedule(self):
        idx = {e: 0 for e in ENGINES}
        semval = {}
        progressed = True
        total = sum(len(v) for v in self.ops.values())
        done = 0
        while progressed:
            progressed = False
            for e in ENGINES:
                while idx[e] < len(self.ops[e]):
                    op = self.ops[e][idx[e]]
                    ok = True
                    if op.pre_wait is not None and \
                            semval.get(id(op.pre_wait[0]), 0) < op.pre_wait[1]:
                        ok = False
                    if ok:
                        for (y, force) in op.deps:
                            if y.done_sem is None:
                                continue
                            if (not force) and (not y.is_dma) and (not op.is_dma) \
                                    and y.engine == e:
                                continue
                            if semval.get(id(y.done_sem), 0) < y.done_val:
                                ok = False
                                break
                    if not ok:
                        break
                    if op.needs_inc:
                        k = id(op.done_sem)
                        semval[k] = semval.get(k, 0) + (16 if op.is_dma else 1)
                    idx[e] += 1
                    done += 1
                    progressed = True
        if done != total:
            stuck = []
            for e in ENGINES:
                if idx[e] < len(self.ops[e]):
                    op = self.ops[e][idx[e]]
                    why = [(y.engine, y.tag) for (y, f) in op.deps
                           if y.done_sem is not None
                           and semval.get(id(y.done_sem), 0) < y.done_val]
                    stuck.append((e, op.tag, why[:4]))
            raise RuntimeError(f"schedule deadlock: {done}/{total} ran; "
                               f"stuck: {stuck}")

    def realize(self):
        nc = self.nc
        tail_deps = list(self.out_dmas)
        for e in ENGINES:
            if self.ops[e]:
                tail_deps.append(self.ops[e][-1])
                tail_deps.extend([o for o in self.ops[e] if o.is_dma][-8:])
        self.emit("sync", lambda eng: eng.nop(), deps=tail_deps, tag="final")

        NS = 8
        csem, dsem = {}, {}
        for e in ENGINES:
            if any(not o.is_dma for o in self.ops[e]):
                csem[e] = self.stack.enter_context(nc.semaphore(f"c_{e}"))
            if any(o.is_dma for o in self.ops[e]):
                dsem[e] = [self.stack.enter_context(nc.semaphore(f"d_{e}{j}"))
                           for j in range(NS)]

        for e in ENGINES:
            for op in self.ops[e]:
                for (y, force) in op.deps:
                    assert y.id < op.id, (op.tag, y.tag)
                    if force or (y.engine != op.engine) or y.is_dma or op.is_dma:
                        y.needs_inc = True
        for e in ENGINES:
            c, dc = 0, 0
            uses = [0] * NS
            for op in self.ops[e]:
                if op.is_dma:
                    slot = dc % NS
                    dc += 1
                    uses[slot] += 1
                    op.done_sem, op.done_val = dsem[e][slot], uses[slot] * 16
                    op.needs_inc = True
                    if uses[slot] > 1:
                        op.pre_wait = (dsem[e][slot], (uses[slot] - 1) * 16)
                elif op.needs_inc:
                    c += 1
                    op.done_sem, op.done_val = csem[e], c

        self._verify_schedule()
        blk = self.stack.enter_context(nc.Block())

        def make_body(e):
            ops = self.ops[e]

            def body(eng):
                hiwater = {}
                for op in ops:
                    need = {}
                    for (y, force) in op.deps:
                        if y.done_sem is None:
                            continue
                        if (not force) and (not y.is_dma) and (not op.is_dma) \
                                and y.engine == e:
                            continue
                        k = y.done_sem
                        if need.get(k, -1) < y.done_val:
                            need[k] = y.done_val
                    if op.pre_wait is not None:
                        sw, vw = op.pre_wait
                        if need.get(sw, -1) < vw:
                            need[sw] = vw
                    for s, v in need.items():
                        if hiwater.get(s, 0) >= v:
                            continue
                        eng.wait_ge(s, v)
                        hiwater[s] = v
                    instr = op.fn(eng)
                    if op.needs_inc:
                        assert instr is not None, f"op {op.tag} returned no instr"
                        instr.then_inc(op.done_sem, 16 if op.is_dma else 1)
            return body

        for e in ENGINES:
            if not self.ops[e]:
                continue
            getattr(blk, e)(make_body(e))
        self.stack.close()



class Consts:
    def __init__(self, p: Prog):
        self.ident = p.sbuf("c_ident", [128, 128], BF16)
        self.ones_col_b = p.sbuf("c_ones_col_b", [128, 1], BF16)
        self.ones_row_f = p.sbuf("c_ones_row_f", [1, 128], F32)
        self.ones_row_b = p.sbuf("c_ones_row_b", [1, 128], BF16)
        self.b_ident = p.buf("ident")
        self.b_ones_col_b = p.buf("ones_col_b")
        self.b_ones_row_f = p.buf("ones_row_f")
        self.b_ones_row_b = p.buf("ones_row_b")

        p.emit("gpsimd", lambda g: g.memset(self.ident[:], 0.0),
               writes=[self.b_ident], tag="ident0")
        p.emit("gpsimd", lambda g: g.affine_select(
            out=self.ident[:], in_=self.ident[:], compare_op=ALU.not_equal,
            fill=1.0, base=0, pattern=[[-1, 128]], channel_multiplier=1),
            writes=[self.b_ident], tag="ident1")
        p.emit("gpsimd", lambda g: g.memset(self.ones_col_b[:], 1.0),
               writes=[self.b_ones_col_b], tag="ones_col")
        p.emit("gpsimd", lambda g: g.memset(self.ones_row_f[:], 1.0),
               writes=[self.b_ones_row_f], tag="ones_row")
        p.emit("gpsimd", lambda g: g.memset(self.ones_row_b[:], 1.0),
               writes=[self.b_ones_row_b], tag="ones_row_b")



class LNKit:
    MAXTT = 512

    def __init__(self, p: Prog, C: Consts, banks=(4, 5, 6, 7)):
        self.p, self.C = p, C
        TT = self.MAXTT
        self.xb = p.sbuf("ln_xb", [128, 2, 2, TT], BF16)
        self.rows = p.sbuf("ln_rows", [1, 3, TT], F32)
        self.mbs = p.sbuf("ln_mbs", [128, TT], F32)
        self.rbs = p.sbuf("ln_rbs", [128, TT], F32)
        self.tmp = p.sbuf("ln_tmp", [128, 2, TT], F32)
        self.bk = banks
        self.b_xb = [p.buf("ln_xb0"), p.buf("ln_xb1")]
        self.b_rows = p.buf("ln_rows")
        self.b_mbs, self.b_rbs = p.buf("ln_mbs"), p.buf("ln_rbs")
        self.b_tmp = [p.buf("ln_tmp0"), p.buf("ln_tmp1")]

    def emit(self, tag, TT, src, dst, resid=None, nchunks=DC, eps=LN_EPS):
        p, C = self.p, self.C
        st0, b_st0 = p.bank(self.bk[0])
        st1, b_st1 = p.bank(self.bk[1])
        psmb, b_psmb = p.bank(self.bk[2])
        psrb, b_psrb = p.bank(self.bk[3])
        sts = [(st0, b_st0), (st1, b_st1)]
        for c in range(nchunks):
            s_ap, s_b = src(c)
            r = c % 2
            xb_ap = self.xb[:, r, 0, 0:TT]
            sq_ap = self.xb[:, r, 1, 0:TT]
            p.emit("vector", lambda e, o=xb_ap, i=s_ap: e.tensor_copy(o, i),
                   reads=[s_b], writes=[self.b_xb[r]], tag=f"{tag}c{c}cast")
            p.emit("vector", lambda e, o=sq_ap, i=xb_ap: e.tensor_tensor(
                o, i, i, op=ALU.mult), writes=[self.b_xb[r]], tag=f"{tag}c{c}sq")
            for h in range(2):
                p.emit("tensor", lambda e, r_=r, c_=c, h_=h: e.matmul(
                    sts[h_][0][0:1, 0:TT], C.ones_col_b[:],
                    self.xb[:, r_, h_, 0:TT],
                    start=(c_ == 0), stop=(c_ == nchunks - 1)),
                    reads=[self.b_xb[r], C.b_ones_col_b], writes=[sts[h][1]],
                    tag=f"{tag}c{c}mm{h}")
        b_rows = self.b_rows
        m, ms, t = (self.rows[0:1, i, 0:TT] for i in range(3))
        var = ms
        rr = ms
        nfeat = float(nchunks * 128)
        p.emit("vector", lambda e: e.tensor_scalar(m, st0[0:1, 0:TT],
               1.0 / nfeat, None, op0=ALU.mult), reads=[b_st0], writes=[b_rows],
               tag=f"{tag}m")
        p.emit("vector", lambda e: e.tensor_scalar(ms, st1[0:1, 0:TT],
               1.0 / nfeat, None, op0=ALU.mult), reads=[b_st1], writes=[b_rows],
               tag=f"{tag}ms")
        p.emit("vector", lambda e: e.tensor_tensor(t, m, m, op=ALU.mult),
               writes=[b_rows], tag=f"{tag}mm")
        p.emit("vector", lambda e: e.tensor_tensor(var, ms, t, op=ALU.subtract),
               writes=[b_rows], tag=f"{tag}var")
        p.emit("scalar", lambda e: e.activation(t, var, AF.Ln,
               bias=np.float32(eps)), reads=[b_rows], writes=[b_rows],
               tag=f"{tag}ln")
        p.emit("scalar", lambda e: e.activation(rr, t, AF.Exp,
               scale=np.float32(-0.5)), writes=[b_rows], tag=f"{tag}exp")
        p.emit("tensor", lambda e: e.matmul(psmb[:, 0:TT], C.ones_row_f[:],
               m, start=True, stop=True), reads=[b_rows, C.b_ones_row_f],
               writes=[b_psmb], tag=f"{tag}bm")
        p.emit("tensor", lambda e: e.matmul(psrb[:, 0:TT], C.ones_row_f[:],
               rr, start=True, stop=True), reads=[b_rows, C.b_ones_row_f],
               writes=[b_psrb], tag=f"{tag}br")
        p.emit("vector", lambda e: e.tensor_copy(self.mbs[:, 0:TT],
               psmb[:, 0:TT]), reads=[b_psmb], writes=[self.b_mbs],
               tag=f"{tag}mbs")
        p.emit("vector", lambda e: e.tensor_copy(self.rbs[:, 0:TT],
               psrb[:, 0:TT]), reads=[b_psrb], writes=[self.b_rbs],
               tag=f"{tag}rbs")
        for c in range(nchunks):
            s_ap, s_b = src(c)
            on_pool = (c % 2 == 1)
            eng = "gpsimd" if on_pool else "vector"
            mb = (self.mbs[:, 0:TT], self.b_mbs) if on_pool else \
                 (psmb[:, 0:TT], b_psmb)
            rb = (self.rbs[:, 0:TT], self.b_rbs) if on_pool else \
                 (psrb[:, 0:TT], b_psrb)
            r = c % 2
            tm = self.tmp[:, r, 0:TT]
            p.emit(eng, lambda e, o=tm, i=s_ap, b=mb[0]: e.tensor_tensor(
                o, i, b, op=ALU.subtract), reads=[s_b, mb[1]],
                writes=[self.b_tmp[r]], tag=f"{tag}c{c}sub")
            if dst is not None:
                d_ap, d_b = dst(c)
                p.emit(eng, lambda e, o=d_ap, i=tm, b=rb[0]: e.tensor_tensor(
                    o, i, b, op=ALU.mult), reads=[self.b_tmp[r], rb[1]],
                    writes=[d_b], tag=f"{tag}c{c}mul")
            if resid is not None:
                assert dst is None
                x_ap, x_b = resid(c)
                p.emit(eng, lambda e, o=tm, b=rb[0]: e.tensor_tensor(
                    o, o, b, op=ALU.mult), reads=[rb[1]],
                    writes=[self.b_tmp[r]], tag=f"{tag}c{c}mul")
                p.emit(eng, lambda e, o=x_ap, i=tm: e.tensor_tensor(
                    o, o, i, op=ALU.add), reads=[self.b_tmp[r]],
                    writes=[x_b], tag=f"{tag}c{c}add")



def emit_l0_ffn_quant(p, C, wsrc, scr, n_cc_total, n_cols, tag):
    with p.scope():
        qind = p.sbuf(f"{tag}_qind", [128, 16, 32], BF16)
        qbc = p.sbuf(f"{tag}_qbc", [32, 16, 128], BF16)
        b_qind = p.buf(f"{tag}_qind")
        b_qbc = p.buf(f"{tag}_qbc")
        p.emit("gpsimd", lambda g: g.memset(qind[:], 0.0),
               writes=[b_qind], tag=f"{tag}qind0")
        p.emit("gpsimd", lambda g: g.affine_select(
            out=qind[0:64], in_=qind[0:64], compare_op=ALU.not_equal,
            fill=1.0, base=0, pattern=[[-2, 16], [1, 32]], channel_multiplier=0),
            writes=[b_qind], tag=f"{tag}qind1")
        p.emit("gpsimd", lambda g: g.affine_select(
            out=qind[64:128], in_=qind[64:128],
            compare_op=ALU.not_equal, fill=1.0, base=-1,
            pattern=[[-2, 16], [1, 32]], channel_multiplier=0),
            writes=[b_qind], tag=f"{tag}qind2")
        p.emit("gpsimd", lambda g: g.memset(qbc[:], 0.0),
               writes=[b_qbc], tag=f"{tag}qbc0")
        p.emit("gpsimd", lambda g: g.affine_select(
            out=qbc[:], in_=qbc[:], compare_op=ALU.not_equal,
            fill=1.0, base=0, pattern=[[2, 16], [1, 2], [0, 64]],
            channel_multiplier=-1),
            writes=[b_qbc], tag=f"{tag}qbc1")
        wb = p.sbuf(f"{tag}_wb", [128, 16, 512], BF16)
        awb = p.sbuf(f"{tag}_awb", [128, 2, 512], BF16)
        srow = p.sbuf(f"{tag}_srow", [32, 512], BF16)
        bsb = p.sbuf(f"{tag}_bsb", [128, 2, 512], BF16)
        ab = p.sbuf(f"{tag}_ab", [128, 2, 2, 512], BF16)
        wq = p.sbuf(f"{tag}_wq", [128, 2, 512], BF16)
        b_wb = [p.buf(f"{tag}_wb{i}") for i in range(16)]
        b_awb = [p.buf(f"{tag}_awb{i}") for i in range(2)]
        b_srow = p.buf(f"{tag}_srow")
        b_bsb = [p.buf(f"{tag}_bsb{i}") for i in range(2)]
        b_ab = [p.buf(f"{tag}_ab{i}") for i in range(2)]
        b_wq = [p.buf(f"{tag}_wq{i}") for i in range(2)]

        n_cs = n_cols // 512
        for qb in range(n_cc_total // 16 * n_cs):
            ccg, cs = divmod(qb, n_cs)
            ps_sc, b_ps_sc = p.bank(0)
            ps_bs, b_ps_bs = p.bank(1 + (qb % 2))
            for i in range(16):
                cc = ccg * 16 + i
                r = i % 2
                p.emit("gpsimd", lambda e, cc_=cc, cs_=cs, i_=i: e.dma_start(
                    out=wb[:, i_, :],
                    in_=wsrc[cc_ * 128:(cc_ + 1) * 128, cs_ * 512:(cs_ + 1) * 512]),
                    writes=[b_wb[i]], dma=True, tag=f"{tag}q{qb}ld{i}")
                p.emit("vector", lambda e, i_=i, r_=r: e.tensor_scalar(
                    awb[:, r_, :].bitcast(U16), wb[:, i_, :].bitcast(U16),
                    0x7fff, None, op0=ALU.bitwise_and),
                    reads=[b_wb[i]], writes=[b_awb[r]], tag=f"{tag}q{qb}abs{i}")
                p.emit("tensor", lambda e, i_=i, r_=r, ps_=ps_sc: e.matmul(
                    ps_[0:32, :], qind[:, i_, :], awb[:, r_, :],
                    start=(i_ == 0), stop=(i_ == 15)),
                    reads=[b_awb[r], b_qind], writes=[b_ps_sc],
                    tag=f"{tag}q{qb}smm{i}")
            p.emit("vector", lambda e, ps_=ps_sc: e.tensor_scalar(
                srow[:], ps_[0:32, :], 1.0 / 64.0, 1e-8, op0=ALU.mult,
                op1=ALU.max), reads=[b_ps_sc], writes=[b_srow],
                tag=f"{tag}q{qb}fin")
            for i in range(16):
                cc = ccg * 16 + i
                r = i % 2
                p.emit("tensor", lambda e, i_=i, ps_=ps_bs: e.matmul(
                    ps_[:, :], qbc[:, i_, :], srow[:],
                    start=True, stop=True),
                    reads=[b_srow, b_qbc], writes=[b_ps_bs],
                    tag=f"{tag}q{qb}bmm{i}")
                p.emit("vector", lambda e, r_=r, ps_=ps_bs: e.tensor_copy(
                    bsb[:, r_, :], ps_[:, :]),
                    reads=[b_ps_bs], writes=[b_bsb[r]], tag=f"{tag}q{qb}bsc{i}")
                p.emit("vector", lambda e, i_=i, r_=r: e.scalar_tensor_tensor(
                    ab[:, r_, 0, :], wb[:, i_, :], TERN_INV_C, bsb[:, r_, :],
                    op0=ALU.mult, op1=ALU.is_gt),
                    reads=[b_wb[i], b_bsb[r]], writes=[b_ab[r]],
                    tag=f"{tag}q{qb}A{i}")
                p.emit("vector", lambda e, i_=i, r_=r: e.scalar_tensor_tensor(
                    ab[:, r_, 1, :], wb[:, i_, :], -TERN_INV_C, bsb[:, r_, :],
                    op0=ALU.mult, op1=ALU.is_gt),
                    reads=[b_wb[i], b_bsb[r]], writes=[b_ab[r]],
                    tag=f"{tag}q{qb}B{i}")
                p.emit("gpsimd", lambda e, r_=r: e.tensor_tensor(
                    ab[:, r_, 0, :], ab[:, r_, 0, :], ab[:, r_, 1, :],
                    op=ALU.subtract), writes=[b_ab[r]], tag=f"{tag}q{qb}D{i}")
                p.emit("vector", lambda e, r_=r: e.tensor_tensor(
                    wq[:, r_, :], ab[:, r_, 0, :], bsb[:, r_, :], op=ALU.mult),
                    reads=[b_ab[r], b_bsb[r]], writes=[b_wq[r]],
                    tag=f"{tag}q{qb}W{i}")
                p.emit("gpsimd", lambda e, cc_=cc, cs_=cs, r_=r: e.dma_start(
                    out=scr[cc_, :, cs_ * 512:(cs_ + 1) * 512],
                    in_=wq[:, r_, :]),
                    reads=[b_wq[r]], dma=True, tag=f"{tag}q{qb}st{i}")


def emit_l0_ffn(p, C, ln, io, xacc):
    emit_l0_ffn_quant(p, C, io["wgu"], io["scr_gu"], DC, 16384, "qgu")
    emit_l0_ffn_quant(p, C, io["wpr"], io["scr_pr"], 64, 2048, "qpr")

    with p.scope():
        rmsrow = p.sbuf("ff_rms", [1, 2, TG], F32)
        b_rmsrow = p.buf("ff_rms")
        rbf = p.sbuf("ff_rbf", [128, TG], F32)
        b_rbf = p.buf("ff_rbf")
        for g in range(NG):
            _l0_ffn_gu(p, C, ln, io, xacc, g, rmsrow, b_rmsrow, rbf, b_rbf)
            _l0_ffn_proj(p, C, ln, io, xacc, g, rbf, b_rbf)


def _l0_ffn_gu(p, C, ln, io, xacc, g, rmsrow, b_rmsrow, rbf, b_rbf):
    with p.scope():
        xln = p.sbuf(f"ff{g}_xln", [128, DC, TG], BF16)
        b_xln = [p.buf(f"ff{g}_xln{c}") for c in range(DC)]
        wgu_sb = p.sbuf(f"ff{g}_wgu", [128, 2, DC, 256], BF16)
        b_wgu = [p.buf(f"ff{g}_wgu{i}") for i in range(2)]
        sg = p.sbuf(f"ff{g}_sg", [128, 2, TG], F32)
        b_sg = [p.buf(f"ff{g}_sg{i}") for i in range(2)]
        hh = p.sbuf(f"ff{g}_h", [128, 2, TG], BF16)
        b_hh = [p.buf(f"ff{g}_h{i}") for i in range(2)]
        hsq = p.sbuf(f"ff{g}_hsq", [128, 2, TG], BF16)
        b_hsq = [p.buf(f"ff{g}_hsq{i}") for i in range(2)]

        ln.emit(f"ff{g}ln1", TG,
                src=lambda c: (xacc(c)[0][:, g * TG:(g + 1) * TG], xacc(c)[1]),
                dst=lambda c: (xln[:, c, :], b_xln[c]))
        ps_rms, b_ps_rms = p.bank(3)
        for sc in range(32):
            rs = sc % 2
            p.emit("sync", lambda e, sc_=sc, rs_=rs: e.dma_start(
                out=wgu_sb[:, rs_, :, :],
                in_=io["scr_gu"][:, :, sc_ * 256:(sc_ + 1) * 256]
                .rearrange("c p n -> p c n")),
                writes=[b_wgu[rs]], dma=True, tag=f"ff{g}s{sc}ldg")
            for sub in range(2):
                hc = sc * 2 + sub
                pg, b_pg = p.bank(sub)
                for k in range(DC):
                    p.emit("tensor", lambda e, k_=k, rs_=rs, sub_=sub, pg_=pg: e.matmul(
                        pg_[:, 0:TG],
                        wgu_sb[:, rs_, k_, sub_ * 128:(sub_ + 1) * 128],
                        xln[:, k_, :],
                        start=(k_ == 0), stop=(k_ == DC - 1)),
                        reads=[b_wgu[rs], b_xln[k]], writes=[b_pg],
                        tag=f"ff{g}h{hc}g{k}")
                p.emit("scalar", lambda e, sub_=sub, pg_=pg: e.activation(
                    sg[:, sub_, :], pg_[:, 0:TG], AF.Silu),
                    reads=[b_pg], writes=[b_sg[sub]], tag=f"ff{g}h{hc}silu")
            p.emit("sync", lambda e, sc_=sc, rs_=rs: e.dma_start(
                out=wgu_sb[:, rs_, :, :],
                in_=io["scr_gu"][:, :, 8192 + sc_ * 256:8192 + (sc_ + 1) * 256]
                .rearrange("c p n -> p c n")),
                writes=[b_wgu[rs]], dma=True, tag=f"ff{g}s{sc}ldu")
            for sub in range(2):
                hc = sc * 2 + sub
                pg, b_pg = p.bank(sub)
                for k in range(DC):
                    p.emit("tensor", lambda e, k_=k, rs_=rs, sub_=sub, pg_=pg: e.matmul(
                        pg_[:, 0:TG],
                        wgu_sb[:, rs_, k_, sub_ * 128:(sub_ + 1) * 128],
                        xln[:, k_, :],
                        start=(k_ == 0), stop=(k_ == DC - 1)),
                        reads=[b_wgu[rs], b_xln[k]], writes=[b_pg],
                        tag=f"ff{g}h{hc}u{k}")
                p.emit("vector", lambda e, sub_=sub, pg_=pg: e.tensor_tensor(
                    hh[:, sub_, :], sg[:, sub_, :], pg_[:, 0:TG], op=ALU.mult),
                    reads=[b_sg[sub], b_pg], writes=[b_hh[sub]],
                    tag=f"ff{g}h{hc}mul")
                p.emit("gpsimd", lambda e, sub_=sub: e.tensor_tensor(
                    hsq[:, sub_, :], hh[:, sub_, :], hh[:, sub_, :],
                    op=ALU.mult),
                    reads=[b_hh[sub]], writes=[b_hsq[sub]], tag=f"ff{g}h{hc}sq")
                p.emit("tensor", lambda e, hc_=hc, sub_=sub: e.matmul(
                    ps_rms[0:1, 0:TG], C.ones_col_b[:], hsq[:, sub_, :],
                    start=(hc_ == 0), stop=(hc_ == 63)),
                    reads=[b_hsq[sub], C.b_ones_col_b], writes=[b_ps_rms],
                    tag=f"ff{g}h{hc}rmm")
                p.emit("gpsimd", lambda e, hc_=hc, g_=g, sub_=sub: e.dma_start(
                    out=io["scr_h"][hc_, :, g_ * TG:(g_ + 1) * TG],
                    in_=hh[:, sub_, :]),
                    reads=[b_hh[sub]], dma=True, tag=f"ff{g}h{hc}sth")
        ms = rmsrow[0:1, 0, :]
        tr = rmsrow[0:1, 1, :]
        p.emit("vector", lambda e: e.tensor_scalar(
            ms, ps_rms[0:1, 0:TG], 1.0 / 8192.0, None, op0=ALU.mult),
            reads=[b_ps_rms], writes=[b_rmsrow], tag=f"ff{g}rms_ms")
        p.emit("scalar", lambda e: e.activation(tr, ms, AF.Ln,
               bias=np.float32(RMS_EPS)), reads=[b_rmsrow],
               writes=[b_rmsrow], tag=f"ff{g}rms_ln")
        p.emit("scalar", lambda e: e.activation(ms, tr, AF.Exp,
               scale=np.float32(-0.5)), writes=[b_rmsrow],
               tag=f"ff{g}rms_exp")
        ps_rb, b_ps_rb = p.bank(0)
        p.emit("tensor", lambda e: e.matmul(ps_rb[:, 0:TG], C.ones_row_f[:],
               ms, start=True, stop=True),
               reads=[b_rmsrow, C.b_ones_row_f], writes=[b_ps_rb],
               tag=f"ff{g}rms_b")
        p.emit("vector", lambda e: e.tensor_copy(rbf[:], ps_rb[:, 0:TG]),
               reads=[b_ps_rb], writes=[b_rbf], tag=f"ff{g}rms_e")


def _l0_ffn_proj(p, C, ln, io, xacc, g, rbf, b_rbf):
    with p.scope():
        hin = p.sbuf(f"ff{g}_hin", [128, 2, TG], BF16)
        b_hin = [p.buf(f"ff{g}_hin{i}") for i in range(2)]
        wpr_sb = p.sbuf(f"ff{g}_wpr", [128, 2, 1024], BF16)
        b_wpr = [p.buf(f"ff{g}_wpr{i}") for i in range(2)]
        ysb = p.sbuf(f"ff{g}_y", [128, DC, TG], F32)
        b_ysb = [p.buf(f"ff{g}_y{c}") for c in range(DC)]

        for ohalf in range(2):
            pys = [p.bank(i) for i in range(8)]
            for k in range(64):
                r = k % 2
                p.emit("sync", lambda e, k_=k, g_=g, r_=r: e.dma_start(
                    out=hin[:, r_, :],
                    in_=io["scr_h"][k_, :, g_ * TG:(g_ + 1) * TG]),
                    writes=[b_hin[r]], dma=True, tag=f"ff{g}o{ohalf}k{k}ldh")
                p.emit("sync", lambda e, k_=k, oh_=ohalf, r_=r: e.dma_start(
                    out=wpr_sb[:, r_, :],
                    in_=io["scr_pr"][k_, :, oh_ * 1024:(oh_ + 1) * 1024]),
                    writes=[b_wpr[r]], dma=True, tag=f"ff{g}o{ohalf}k{k}ldw")
                for oc in range(8):
                    p.emit("tensor", lambda e, k_=k, oc_=oc, r_=r: e.matmul(
                        pys[oc_][0][:, 0:TG],
                        wpr_sb[:, r_, oc_ * 128:(oc_ + 1) * 128],
                        hin[:, r_, :],
                        start=(k_ == 0), stop=(k_ == 63)),
                        reads=[b_wpr[r], b_hin[r]], writes=[pys[oc][1]],
                        tag=f"ff{g}o{ohalf}k{k}m{oc}")
            for oc in range(8):
                c = ohalf * 8 + oc
                p.emit("vector", lambda e, oc_=oc, c_=c: e.tensor_tensor(
                    ysb[:, c_, :], pys[oc_][0][:, 0:TG], rbf[:],
                    op=ALU.mult), reads=[pys[oc][1], b_rbf],
                    writes=[b_ysb[c]], tag=f"ff{g}o{ohalf}ev{oc}")
        ln.emit(f"ff{g}ln2", TG,
                src=lambda c: (ysb[:, c, :], b_ysb[c]),
                dst=None,
                resid=lambda c: (xacc(c)[0][:, g * TG:(g + 1) * TG],
                                 xacc(c)[1]))



NHEAD = 32


K_AFM = [None, None]


def emit_l0_attn(p, C, ln, io, xacc, ctxacc):
    lam = LOG_DECAY
    afm = p.sbuf("at_afm", [128, 2, T_OWN], BF16)
    b_afm = [p.buf(f"at_afm{c}") for c in range(2)]
    K_AFM[0], K_AFM[1] = afm, b_afm
    tri = p.sbuf("at_tri", [128, 128], BF16)
    b_tri = p.buf("at_tri")
    biasc = p.sbuf("at_biasc", [128, 2], F32)
    b_biasc = p.buf("at_biasc")
    beta = p.sbuf("at_beta", [128, 16], F32)
    b_beta = p.buf("at_beta")
    dv = p.sbuf("at_dv", [128, 2, 2], F32)
    b_dv = p.buf("at_dv")
    dvb = p.sbuf("at_dvb", [128, 2, 1], BF16)
    b_dvb = p.buf("at_dvb")

    p.emit("gpsimd", lambda g: g.memset(tri[:], 0.0), writes=[b_tri], tag="tri0")
    p.emit("gpsimd", lambda g: g.affine_select(
        out=tri[:], in_=tri[:], compare_op=ALU.is_ge, fill=NEG_BIG,
        base=0, pattern=[[1, 128]], channel_multiplier=-1),
        writes=[b_tri], tag="tri1")
    p.emit("gpsimd", lambda g: g.iota(
        biasc[:, 0:1], pattern=[[0, 1]], base=0, channel_multiplier=1,
        allow_small_or_imprecise_dtypes=True), writes=[b_biasc], tag="bc0")
    p.emit("vector", lambda e: e.tensor_scalar(
        biasc[:, 1:2], biasc[:, 0:1], -lam, 192.0 * lam, op0=ALU.mult,
        op1=ALU.add), reads=[b_biasc], writes=[b_biasc], tag="bc1")
    p.emit("vector", lambda e: e.tensor_scalar(
        biasc[:, 0:1], biasc[:, 0:1], -lam, 64.0 * lam, op0=ALU.mult,
        op1=ALU.add), writes=[b_biasc], tag="bc2")
    for c in range(2):
        p.emit("gpsimd", lambda g, c_=c: g.dma_start(
            out=dv[:, c_, 0:1],
            in_=io["vemb"][0, c_ * 128:(c_ + 1) * 128].rearrange(
                "(p o) -> p o", o=1)), writes=[b_dv], dma=True, tag=f"ve0{c}")
        p.emit("gpsimd", lambda g, c_=c: g.dma_start(
            out=dv[:, c_, 1:2],
            in_=io["vemb"][1, c_ * 128:(c_ + 1) * 128].rearrange(
                "(p o) -> p o", o=1)), writes=[b_dv], dma=True, tag=f"ve1{c}")
    for c in range(2):
        p.emit("vector", lambda e, c_=c: e.tensor_copy(
            dvb[:, c_, :], dv[:, c_, 0:1]), reads=[b_dv], writes=[b_dvb],
            tag=f"dvb{c}")
        p.emit("vector", lambda e, c_=c: e.tensor_tensor(
            dv[:, c_, 1:2], dv[:, c_, 1:2], dv[:, c_, 0:1], op=ALU.subtract),
            reads=[b_dv], writes=[b_dv], tag=f"dv{c}")

    _l0_attn_phase_a(p, C, ln, io, xacc, ctxacc, afm, b_afm, biasc, b_biasc,
                     tri, b_tri)
    _l0_attn_phase_b(p, C, ln, io, xacc, afm, b_afm, beta, b_beta, dv, b_dv,
                     dvb, b_dvb)


def _l0_attn_phase_a(p, C, ln, io, xacc, ctxacc, afm, b_afm, biasc, b_biasc,
                     tri, b_tri):
    with p.scope():
        xln = p.sbuf("atA_xln", [128, DC, 256], BF16)
        b_xln = [p.buf(f"atA_xln{c}") for c in range(DC)]
        wqp = p.sbuf("atA_wqp", [128, 2, DC, 128], BF16)
        b_wqp = [p.buf(f"atA_wqp{i}") for i in range(2)]
        wv = p.sbuf("atA_wv", [128, DC, 256], BF16)
        b_wv = p.buf("atA_wv")
        qtp = p.sbuf("atA_qtp", [128, 8, 2, 128], BF16)
        b_qtp = [p.buf(f"atA_qtp{i}") for i in range(2)]
        ktp = p.sbuf("atA_ktp", [128, 8, 3, 128], BF16)
        b_ktp = [p.buf(f"atA_ktp{i}") for i in range(3)]
        vtok = p.sbuf("atA_vtok", [128, 3, 9 * NHEAD], BF16)
        b_vtok = [p.buf(f"atA_vtok{i}") for i in range(3)]
        probs = p.sbuf("atA_probs", [128, 4, 2, 128], BF16)
        b_probs = [p.buf(f"atA_probs{i}") for i in range(4)]
        atok = p.sbuf("atA_atok", [128, 2, 256], BF16)
        b_atok = [p.buf(f"atA_atok{i}") for i in range(2)]
        recip = p.sbuf("atA_recip", [128, 2, NHEAD], F32)
        b_recip = [p.buf(f"atA_recip{i}") for i in range(2)]

        p.emit("gpsimd", lambda g: g.memset(
            vtok[:].rearrange("p t (h n) -> p t h n", n=9)[:, :, :, 8:9],
            1.0), writes=[b_vtok[0], b_vtok[1], b_vtok[2]], tag="atA_vones")
        p.emit("gpsimd", lambda g: g.dma_start(
            out=wv[:],
            in_=io["wv_t"][:, :].rearrange("(c p) n -> p c n", p=128)),
            writes=[b_wv], dma=True, tag="atA_ldwv")

        def ln_src(c, grp):
            if grp == 0:
                a, b = ctxacc(c)
                return a[:, 0:CTX], b
            a, b = xacc(c)
            return a[:, (grp - 1) * 256:grp * 256], b

        for grp in range(9):
            TT = 128 if grp == 0 else 256
            ln.emit(f"atA{grp}ln", TT,
                    src=lambda c: ln_src(c, grp),
                    dst=lambda c: (xln[:, c, 0:TT], b_xln[c]))
            ext_tiles = [0] if grp == 0 else [2 * grp - 1, 2 * grp]
            for (which, dstring, b_ring, nring, wsrc) in (
                    ("q", qtp, b_qtp, 2, io["wqp"]),
                    ("k", ktp, b_ktp, 3, io["wkp"])):
                if which == "q" and grp == 0:
                    continue
                for oc in range(8):
                    r = oc % 2
                    p.emit("gpsimd", lambda g, oc_=oc, r_=r, w_=wsrc: g.dma_start(
                        out=wqp[:, r_, :, :],
                        in_=w_[:, oc_ * 128:(oc_ + 1) * 128]
                        .rearrange("(c p) n -> p c n", p=128)),
                        writes=[b_wqp[r]], dma=True,
                        tag=f"atA{grp}{which}w{oc}")
                    pq, b_pq = p.bank(oc % 2)
                    for k in range(DC):
                        p.emit("tensor", lambda e, k_=k, r_=r, TT_=TT, pq_=pq:
                               e.matmul(pq_[:, 0:TT_], wqp[:, r_, k_, :],
                                        xln[:, k_, 0:TT_],
                                        start=(k_ == 0), stop=(k_ == DC - 1)),
                               reads=[b_wqp[r], b_xln[k]], writes=[b_pq],
                               tag=f"atA{grp}{which}m{oc}_{k}")
                    for ti, te in enumerate(ext_tiles):
                        sl = te % nring
                        p.emit("scalar", lambda e, oc_=oc, ti_=ti, sl_=sl,
                               d_=dstring, pq_=pq: e.activation(
                                   d_[:, oc_, sl_, :],
                                   pq_[:, ti_ * 128:(ti_ + 1) * 128], AF.Tanh),
                               reads=[b_pq], writes=[b_ring[sl]],
                               tag=f"atA{grp}{which}e{oc}_{ti}")
            for vc in range(2):
                pv, b_pv = p.bank(vc)
                for k in range(DC):
                    p.emit("tensor", lambda e, k_=k, vc_=vc, TT_=TT, pv_=pv:
                           e.matmul(pv_[:, 0:TT_], wv[:, k_, vc_ * 128:(vc_ + 1) * 128],
                                    xln[:, k_, 0:TT_],
                                    start=(k_ == 0), stop=(k_ == DC - 1)),
                           reads=[b_wv, b_xln[k]], writes=[b_pv],
                           tag=f"atA{grp}vm{vc}_{k}")
                p.emit("scalar", lambda e, vc_=vc, TT_=TT, pv_=pv: e.activation(
                    atok[:, vc_, 0:TT_], pv_[:, 0:TT_], AF.Sigmoid),
                    reads=[b_pv], writes=[b_atok[vc]], tag=f"atA{grp}vs{vc}")
            for ti, te in enumerate(ext_tiles):
                sl = te % 3
                pt, b_pt = p.bank(2)
                for vc in range(2):
                    p.emit("tensor", lambda e, vc_=vc, ti_=ti, pt_=pt: e.transpose(
                        pt_.bitcast(BF16)[:, vc_ * 128:(vc_ + 1) * 128],
                        atok[:, vc_, ti_ * 128:(ti_ + 1) * 128], C.ident[:]),
                        reads=[b_atok[vc], C.b_ident], writes=[b_pt],
                        tag=f"atA{grp}vt{te}_{vc}")
                p.emit("vector", lambda e, sl_=sl, pt_=pt: e.tensor_copy(
                    vtok[:, sl_, :].rearrange("p (h n) -> p h n", n=9)[:, :, 0:8],
                    pt_.bitcast(BF16)[:, 0:256].rearrange(
                        "p (h n) -> p h n", n=8)),
                    reads=[b_pt], writes=[b_vtok[sl]], tag=f"atA{grp}ve{te}")
            if grp == 0:
                continue
            for ti, te in enumerate(ext_tiles):
                tq = te - 1
                qsl = te % 2
                avb, b_avb = p.bank(6 + te % 2)
                for h in range(NHEAD):
                    hb = 32 * (h % 4)
                    htile = h // 4
                    psc, b_psc = p.bank(2 + h % 4)
                    p.emit("tensor", lambda e, h_=h, hb_=hb, ht_=htile,
                           qsl_=qsl, te_=te, ps_=psc: e.matmul(
                               ps_[:, 0:128],
                               ktp[hb_:hb_ + 32, ht_, te_ % 3, :],
                               qtp[hb_:hb_ + 32, ht_, qsl_, :],
                               start=True, stop=False,
                               tile_position=(hb_, 0)),
                           reads=[b_ktp[te % 3], b_qtp[qsl]], writes=[b_psc],
                           tag=f"atA_s{te}_{h}")
                    p.emit("tensor", lambda e, ps_=psc: e.matmul(
                        ps_[:, 0:128], C.ident[:], tri[:],
                        start=False, stop=True),
                        reads=[C.b_ident, b_tri], writes=[b_psc],
                        tag=f"atA_m{te}_{h}")
                    p.emit("tensor", lambda e, h_=h, hb_=hb, ht_=htile,
                           qsl_=qsl, te_=te, ps_=psc: e.matmul(
                               ps_[:, 128:256],
                               ktp[hb_:hb_ + 32, ht_, (te_ - 1) % 3, :],
                               qtp[hb_:hb_ + 32, ht_, qsl_, :],
                               start=True, stop=True,
                               tile_position=(hb_, 0)),
                           reads=[b_ktp[(te - 1) % 3], b_qtp[qsl]],
                           writes=[b_psc], tag=f"atA_l{te}_{h}")
                    for kt in range(2):
                        p.emit("scalar", lambda e, h_=h, kt_=kt, ps_=psc:
                               e.activation(
                                   probs[:, h_ % 4, kt_, :],
                                   ps_[:, kt_ * 128:(kt_ + 1) * 128], AF.Exp,
                                   bias=biasc[:, kt_:kt_ + 1],
                                   scale=np.float32(0.125)),
                               reads=[b_psc], slots=[b_biasc],
                               writes=[b_probs[h % 4]],
                               tag=f"atA_e{te}_{h}_{kt}")
                    for kt in range(2):
                        ksl = (te - kt) % 3
                        p.emit("tensor", lambda e, h_=h, kt_=kt, ksl_=ksl,
                               av_=avb: e.matmul(
                                   av_[:, 9 * h_:9 * h_ + 9],
                                   probs[:, h_ % 4, kt_, :],
                                   vtok[:, ksl_, 9 * h_:9 * h_ + 9],
                                   start=(kt_ == 0), stop=(kt_ == 1)),
                               reads=[b_probs[h % 4], b_vtok[ksl]],
                               writes=[b_avb], tag=f"atA_av{te}_{h}_{kt}")
                avv = avb[:, 0:288].rearrange("p (h n) -> p h n", n=9)
                p.emit("vector", lambda e, av_=avv, sl_=te % 2: e.reciprocal(
                    recip[:, sl_, :], av_[:, :, 8:9]),
                    reads=[b_avb], writes=[b_recip[te % 2]],
                    tag=f"atA_r{te}")
                p.emit("vector", lambda e, av_=avv, sl_=te % 2, tq_=tq:
                       e.tensor_tensor(
                           atok[:, tq_ % 2, :].rearrange("p (h n) -> p h n", n=8),
                           av_[:, :, 0:8],
                           recip[:, sl_, :].unsqueeze(2).broadcast_to(
                               [128, NHEAD, 8]),
                           op=ALU.mult),
                       reads=[b_avb, b_recip[te % 2]], writes=[b_atok[tq % 2]],
                       tag=f"atA_o{te}")
                ptf, b_ptf = p.bank(2 + (te % 2))
                for vc in range(2):
                    p.emit("tensor", lambda e, vc_=vc, tq_=tq, pt_=ptf:
                           e.transpose(
                               pt_.bitcast(BF16)[:, vc_ * 128:(vc_ + 1) * 128],
                               atok[:, tq_ % 2, vc_ * 128:(vc_ + 1) * 128],
                               C.ident[:]),
                           reads=[b_atok[tq % 2], C.b_ident], writes=[b_ptf],
                           tag=f"atA_t{te}_{vc}")
                for vc in range(2):
                    p.emit("vector", lambda e, vc_=vc, tq_=tq, pt_=ptf:
                           e.tensor_copy(
                               afm[:, vc_, tq_ * 128:(tq_ + 1) * 128],
                               pt_.bitcast(BF16)[:, vc_ * 128:(vc_ + 1) * 128]),
                           reads=[b_ptf], writes=[b_afm[vc]],
                           tag=f"atA_f{te}_{vc}")


def _l0_attn_phase_b(p, C, ln, io, xacc, afm, b_afm, beta, b_beta, dv, b_dv,
                     dvb, b_dvb):
    with p.scope():
        xln = p.sbuf("atB_xln", [128, DC, 256], BF16)
        b_xln = [p.buf(f"atB_xln{c}") for c in range(DC)]
        wg = p.sbuf("atB_wg", [128, 2, DC, 128], BF16)
        b_wg = [p.buf(f"atB_wg{i}") for i in range(2)]
        wo = p.sbuf("atB_wo", [128, 2, 2, 128], BF16)
        b_wo = [p.buf(f"atB_wo{i}") for i in range(2)]
        gate = p.sbuf("atB_gate", [128, 2, 256], BF16)
        b_gate = [p.buf(f"atB_gate{i}") for i in range(2)]
        ysb = p.sbuf("atB_y", [128, DC, 256], F32)
        b_ysb = [p.buf(f"atB_y{c}") for c in range(DC)]

        for oc in range(16):
            r = oc % 2
            pb, b_pb = p.bank(r)
            p.emit("gpsimd", lambda g, oc_=oc, r_=r: g.dma_start(
                out=wo[:, r_, :, :],
                in_=io["wo_t"][:, oc_ * 128:(oc_ + 1) * 128]
                .rearrange("(c p) n -> p c n", p=128)),
                writes=[b_wo[r]], dma=True, tag=f"atBbw{oc}")
            for c in range(2):
                p.emit("tensor", lambda e, c_=c, r_=r, pb_=pb: e.matmul(
                    pb_[:, 0:1], wo[:, r_, c_, :], dvb[:, c_, :],
                    start=(c_ == 0), stop=(c_ == 1)),
                    reads=[b_wo[r], b_dvb], writes=[b_pb], tag=f"atBbm{oc}_{c}")
            p.emit("vector", lambda e, oc_=oc, pb_=pb: e.tensor_copy(
                beta[:, oc_:oc_ + 1], pb_[:, 0:1]),
                reads=[b_pb], writes=[b_beta], tag=f"atBbe{oc}")

        for grp in range(8):
            ln.emit(f"atB{grp}ln", 256,
                    src=lambda c: (xacc(c)[0][:, grp * 256:(grp + 1) * 256],
                                   xacc(c)[1]),
                    dst=lambda c: (xln[:, c, :], b_xln[c]))
            for oc in range(DC):
                r = oc % 2
                pgt, b_pgt = p.bank(r)
                p.emit("gpsimd", lambda g, oc_=oc, r_=r: g.dma_start(
                    out=wg[:, r_, :, :],
                    in_=io["wg_t"][:, oc_ * 128:(oc_ + 1) * 128]
                    .rearrange("(c p) n -> p c n", p=128)),
                    writes=[b_wg[r]], dma=True, tag=f"atB{grp}gw{oc}")
                for k in range(DC):
                    p.emit("tensor", lambda e, k_=k, r_=r, pg_=pgt: e.matmul(
                        pg_[:, 0:256], wg[:, r_, k_, :], xln[:, k_, :],
                        start=(k_ == 0), stop=(k_ == DC - 1)),
                        reads=[b_wg[r], b_xln[k]], writes=[b_pgt],
                        tag=f"atB{grp}gm{oc}_{k}")
                p.emit("scalar", lambda e, r_=r, pg_=pgt: e.activation(
                    gate[:, r_, :], pg_[:, 0:256], AF.Sigmoid),
                    reads=[b_pgt], writes=[b_gate[r]], tag=f"atB{grp}gs{oc}")
                pat, b_pat = p.bank(2 + r)
                p.emit("gpsimd", lambda g, oc_=oc, r_=r: g.dma_start(
                    out=wo[:, r_, :, :],
                    in_=io["wo_t"][:, oc_ * 128:(oc_ + 1) * 128]
                    .rearrange("(c p) n -> p c n", p=128)),
                    writes=[b_wo[r]], dma=True, tag=f"atB{grp}ow{oc}")
                for c in range(2):
                    p.emit("vector", lambda e, c_=c, r_=r: e.tensor_scalar(
                        wo[:, r_, c_, :], wo[:, r_, c_, :], dv[:, c_, 1:2],
                        None, op0=ALU.mult),
                        reads=[b_dv], writes=[b_wo[r]], tag=f"atB{grp}os{oc}_{c}")
                for c in range(2):
                    p.emit("tensor", lambda e, c_=c, r_=r, grp_=grp, pa_=pat:
                           e.matmul(
                               pa_[:, 0:256], wo[:, r_, c_, :],
                               afm[:, c_, grp_ * 256:(grp_ + 1) * 256],
                               start=(c_ == 0), stop=(c_ == 1)),
                           reads=[b_wo[r], b_afm[c]], writes=[b_pat],
                           tag=f"atB{grp}om{oc}_{c}")
                p.emit("vector", lambda e, oc_=oc, r_=r, pa_=pat:
                       e.scalar_tensor_tensor(
                           ysb[:, oc_, :], pa_[:, 0:256], beta[:, oc_:oc_ + 1],
                           gate[:, r_, :], op0=ALU.add, op1=ALU.mult),
                       reads=[b_pat, b_gate[r]], slots=[b_beta],
                       writes=[b_ysb[oc]], tag=f"atB{grp}ev{oc}")
            ln.emit(f"atB{grp}ln2", 256,
                    src=lambda c: (ysb[:, c, :], b_ysb[c]),
                    dst=None,
                    resid=lambda c: (xacc(c)[0][:, grp * 256:(grp + 1) * 256],
                                     xacc(c)[1]))



LH = 16
LD = 128


def emit_l1_attn(p, C, ln, io, xacc):
    _l1_main(p, C, ln, io, xacc)
    _l1_p3(p, C, ln, io, xacc)


def _l1_main(p, C, ln, io, xacc):
    with p.scope():
        _l1_main2(p, C, ln, io, xacc)


def _l1_main2(p, C, ln, io, xacc):
    dk = p.sbuf("la_dk", [1, 4, LH], F32)
    b_dk = p.buf("la_dk")
    lndb = p.sbuf("la_lndb", [128, LH], F32)
    b_lndb = p.buf("la_lndb")
    mm = p.sbuf("la_mm", [128, 128], F32)
    b_mm = p.buf("la_mm")
    tri01 = p.sbuf("la_tri01", [128, 128], BF16)
    b_tri01 = p.buf("la_tri01")
    Ah = p.sbuf("la_A", [128, LH, 128], BF16)
    b_Ah = p.buf("la_A")
    af = p.sbuf("la_af", [128, 1, 128], F32)
    b_af = p.buf("la_af")
    dprow = p.sbuf("la_dprow", [1, LH, 128], BF16)
    b_dprow = p.buf("la_dprow")
    dprf = p.sbuf("la_dprf", [1, 2, 128], F32)
    b_dprf = p.buf("la_dprf")
    d128b = p.sbuf("la_d128b", [1, LH], BF16)
    b_d128b = p.buf("la_d128b")
    dvall = p.sbuf("la_dvall", [128, 16, LH], BF16)
    b_dvall = p.buf("la_dvall")
    dvf = p.sbuf("la_dvf", [128, 2, 16], F32)
    b_dvf = p.buf("la_dvf")

    p.emit("gpsimd", lambda g: g.dma_start(out=dk[:, 0, :], in_=io["decay"][:, :]),
           writes=[b_dk], dma=True, tag="la_lddk")
    p.emit("scalar", lambda e: e.activation(dk[:, 1, :], dk[:, 0, :], AF.Sigmoid),
           reads=[b_dk], writes=[b_dk], tag="la_sig")
    p.emit("scalar", lambda e: e.activation(dk[:, 2, :], dk[:, 1, :], AF.Ln),
           writes=[b_dk], tag="la_ln")
    p.emit("scalar", lambda e: e.activation(dk[:, 3, :], dk[:, 2, :], AF.Exp,
           scale=np.float32(128.0)), writes=[b_dk], tag="la_d128")
    p.emit("vector", lambda e: e.tensor_copy(d128b[:, :], dk[:, 3, :]),
           reads=[b_dk], writes=[b_d128b], tag="la_d128b")
    pb0, b_pb0 = p.bank(0)
    p.emit("tensor", lambda e, ps_=pb0: e.matmul(ps_[:, 0:LH], C.ones_row_f[:],
           dk[:, 2, :], start=True, stop=True),
           reads=[b_dk, C.b_ones_row_f], writes=[b_pb0], tag="la_bl")
    p.emit("vector", lambda e, ps_=pb0: e.tensor_copy(lndb[:], ps_[:, 0:LH]),
           reads=[b_pb0], writes=[b_lndb], tag="la_ble")
    p.emit("gpsimd", lambda g: g.iota(mm[:], pattern=[[1, 128]], base=0,
           channel_multiplier=-1, allow_small_or_imprecise_dtypes=True),
           writes=[b_mm], tag="la_m0")
    p.emit("vector", lambda e: e.tensor_scalar(mm[:], mm[:], 0.0, None,
           op0=ALU.max), reads=[b_mm], writes=[b_mm], tag="la_m1")
    p.emit("gpsimd", lambda g: g.memset(tri01[:], 0.0), writes=[b_tri01],
           tag="la_t0")
    p.emit("gpsimd", lambda g: g.affine_select(
        out=tri01[:], in_=tri01[:], compare_op=ALU.is_gt, fill=1.0,
        base=0, pattern=[[-1, 128]], channel_multiplier=1),
        writes=[b_tri01], tag="la_t1")
    p.emit("gpsimd", lambda g: g.iota(dprf[:, 0, :], pattern=[[1, 128]], base=1,
           channel_multiplier=0, allow_small_or_imprecise_dtypes=True),
           writes=[b_dprf], tag="la_dp0")
    p.emit("gpsimd", lambda g: g.iota(dvf[:, 0, :], pattern=[[-128, 16]],
           base=2047, channel_multiplier=-1,
           allow_small_or_imprecise_dtypes=True),
           writes=[b_dvf], tag="la_dv0")
    for h in range(LH):
        p.emit("scalar", lambda e, h_=h: e.activation(
            af[:, 0, :], mm[:], AF.Exp, scale=lndb[:, h_:h_ + 1]),
            reads=[b_mm], slots=[b_lndb], writes=[b_af], tag=f"la_Ae{h}")
        p.emit("vector", lambda e, h_=h: e.tensor_tensor(
            Ah[:, h_, :], af[:, 0, :], tri01[:], op=ALU.mult),
            reads=[b_af, b_tri01], writes=[b_Ah], tag=f"la_Am{h}")
        p.emit("scalar", lambda e, h_=h: e.activation(
            dprf[:, 1, :], dprf[:, 0, :], AF.Exp, scale=dk[:, 2, h_:h_ + 1]),
            reads=[b_dprf], slots=[b_dk], writes=[b_dprf], tag=f"la_dpe{h}")
        p.emit("vector", lambda e, h_=h: e.tensor_copy(
            dprow[:, h_, :], dprf[:, 1, :]),
            reads=[b_dprf], writes=[b_dprow], tag=f"la_dpc{h}")
        p.emit("scalar", lambda e, h_=h: e.activation(
            dvf[:, 1, :], dvf[:, 0, :], AF.Exp, scale=lndb[:, h_:h_ + 1]),
            reads=[b_dvf], slots=[b_lndb], writes=[b_dvf], tag=f"la_dve{h}")
        p.emit("vector", lambda e, h_=h: e.tensor_copy(
            dvall[:, :, h_], dvf[:, 1, :]),
            reads=[b_dvf], writes=[b_dvall], tag=f"la_dvc{h}")

    gkv = p.sbuf("la_gkv", [1, 2048], F32)
    b_gkv = p.buf("la_gkv")
    gks = p.sbuf("la_gks", [1, 2048], F32)
    b_gks = p.buf("la_gks")
    p.emit("gpsimd", lambda g: g.memset(gkv[:], 0.0), writes=[b_gkv], tag="la_g0")
    p.emit("gpsimd", lambda g: g.memset(gks[:], 0.0), writes=[b_gks], tag="la_g1")

    _l1_p1(p, C, ln, io, xacc, dvall, b_dvall, gkv, b_gkv, gks, b_gks)
    Tkv, b_Tkv = gkv, b_gkv
    Tks, b_Tks = gks, b_gks
    _l1_exchange(p, C, io, gkv, b_gkv, gks, b_gks, Tkv, b_Tkv, Tks, b_Tks)
    _l1_p2(p, C, io, Ah, b_Ah, dprow, b_dprow, d128b, b_d128b,
           Tkv, b_Tkv, Tks, b_Tks)


def _l1_p1(p, C, ln, io, xacc, dvall, b_dvall, gkv, b_gkv, gks, b_gks):
    NBORDER = [0, 1, 2, 3, 4, 8, 5, 9, 6, 10, 7, 11]
    with p.scope():
        xln = p.sbuf("l1a_xln", [128, DC, TG], BF16)
        b_xln = [p.buf(f"l1a_xln{c}") for c in range(DC)]
        wt = p.sbuf("l1a_wt", [128, 2, 512], BF16)
        b_wt = [p.buf(f"l1a_wt{i}") for i in range(2)]
        s1 = p.sbuf("l1a_s1", [128, 1, 512], F32)
        b_s1 = [p.buf("l1a_s10")]
        kb = p.sbuf("l1a_kb", [128, 4, 512], BF16)
        b_kb = [p.buf(f"l1a_kb{i}") for i in range(4)]
        vb = p.sbuf("l1a_vb", [128, 2, 512], BF16)
        b_vb = [p.buf(f"l1a_vb{i}") for i in range(2)]
        kvb = p.sbuf("l1a_kvb", [128, 2, 512], BF16)
        b_kvb = [p.buf(f"l1a_kvb{i}") for i in range(2)]

        for g in range(NG):
            ln.emit(f"l1a{g}ln", TG,
                    src=lambda c: (xacc(c)[0][:, g * TG:(g + 1) * TG], xacc(c)[1]),
                    dst=lambda c: (xln[:, c, :], b_xln[c]))
            for nb in NBORDER:
                kind = nb // 4
                co = nb % 4
                pqs = [p.bank(t) for t in range(4)]
                for kc in range(DC):
                    r = kc % 2
                    p.emit("gpsimd", lambda e, kc_=kc, nb_=nb, r_=r:
                           e.dma_start(
                               out=wt[:, r_, :],
                               in_=io["wqkv"][kc_ * 128:(kc_ + 1) * 128,
                                              nb_ * 512:(nb_ + 1) * 512]),
                           writes=[b_wt[r]], dma=True,
                           tag=f"l1a{g}w{nb}_{kc}")
                    for tt in range(4):
                        p.emit("tensor", lambda e, kc_=kc, tt_=tt, r_=r,
                               pq_=pqs[tt][0]: e.matmul(
                                   pq_[:, 0:512],
                                   xln[:, kc_, tt_ * 128:(tt_ + 1) * 128],
                                   wt[:, r_, :],
                                   start=(kc_ == 0), stop=(kc_ == DC - 1)),
                               reads=[b_wt[r], b_xln[kc]], writes=[pqs[tt][1]],
                               tag=f"l1a{g}m{nb}_{tt}_{kc}")
                for tt in range(4):
                    cc = 4 * g + tt
                    pq, b_pq = pqs[tt]
                    rr = tt % 2
                    if kind == 0:
                        _phi_evac(p, pq, b_pq, s1, b_s1, kvb, b_kvb, rr,
                                  f"l1a{g}q{nb}_{tt}")
                        p.emit("gpsimd", lambda e, cc_=cc, co_=co, r_=rr:
                               e.dma_start(
                                   out=io["scr_q"][cc_, :,
                                                   co_ * 512:(co_ + 1) * 512],
                                   in_=kvb[:, r_, :]),
                               reads=[b_kvb[rr]], dma=True,
                               tag=f"l1a{g}qs{nb}_{tt}")
                    elif kind == 1:
                        _phi_evac(p, pq, b_pq, s1, b_s1, kb, b_kb, tt,
                                  f"l1a{g}k{nb}_{tt}")
                        p.emit("gpsimd", lambda e, cc_=cc, co_=co, tt_=tt:
                               e.dma_start(
                                   out=io["scr_k"][cc_, :,
                                                   co_ * 512:(co_ + 1) * 512],
                                   in_=kb[:, tt_, :]),
                               reads=[b_kb[tt]], dma=True,
                               tag=f"l1a{g}ks{nb}_{tt}")
                    else:
                        p.emit("vector", lambda e, tt_=tt, r_=rr, pq_=pq:
                               e.tensor_copy(vb[:, r_, :], pq_[:, 0:512]),
                               reads=[b_pq], writes=[b_vb[rr]],
                               tag=f"l1a{g}v{nb}_{tt}")
                        p.emit("vector", lambda e, tt_=tt, r_=rr:
                               e.tensor_tensor(
                                   kvb[:, r_, :], kb[:, tt_, :], vb[:, r_, :],
                                   op=ALU.mult),
                               reads=[b_kb[tt], b_vb[rr]], writes=[b_kvb[rr]],
                               tag=f"l1a{g}kv{nb}_{tt}")
                        p.emit("gpsimd", lambda e, cc_=cc, co_=co, r_=rr:
                               e.dma_start(
                                   out=io["scr_kv"][cc_, :,
                                                    co_ * 512:(co_ + 1) * 512],
                                   in_=kvb[:, r_, :]),
                               reads=[b_kvb[rr]], dma=True,
                               tag=f"l1a{g}kvs{nb}_{tt}")
                        p4, b_p4 = p.bank(4)
                        p5, b_p5 = p.bank(5)
                        for hh in range(4):
                            h = co * 4 + hh
                            p.emit("tensor", lambda e, h_=h, hh_=hh, cc_=cc,
                                   r_=rr, ps_=p4: e.matmul(
                                       ps_[0:1, 128 * hh_:128 * hh_ + 128],
                                       dvall[:, cc_, h_:h_ + 1],
                                       kvb[:, r_, 128 * hh_:128 * hh_ + 128],
                                       start=True, stop=True),
                                   reads=[b_kvb[rr], b_dvall],
                                   writes=[b_p4], tag=f"l1a{g}st{nb}_{tt}_{hh}")
                            p.emit("tensor", lambda e, h_=h, hh_=hh, cc_=cc,
                                   tt_=tt, ps_=p5: e.matmul(
                                       ps_[0:1, 128 * hh_:128 * hh_ + 128],
                                       dvall[:, cc_, h_:h_ + 1],
                                       kb[:, tt_, 128 * hh_:128 * hh_ + 128],
                                       start=True, stop=True),
                                   reads=[b_kb[tt], b_dvall],
                                   writes=[b_p5], tag=f"l1a{g}su{nb}_{tt}_{hh}")
                        cols = slice(co * 512, co * 512 + 512)
                        p.emit("vector", lambda e, cols_=cols, ps_=p4:
                               e.tensor_tensor(gkv[0:1, cols_],
                                               gkv[0:1, cols_],
                                               ps_[0:1, 0:512], op=ALU.add),
                               reads=[b_p4], writes=[b_gkv],
                               tag=f"l1a{g}ga{nb}_{tt}")
                        p.emit("vector", lambda e, cols_=cols, ps_=p5:
                               e.tensor_tensor(gks[0:1, cols_],
                                               gks[0:1, cols_],
                                               ps_[0:1, 0:512], op=ALU.add),
                               reads=[b_p5], writes=[b_gks],
                               tag=f"l1a{g}gb{nb}_{tt}")


def _phi_evac(p, pq, b_pq, s1, b_s1, dst, b_dst, slot, tag):
    r = 0
    p.emit("vector", lambda e, r_=r, pq_=pq: e.tensor_scalar(
        s1[:, r_, :], pq_[:, 0:512], 0.0, None, op0=ALU.min),
        reads=[b_pq], writes=[b_s1[r]], tag=tag + "min")
    p.emit("scalar", lambda e, r_=r: e.activation(
        s1[:, r_, :], s1[:, r_, :], AF.Exp),
        reads=[b_s1[r]], writes=[b_s1[r]], tag=tag + "exp")
    p.emit("vector", lambda e, r_=r, s_=slot, pq_=pq: e.scalar_tensor_tensor(
        dst[:, s_, :], pq_[:, 0:512], 0.0, s1[:, r_, :],
        op0=ALU.max, op1=ALU.add),
        reads=[b_pq, b_s1[r]], writes=[b_dst[slot]], tag=tag + "stt")


def _l1_exchange(p, C, io, gkv, b_gkv, gks, b_gks, Tkv, b_Tkv, Tks, b_Tks):
    with p.scope():
        _l1_exchange2(p, C, io, gkv, b_gkv, gks, b_gks, Tkv, b_Tkv, Tks, b_Tks)


def _l1_exchange2(p, C, io, gkv, b_gkv, gks, b_gks, Tkv, b_Tkv, Tks, b_Tks):
    selb = p.sbuf("la_sel", [16, 2], F32)
    b_selb = p.buf("la_sel")
    gath = p.sbuf("la_gath", [16, 2048], F32)
    b_gath = p.buf("la_gath")
    p.emit("gpsimd", lambda g: g.dma_start(out=selb[:], in_=io["sel"][:, :]),
           writes=[b_selb], dma=True, tag="la_ldsel")
    w0 = p.emit("gpsimd", lambda g: g.dma_start(out=io["cc_in"][0:1, :],
                in_=gkv[:]), reads=[b_gkv], dma=True, tag="la_ccw0")
    w1 = p.emit("gpsimd", lambda g: g.dma_start(out=io["cc_in"][1:2, :],
                in_=gks[:]), reads=[b_gks], dma=True, tag="la_ccw1")
    cc_op = p.emit("gpsimd", lambda g: g.collective_compute(
        "AllGather", ALU.bypass, replica_groups=[list(range(NCORES))],
        ins=[io["cc_in"].opt()], outs=[io["cc_out"].opt()]),
        deps=[w0, w1], tag="la_cc")
    p.emit("gpsimd", lambda g: g.dma_start(out=gath[:], in_=io["cc_out"][:, :]),
           deps=[cc_op], writes=[b_gath], dma=True, tag="la_ccr")
    for j in range(4):
        pr, b_pr = p.bank(j % 2)
        pr2, b_pr2 = p.bank(2 + j % 2)
        p.emit("tensor", lambda e, j_=j, pr_=pr: e.matmul(
            pr_[0:1, 0:512], selb[:, 0:1], gath[:, j_ * 512:(j_ + 1) * 512],
            start=True, stop=True),
            reads=[b_selb, b_gath], writes=[b_pr], tag=f"la_sm{j}")
        p.emit("tensor", lambda e, j_=j, pr_=pr2: e.matmul(
            pr_[0:1, 0:512], selb[:, 1:2], gath[:, j_ * 512:(j_ + 1) * 512],
            start=True, stop=True),
            reads=[b_selb, b_gath], writes=[b_pr2], tag=f"la_sn{j}")
        p.emit("vector", lambda e, j_=j, pr_=pr: e.tensor_copy(
            Tkv[0:1, 512 * j_:512 * (j_ + 1)], pr_[0:1, 0:512]),
            reads=[b_pr], writes=[b_Tkv], tag=f"la_se0{j}")
        p.emit("vector", lambda e, j_=j, pr_=pr2: e.tensor_copy(
            Tks[0:1, 512 * j_:512 * (j_ + 1)], pr_[0:1, 0:512]),
            reads=[b_pr2], writes=[b_Tks], tag=f"la_se1{j}")


def _l1_p2(p, C, io, Ah, b_Ah, dprow, b_dprow, d128b, b_d128b,
           Tkv, b_Tkv, Tks, b_Tks):
    with p.scope():
        kvt = p.sbuf("l1b_kvt", [128, 1, 2048], BF16)
        b_kvt = [p.buf("l1b_kvt0")]
        kt = p.sbuf("l1b_kt", [128, 1, 2048], BF16)
        b_kt = [p.buf("l1b_kt0")]
        qt = p.sbuf("l1b_qt", [128, 2048], BF16)
        b_qt = p.buf("l1b_qt")
        Tbf = p.sbuf("l1b_Tbf", [1, 2, 2048], BF16)
        b_Tbf = p.buf("l1b_Tbf")
        tmp = p.sbuf("l1b_tmp", [128, 512], F32)
        b_tmp = p.buf("l1b_tmp")
        den = p.sbuf("l1b_den", [128, 2, LH], F32)
        b_den = p.buf("l1b_den")
        atk = p.sbuf("l1b_atk", [128, 2, 512], BF16)
        b_atk = [p.buf(f"l1b_atk{i}") for i in range(2)]

        for cc in range(16):
            r = 0
            p.emit("sync", lambda e, cc_=cc, r_=r: e.dma_start(
                out=kvt[:, r_, :], in_=io["scr_kv"][cc_, :, :]),
                writes=[b_kvt[r]], dma=True, tag=f"l1b{cc}lkv")
            p.emit("sync", lambda e, cc_=cc, r_=r: e.dma_start(
                out=kt[:, r_, :], in_=io["scr_k"][cc_, :, :]),
                writes=[b_kt[r]], dma=True, tag=f"l1b{cc}lk")
            p.emit("sync", lambda e, cc_=cc: e.dma_start(
                out=qt[:], in_=io["scr_q"][cc_, :, :]),
                writes=[b_qt], dma=True, tag=f"l1b{cc}lq")
            p.emit("gpsimd", lambda g: g.dma_start(out=Tbf[0:1, 0, :],
                   in_=Tkv[:]), reads=[b_Tkv], writes=[b_Tbf], dma=True,
                   tag=f"l1b{cc}cb0")
            p.emit("gpsimd", lambda g: g.dma_start(out=Tbf[0:1, 1, :],
                   in_=Tks[:]), reads=[b_Tks], writes=[b_Tbf], dma=True,
                   tag=f"l1b{cc}cb1")
            for sc in range(2):
                srcb = (b_kvt if sc == 0 else b_kt)[r]
                srct = kvt if sc == 0 else kt
                for h in range(LH):
                    bk, b_bk = p.bank(4 * sc + h // 4)
                    colr = slice(128 * (h % 4), 128 * (h % 4) + 128)
                    p.emit("tensor", lambda e, h_=h, colr_=colr, bk_=bk,
                           st_=srct, r_=r: e.matmul(
                               bk_[:, colr_], Ah[:, h_, :],
                               st_[:, r_, 128 * h_:128 * h_ + 128],
                               start=True, stop=False),
                           reads=[srcb, b_Ah], writes=[b_bk],
                           tag=f"l1b{cc}A{sc}_{h}")
                    p.emit("tensor", lambda e, h_=h, sc_=sc, colr_=colr,
                           bk_=bk: e.matmul(
                               bk_[:, colr_], dprow[:, h_, :],
                               Tbf[0:1, sc_, 128 * h_:128 * h_ + 128],
                               start=False, stop=True),
                           reads=[b_dprow, b_Tbf], writes=[b_bk],
                           tag=f"l1b{cc}C{sc}_{h}")
            for j in range(4):
                bk, b_bk = p.bank(4 + j)
                p.emit("vector", lambda e, j_=j, bk_=bk: e.tensor_tensor(
                    tmp[:], qt[:, 512 * j_:512 * (j_ + 1)], bk_[:, :],
                    op=ALU.mult), reads=[b_qt, b_bk], writes=[b_tmp],
                    tag=f"l1b{cc}dm{j}")
                p.emit("vector", lambda e, j_=j: e.tensor_reduce(
                    den[:, 0, 4 * j_:4 * j_ + 4],
                    tmp[:].rearrange("p (h d) -> p h d", d=128),
                    axis=AX.X, op=ALU.add),
                    reads=[b_tmp], writes=[b_den], tag=f"l1b{cc}dr{j}")
            p.emit("vector", lambda e: e.tensor_scalar(
                den[:, 0, :], den[:, 0, :], 1e-6, None, op0=ALU.max),
                writes=[b_den], tag=f"l1b{cc}dc")
            p.emit("vector", lambda e: e.reciprocal(den[:, 1, :], den[:, 0, :]),
                   reads=[b_den], writes=[b_den], tag=f"l1b{cc}di")
            for j in range(4):
                bk, b_bk = p.bank(j)
                p.emit("vector", lambda e, j_=j, bk_=bk: e.tensor_tensor(
                    tmp[:], qt[:, 512 * j_:512 * (j_ + 1)], bk_[:, :],
                    op=ALU.mult), reads=[b_qt, b_bk], writes=[b_tmp],
                    tag=f"l1b{cc}om{j}")
                p.emit("vector", lambda e, j_=j: e.tensor_tensor(
                    atk[:, j_ % 2, :].rearrange("p (h d) -> p h d", d=128),
                    tmp[:].rearrange("p (h d) -> p h d", d=128),
                    den[:, 1, 4 * j_:4 * j_ + 4].unsqueeze(2)
                    .broadcast_to([128, 4, 128]),
                    op=ALU.mult),
                    reads=[b_tmp, b_den], writes=[b_atk[j % 2]],
                    tag=f"l1b{cc}oo{j}")
                p.emit("gpsimd", lambda e, cc_=cc, j_=j: e.dma_start(
                    out=io["scr_at"][cc_, :, 512 * j_:512 * (j_ + 1)],
                    in_=atk[:, j_ % 2, :]),
                    reads=[b_atk[j % 2]], dma=True, tag=f"l1b{cc}os{j}")
            if cc < 15:
                for sc in range(2):
                    srcb = (b_kvt if sc == 0 else b_kt)[r]
                    srct = kvt if sc == 0 else kt
                    for h in range(LH):
                        bk, b_bk = p.bank(4 * sc + h // 4)
                        colr = slice(128 * (h % 4), 128 * (h % 4) + 128)
                        p.emit("tensor", lambda e, h_=h, colr_=colr, bk_=bk,
                               st_=srct, r_=r: e.matmul(
                                   bk_[0:1, colr_], Ah[:, h_, 127:128],
                                   st_[:, r_, 128 * h_:128 * h_ + 128],
                                   start=True, stop=False),
                               reads=[srcb, b_Ah], writes=[b_bk],
                               tag=f"l1b{cc}sv{sc}_{h}")
                        p.emit("tensor", lambda e, h_=h, sc_=sc, colr_=colr,
                               bk_=bk: e.matmul(
                                   bk_[0:1, colr_], d128b[:, h_:h_ + 1],
                                   Tbf[0:1, sc_, 128 * h_:128 * h_ + 128],
                                   start=False, stop=True),
                               reads=[b_d128b, b_Tbf], writes=[b_bk],
                               tag=f"l1b{cc}sw{sc}_{h}")
                for j in range(4):
                    b0, bb0 = p.bank(j)
                    b1, bb1 = p.bank(4 + j)
                    p.emit("vector", lambda e, j_=j, b_=b0: e.tensor_copy(
                        Tkv[0:1, 512 * j_:512 * (j_ + 1)], b_[0:1, 0:512]),
                        reads=[bb0], writes=[b_Tkv], tag=f"l1b{cc}te0{j}")
                    p.emit("vector", lambda e, j_=j, b_=b1: e.tensor_copy(
                        Tks[0:1, 512 * j_:512 * (j_ + 1)], b_[0:1, 0:512]),
                        reads=[bb1], writes=[b_Tks], tag=f"l1b{cc}te1{j}")


def _l1_p3(p, C, ln, io, xacc):
    with p.scope():
        ain = p.sbuf("l1c_ain", [128, 1, 2048], BF16)
        b_ain = [p.buf("l1c_ain0")]
        afm = p.sbuf("l1c_afm", [128, DC, 256], BF16)
        b_afm = [p.buf(f"l1c_afm{i}") for i in range(DC)]
        wout = p.sbuf("l1c_wout", [128, 2, DC, 128], BF16)
        b_wout = [p.buf(f"l1c_wout{i}") for i in range(2)]
        ysb = p.sbuf("l1c_y", [128, DC, 256], F32)
        b_ysb = [p.buf(f"l1c_y{c}") for c in range(DC)]

        for grp in range(8):
            for ti in range(2):
                cc = 2 * grp + ti
                r = 0
                p.emit("sync", lambda e, cc_=cc, r_=r: e.dma_start(
                    out=ain[:, r_, :], in_=io["scr_at"][cc_, :, :]),
                    writes=[b_ain[r]], dma=True, tag=f"l1c{cc}l")
                for c in range(DC):
                    pt, b_pt = p.bank(c % 2)
                    p.emit("tensor", lambda e, c_=c, r_=r, pt_=pt: e.transpose(
                        pt_.bitcast(BF16)[:, 0:128],
                        ain[:, r_, c_ * 128:(c_ + 1) * 128], C.ident[:]),
                        reads=[b_ain[r], C.b_ident], writes=[b_pt],
                        tag=f"l1c{cc}t{c}")
                    p.emit("vector", lambda e, c_=c, ti_=ti, pt_=pt:
                           e.tensor_copy(
                               afm[:, c_, ti_ * 128:(ti_ + 1) * 128],
                               pt_.bitcast(BF16)[:, 0:128]),
                           reads=[b_pt], writes=[b_afm[c]], tag=f"l1c{cc}e{c}")
            for oc in range(DC):
                r = oc % 2
                po, b_po = p.bank(2 + r)
                p.emit("gpsimd", lambda e, oc_=oc, r_=r: e.dma_start(
                    out=wout[:, r_, :, :],
                    in_=io["wout"][:, oc_ * 128:(oc_ + 1) * 128]
                    .rearrange("(c p) n -> p c n", p=128)),
                    writes=[b_wout[r]], dma=True, tag=f"l1c{grp}w{oc}")
                for k in range(DC):
                    p.emit("tensor", lambda e, k_=k, r_=r, po_=po: e.matmul(
                        po_[:, 0:256], wout[:, r_, k_, :], afm[:, k_, :],
                        start=(k_ == 0), stop=(k_ == DC - 1)),
                        reads=[b_wout[r], b_afm[k]], writes=[b_po],
                        tag=f"l1c{grp}m{oc}_{k}")
                p.emit("vector", lambda e, oc_=oc, po_=po: e.tensor_copy(
                    ysb[:, oc_, :], po_[:, 0:256]),
                    reads=[b_po], writes=[b_ysb[oc]], tag=f"l1c{grp}ev{oc}")
            ln.emit(f"l1c{grp}ln2", 256,
                    src=lambda c: (ysb[:, c, :], b_ysb[c]),
                    dst=None,
                    resid=lambda c: (xacc(c)[0][:, grp * 256:(grp + 1) * 256],
                                     xacc(c)[1]))



MTG = 1024
NIC = {0: 11, 1: 11, 2: 11, 3: 11, 4: 11, 5: 11, 6: 11, 7: 11, 8: 22}


def emit_l1_ffn(p, C, ln, io, xacc):
    b_scrw = p.buf("mf_scrw")
    b_scrw3 = p.buf("mf_scrw3")
    p.b_scrw, p.b_scrw3 = b_scrw, b_scrw3
    for w in range(2):
        for e in range(8):
            p.emit("gpsimd", lambda g, w_=w, e_=e: g.dma_start(
                out=io["scr_w12"][w_, e_],
                in_=io["w12t"][w_, e_].rearrange("a p k n -> a p (k n)")),
                writes=[b_scrw], dma=True, tag=f"mfW{w}_{e}")
        for hf in range(2):
            p.emit("gpsimd", lambda g, w_=w, hf_=hf: g.dma_start(
                out=io["scr_sw12"][w_, hf_ * 11:(hf_ + 1) * 11],
                in_=io["sw12t"][w_, hf_ * 11:(hf_ + 1) * 11]
                .rearrange("a p k n -> a p (k n)")),
                writes=[b_scrw], dma=True, tag=f"mfWs{w}_{hf}")
    for j in range(10):
        p.emit("gpsimd", lambda g, j_=j: g.dma_start(
            out=io["scr_w3"][j_ * 11:(j_ + 1) * 11],
            in_=io["w3catt"][j_ * 11:(j_ + 1) * 11]
            .rearrange("a p k n -> a p (k n)")),
            writes=[b_scrw3], dma=True, tag=f"mfW3_{j}")

    combT = p.sbuf("mf_combT", [8, 2048], BF16)
    b_combT = p.buf("mf_combT")
    eye8 = p.sbuf("mf_eye8", [8, 8, 128], BF16)
    b_eye8 = p.buf("mf_eye8")
    p.emit("gpsimd", lambda g: g.memset(eye8[:], 0.0), writes=[b_eye8],
           tag="mf_eye0")
    p.emit("gpsimd", lambda g: g.affine_select(
        out=eye8[:], in_=eye8[:], compare_op=ALU.not_equal, fill=1.0,
        base=0, pattern=[[-1, 8], [0, 128]], channel_multiplier=1),
        writes=[b_eye8], tag="mf_eye1")
    for g in range(2):
        _l1f_group(p, C, ln, io, xacc, g, combT, b_combT, eye8, b_eye8)
    _l1f_y(p, C, ln, io, xacc)


def _l1f_group(p, C, ln, io, xacc, g, combT, b_combT, eye8, b_eye8):
    with p.scope():
        xln = p.sbuf(f"mf{g}_xln", [128, DC, MTG], BF16)
        b_xln = [p.buf(f"mf{g}_xln{c}") for c in range(DC)]
        gw = p.sbuf(f"mf{g}_gw", [128, DC, 8], BF16)
        b_gw = p.buf(f"mf{g}_gw")
        lg = p.sbuf(f"mf{g}_lg", [128, 6, 8], F32)
        b_lg = p.buf(f"mf{g}_lg")
        lrow = p.sbuf(f"mf{g}_lrow", [128, 4], F32)
        b_lrow = p.buf(f"mf{g}_lrow")
        lgb = p.sbuf(f"mf{g}_lgb", [128, 8], BF16)
        b_lgb = p.buf(f"mf{g}_lgb")
        wt = p.sbuf(f"mf{g}_wt", [128, 2, DC, 128], BF16)
        b_wt = [p.buf(f"mf{g}_wt{i}") for i in range(2)]
        cb = p.sbuf(f"mf{g}_cb", [128, 2, 512], BF16)
        b_cb = [p.buf(f"mf{g}_cb{i}") for i in range(2)]
        sv = p.sbuf(f"mf{g}_sv", [128, 512], F32)
        b_sv = p.buf(f"mf{g}_sv")
        uv = p.sbuf(f"mf{g}_uv", [128, 1, 512], F32)
        b_uv = [p.buf(f"mf{g}_uv0"), None]
        hv = p.sbuf(f"mf{g}_hv", [128, 2, 512], BF16)
        b_hv = [p.buf(f"mf{g}_hv{i}") for i in range(2)]

        p.emit("gpsimd", lambda e: e.dma_start(
            out=gw[:],
            in_=io["gw_t"][:, :].rearrange("(c p) n -> p c n", p=128)),
            writes=[b_gw], dma=True, tag=f"mf{g}gw")
        for hf in range(2):
            ln.emit(f"mf{g}ln{hf}", 512,
                    src=lambda c: (xacc(c)[0][:, g * MTG + hf * 512:
                                              g * MTG + (hf + 1) * 512],
                                   xacc(c)[1]),
                    dst=lambda c: (xln[:, c, hf * 512:(hf + 1) * 512],
                                   b_xln[c]))
        for tt in range(8):
            pgt, b_pgt = p.bank(tt % 2)
            for k in range(DC):
                p.emit("tensor", lambda e, k_=k, tt_=tt, pg_=pgt: e.matmul(
                    pg_[:, 0:8], xln[:, k_, tt_ * 128:(tt_ + 1) * 128],
                    gw[:, k_, :], start=(k_ == 0), stop=(k_ == DC - 1)),
                    reads=[b_gw, b_xln[k]], writes=[b_pgt],
                    tag=f"mf{g}g{tt}_{k}")
            p.emit("vector", lambda e, pg_=pgt: e.tensor_copy(
                lg[:, 0, :], pg_[:, 0:8]), reads=[b_pgt], writes=[b_lg],
                tag=f"mf{g}s0_{tt}")
            p.emit("vector", lambda e: e.tensor_reduce(
                lrow[:, 0:1], lg[:, 0, :], axis=AX.X, op=ALU.max, negate=True),
                reads=[b_lg], writes=[b_lrow], tag=f"mf{g}s1_{tt}")
            p.emit("scalar", lambda e: e.activation(
                lg[:, 1, :], lg[:, 0, :], AF.Exp, bias=lrow[:, 0:1]),
                reads=[b_lg], slots=[b_lrow], writes=[b_lg],
                tag=f"mf{g}s2_{tt}")
            p.emit("vector", lambda e: e.tensor_reduce(
                lrow[:, 1:2], lg[:, 1, :], axis=AX.X, op=ALU.add),
                reads=[b_lg], writes=[b_lrow], tag=f"mf{g}s3_{tt}")
            p.emit("vector", lambda e: e.reciprocal(
                lrow[:, 3:4], lrow[:, 1:2]), reads=[b_lrow], writes=[b_lrow],
                tag=f"mf{g}s4_{tt}")
            p.emit("vector", lambda e: e.tensor_scalar(
                lg[:, 2, :], lg[:, 1, :], lrow[:, 3:4], None, op0=ALU.mult),
                reads=[b_lg], slots=[b_lrow], writes=[b_lg],
                tag=f"mf{g}s5_{tt}")
            p.emit("vector", lambda e: e.tensor_reduce(
                lrow[:, 0:1], lg[:, 2, :], axis=AX.X, op=ALU.max),
                reads=[b_lg], writes=[b_lrow], tag=f"mf{g}t0_{tt}")
            p.emit("vector", lambda e: e.tensor_scalar(
                lg[:, 3, :], lg[:, 2, :], lrow[:, 0:1], None, op0=ALU.is_lt),
                reads=[b_lg], slots=[b_lrow], writes=[b_lg],
                tag=f"mf{g}t1_{tt}")
            p.emit("vector", lambda e: e.tensor_tensor(
                lg[:, 4, :], lg[:, 2, :], lg[:, 3, :], op=ALU.mult),
                writes=[b_lg], tag=f"mf{g}t2_{tt}")
            p.emit("vector", lambda e: e.tensor_reduce(
                lrow[:, 2:3], lg[:, 4, :], axis=AX.X, op=ALU.max),
                reads=[b_lg], writes=[b_lrow], tag=f"mf{g}t3_{tt}")
            p.emit("vector", lambda e: e.tensor_scalar(
                lg[:, 5, :], lg[:, 2, :], lrow[:, 2:3], None, op0=ALU.is_ge),
                reads=[b_lg], slots=[b_lrow], writes=[b_lg],
                tag=f"mf{g}t4_{tt}")
            p.emit("vector", lambda e: e.tensor_tensor(
                lg[:, 5, :], lg[:, 5, :], lg[:, 2, :], op=ALU.mult),
                writes=[b_lg], tag=f"mf{g}t5_{tt}")
            p.emit("vector", lambda e: e.tensor_copy(
                lg[:, 5, :].bitcast(U16)[:, 0:8].bitcast(BF16) if False else
                lgb[:, :], lg[:, 5, :]),
                reads=[b_lg], writes=[b_lgb], tag=f"mf{g}tb{tt}")
            ptc, b_ptc = p.bank(2 + tt % 2)
            p.emit("tensor", lambda e, pt_=ptc: e.transpose(
                pt_.bitcast(BF16)[0:8, 0:128], lgb[:, :], C.ident[:]),
                reads=[b_lgb, C.b_ident], writes=[b_ptc],
                tag=f"mf{g}tc{tt}")
            p.emit("vector", lambda e, g_=g, tt_=tt, pt_=ptc: e.tensor_copy(
                combT[:, g_ * MTG + tt_ * 128:g_ * MTG + (tt_ + 1) * 128],
                pt_.bitcast(BF16)[0:8, 0:128]),
                reads=[b_ptc], writes=[b_combT], tag=f"mf{g}te{tt}")
        for e9 in range(9):
            for tt in range(2):
                if e9 < 8:
                    pcb, b_pcb = p.bank(4 + tt % 2)
                    p.emit("tensor", lambda e, e_=e9, tt_=tt, g_=g, ps_=pcb:
                           e.matmul(ps_[:, 0:512], eye8[:, e_, :],
                                    combT[:, g_ * MTG + tt_ * 512:
                                          g_ * MTG + (tt_ + 1) * 512],
                                    start=True, stop=True),
                           reads=[b_combT, b_eye8], writes=[b_pcb],
                           tag=f"mf{g}cb{e9}_{tt}")
                    p.emit("vector", lambda e, tt_=tt, ps_=pcb: e.tensor_copy(
                        cb[:, tt_, :], ps_[:, 0:512]),
                        reads=[b_pcb], writes=[b_cb[tt]],
                        tag=f"mf{g}cbe{e9}_{tt}")
            for ic in range(NIC[e9]):
                row = e9 * 11 + ic if e9 < 8 else 88 + ic
                for w in range(2):
                    r = w
                    src = (io["scr_w12"][w, e9, ic] if e9 < 8 else
                           io["scr_sw12"][w, ic])
                    p.emit("sync", lambda e, r_=r, src_=src: e.dma_start(
                        out=wt[:, r_, :, :].rearrange("p c n -> p (c n)"),
                        in_=src_),
                        reads=[p.b_scrw], writes=[b_wt[r]], dma=True,
                        tag=f"mf{g}wl{e9}_{ic}_{w}")
                    for tt in range(2):
                        ph, b_ph = p.bank(w * 2 + tt)
                        for k in range(DC):
                            p.emit("tensor", lambda e, k_=k, tt_=tt, r_=r,
                                   ph_=ph: e.matmul(
                                       ph_[:, 0:512], wt[:, r_, k_, :],
                                       xln[:, k_, tt_ * 512:(tt_ + 1) * 512],
                                       start=(k_ == 0), stop=(k_ == DC - 1)),
                                   reads=[b_wt[r], b_xln[k]], writes=[b_ph],
                                   tag=f"mf{g}hm{e9}_{ic}_{w}_{tt}_{k}")
                for tt in range(2):
                    p1, b_p1 = p.bank(tt)
                    p2, b_p2 = p.bank(2 + tt)
                    p.emit("scalar", lambda e, ps_=p1: e.activation(
                        sv[:, :], ps_[:, 0:512], AF.Silu),
                        reads=[b_p1], writes=[b_sv], tag=f"mf{g}si{e9}_{ic}_{tt}")
                    p.emit("vector", lambda e, ps_=p2: e.tensor_tensor(
                        uv[:, 0, :], sv[:, :], ps_[:, 0:512], op=ALU.mult),
                        reads=[b_sv, b_p2], writes=[b_uv[0]],
                        tag=f"mf{g}u{e9}_{ic}_{tt}")
                    if e9 < 8:
                        p.emit("gpsimd", lambda e, tt_=tt: e.tensor_tensor(
                            hv[:, tt_, :], uv[:, 0, :], cb[:, tt_, :],
                            op=ALU.mult),
                            reads=[b_uv[0], b_cb[tt]], writes=[b_hv[tt]],
                            tag=f"mf{g}hc{e9}_{ic}_{tt}")
                    else:
                        p.emit("gpsimd", lambda e, tt_=tt: e.tensor_copy(
                            hv[:, tt_, :], uv[:, 0, :]),
                            reads=[b_uv[0]], writes=[b_hv[tt]],
                            tag=f"mf{g}hc{e9}_{ic}_{tt}")
                    p.emit("gpsimd", lambda e, row_=row, g_=g, tt_=tt:
                           e.dma_start(
                               out=io["scr_H"][row_, :,
                                               g_ * MTG + tt_ * 512:
                                               g_ * MTG + (tt_ + 1) * 512],
                               in_=hv[:, tt_, :]),
                           reads=[b_hv[tt]], dma=True,
                           tag=f"mf{g}hs{e9}_{ic}_{tt}")


def _l1f_y(p, C, ln, io, xacc):
    with p.scope():
        hrow = p.sbuf("mfy_h", [128, 2, 512], BF16)
        b_hrow = [p.buf(f"mfy_h{i}") for i in range(2)]
        w3 = p.sbuf("mfy_w3", [128, 2, 8, 128], BF16)
        b_w3 = [p.buf(f"mfy_w3{i}") for i in range(2)]
        ysb = p.sbuf("mfy_y", [128, DC, 512], F32)
        b_ysb = [p.buf(f"mfy_y{c}") for c in range(DC)]

        for tt in range(4):
            for oh in range(2):
                pys = [p.bank(i) for i in range(8)]
                for kc in range(110):
                    r = kc % 2
                    p.emit("sync", lambda e, kc_=kc, tt_=tt, r_=r: e.dma_start(
                        out=hrow[:, r_, :],
                        in_=io["scr_H"][kc_, :, tt_ * 512:(tt_ + 1) * 512]),
                        writes=[b_hrow[r]], dma=True, tag=f"mfy{tt}_{oh}l{kc}")
                    p.emit("sync", lambda e, kc_=kc, oh_=oh, r_=r: e.dma_start(
                        out=w3[:, r_, :, :].rearrange("p c n -> p (c n)"),
                        in_=io["scr_w3"][kc_, :, oh_ * 1024:(oh_ + 1) * 1024]),
                        reads=[p.b_scrw3], writes=[b_w3[r]], dma=True,
                        tag=f"mfy{tt}_{oh}w{kc}")
                    for oc in range(8):
                        p.emit("tensor", lambda e, kc_=kc, oc_=oc, r_=r,
                               ps_=pys[oc][0]: e.matmul(
                                   ps_[:, 0:512], w3[:, r_, oc_, :],
                                   hrow[:, r_, :],
                                   start=(kc_ == 0), stop=(kc_ == 109)),
                               reads=[b_w3[r], b_hrow[r]], writes=[pys[oc][1]],
                               tag=f"mfy{tt}_{oh}m{kc}_{oc}")
                for oc in range(8):
                    c = oh * 8 + oc
                    p.emit("vector", lambda e, c_=c, ps_=pys[oc][0]:
                           e.tensor_copy(ysb[:, c_, :], ps_[:, 0:512]),
                           reads=[pys[oc][1]], writes=[b_ysb[c]],
                           tag=f"mfy{tt}_{oh}e{oc}")
            ln.emit(f"mfy{tt}ln", 512,
                    src=lambda c: (ysb[:, c, :], b_ysb[c]),
                    dst=None,
                    resid=lambda c: (xacc(c)[0][:, tt * 512:(tt + 1) * 512],
                                     xacc(c)[1]))



def _t(a):
    return np.ascontiguousarray(np.transpose(a))


def _tile_w12(w1, w2):
    out = []
    for w in (w1, w2):
        wt = np.transpose(np.asarray(w), (0, 2, 1))
        out.append(wt.reshape(8, 16, 128, 11, 128).transpose(0, 3, 2, 1, 4))
    return np.ascontiguousarray(np.stack(out), dtype=np.float32)


def _tile_sw12(sw1, sw2):
    out = []
    for w in (sw1, sw2):
        wt = _t(np.asarray(w))
        out.append(wt.reshape(16, 128, 22, 128).transpose(2, 1, 0, 3))
    return np.ascontiguousarray(np.stack(out), dtype=np.float32)


def _tile_w3(w3, sw3):
    w3t = np.transpose(np.asarray(w3), (0, 2, 1)).reshape(11264, 2048)
    cat = np.concatenate([w3t, _t(np.asarray(sw3))], axis=0)
    return np.ascontiguousarray(cat.reshape(110, 128, 16, 128),
                                dtype=np.float32)


def _pad_qk(w_t):
    w = np.zeros((D, 1024), dtype=np.float32)
    for h in range(NHEAD):
        base = 128 * (h // 4) + 32 * (h % 4)
        w[:, base:base + 8] = w_t[:, 8 * h:8 * h + 8]
    return w


def shard_inputs(inputs):
    x = inputs["L0_attn_hidden_states"]
    w_shared = {
        "wqp": _pad_qk(_t(inputs["L0_attn_Wq"])),
        "wkp": _pad_qk(_t(inputs["L0_attn_Wk"])),
        "wv_t": _t(inputs["L0_attn_Wv"]),
        "wo_t": _t(inputs["L0_attn_Wo"]),
        "wg_t": _t(inputs["L0_attn_Wg"]),
        "vemb": np.stack([np.asarray(inputs["L0_attn_v_emb0"]),
                          np.asarray(inputs["L0_attn_v_emb1"])]),
        "wgu_t": _t(inputs["L0_ffn_w_gate_up"]),
        "wpr_t": _t(inputs["L0_ffn_w_proj"]),
        "wqkv": np.ascontiguousarray(inputs["L1_attn_w_qkv"]),
        "wout": np.ascontiguousarray(inputs["L1_attn_w_out"]),
        "decay": np.asarray(inputs["L1_attn_decay_param"]).reshape(1, 16),
        "gw_t": _t(inputs["L1_ffn_gate_w"]),
        "w12t": _tile_w12(inputs["L1_ffn_w1"], inputs["L1_ffn_w2"]),
        "sw12t": _tile_sw12(inputs["L1_ffn_sw1"], inputs["L1_ffn_sw2"]),
        "w3catt": _tile_w3(inputs["L1_ffn_w3"], inputs["L1_ffn_sw3"]),
    }
    w_shared = {k: np.ascontiguousarray(v, dtype=np.float32)
                for k, v in w_shared.items()}
    maps = []
    for c in range(NCORES):
        b, half = c // 2, c % 2
        t0 = half * T_OWN
        xs = np.asarray(x[b], dtype=np.float32)
        own = xs[t0:t0 + T_OWN]
        if half == 0:
            ctx = np.zeros((CTX, D), dtype=np.float32)
        else:
            ctx = xs[t0 - CTX:t0]
        ext = np.concatenate([ctx, own], axis=0)
        xin = np.ascontiguousarray(ext.T)
        sel = np.zeros((16, 2), dtype=np.float32)
        if half == 1:
            prev = c - 1
            sel[2 * prev, 0] = 1.0
            sel[2 * prev + 1, 1] = 1.0
        m = {"xin": xin, "sel": sel}
        m.update(w_shared)
        maps.append(m)
    return maps


def unshard_output(results):
    out = np.empty((4, 4096, D), dtype=np.float32)
    for c in range(NCORES):
        b, half = c // 2, c % 2
        xo = results[c]["xout"]
        out[b, half * T_OWN:(half + 1) * T_OWN] = xo.T
    return out



def build_block_l0_ffn():
    nc = bass.Bass("TRN2", target_bir_lowering=False)
    xin = nc.dram_tensor("xin", [D, T_OWN], F32, kind="ExternalInput").ap()
    wgu = nc.dram_tensor("wgu_t", [D, 16384], F32, kind="ExternalInput").ap()
    wpr = nc.dram_tensor("wpr_t", [8192, D], F32, kind="ExternalInput").ap()
    xout = nc.dram_tensor("xout", [D, T_OWN], F32, kind="ExternalOutput").ap()
    scr_gu = nc.dram_tensor("scr_gu", [DC, 128, 16384], BF16).ap()
    scr_pr = nc.dram_tensor("scr_pr", [64, 128, 2048], BF16).ap()
    scr_h = nc.dram_tensor("scr_h", [64, 128, T_OWN], BF16).ap()
    p = Prog(nc)
    C = Consts(p)
    ln = LNKit(p, C)

    xres = p.sbuf("xres", [128, DC, T_OWN], F32)
    b_xres = [p.buf(f"xres{c}") for c in range(DC)]
    for c in range(DC):
        p.emit("sync", lambda e, c_=c: e.dma_start(
            out=xres[:, c_, :], in_=xin[c_ * 128:(c_ + 1) * 128, :]),
            writes=[b_xres[c]], dma=True, tag=f"ldx{c}")

    io = {"wgu": wgu, "wpr": wpr, "scr_gu": scr_gu, "scr_pr": scr_pr,
          "scr_h": scr_h}
    emit_l0_ffn(p, C, ln, io, lambda c: (xres[:, c, :], b_xres[c]))

    for c in range(DC):
        p.out_dma("gpsimd", lambda e, c_=c: e.dma_start(
            out=xout[c_ * 128:(c_ + 1) * 128, :], in_=xres[:, c_, :]),
            reads=[b_xres[c]], tag=f"stx{c}")
    p.realize()
    return nc


def build_block_l0_attn(debug=False):
    nc = bass.Bass("TRN2", target_bir_lowering=False)
    xin = nc.dram_tensor("xin", [D, XT], F32, kind="ExternalInput").ap()
    wqp = nc.dram_tensor("wqp", [D, 1024], F32, kind="ExternalInput").ap()
    wkp = nc.dram_tensor("wkp", [D, 1024], F32, kind="ExternalInput").ap()
    wv_t = nc.dram_tensor("wv_t", [D, 256], F32, kind="ExternalInput").ap()
    wo_t = nc.dram_tensor("wo_t", [256, D], F32, kind="ExternalInput").ap()
    wg_t = nc.dram_tensor("wg_t", [D, D], F32, kind="ExternalInput").ap()
    vemb = nc.dram_tensor("vemb", [2, 256], F32, kind="ExternalInput").ap()
    xout = nc.dram_tensor("xout", [D, T_OWN], F32, kind="ExternalOutput").ap()
    p = Prog(nc)
    C = Consts(p)
    ln = LNKit(p, C)

    xres = p.sbuf("xres", [128, DC, T_OWN], F32)
    b_xres = [p.buf(f"xres{c}") for c in range(DC)]
    xctx = p.sbuf("xctx", [128, DC, CTX], BF16)
    b_xctx = [p.buf(f"xctx{c}") for c in range(DC)]
    for c in range(DC):
        p.emit("gpsimd", lambda e, c_=c: e.dma_start(
            out=xctx[:, c_, :], in_=xin[c_ * 128:(c_ + 1) * 128, 0:CTX]),
            writes=[b_xctx[c]], dma=True, tag=f"ldctx{c}")
        p.emit("sync", lambda e, c_=c: e.dma_start(
            out=xres[:, c_, :], in_=xin[c_ * 128:(c_ + 1) * 128, CTX:XT]),
            writes=[b_xres[c]], dma=True, tag=f"ldx{c}")

    io = {"wqp": wqp, "wkp": wkp, "wv_t": wv_t, "wo_t": wo_t, "wg_t": wg_t,
          "vemb": vemb}
    emit_l0_attn(p, C, ln, io,
                 lambda c: (xres[:, c, :], b_xres[c]),
                 lambda c: (xctx[:, c, :], b_xctx[c]))

    if debug:
        dbg = nc.dram_tensor("dbg", [256, T_OWN], F32, kind="ExternalOutput").ap()
        for c in range(2):
            for t in range(4):
                p.emit("vector", lambda e, c_=c, t_=t: e.tensor_copy(
                    ln.tmp[:, 0, :], K_AFM[0][:, c_, t_ * 512:(t_ + 1) * 512]),
                    reads=[K_AFM[1][c]], writes=[ln.b_tmp[0]],
                    tag=f"dbgu{c}_{t}")
                p.out_dma("gpsimd", lambda e, c_=c, t_=t: e.dma_start(
                    out=dbg[c_ * 128:(c_ + 1) * 128, t_ * 512:(t_ + 1) * 512],
                    in_=ln.tmp[:, 0, :]),
                    reads=[ln.b_tmp[0]], tag=f"dbgs{c}_{t}")
    for c in range(DC):
        p.out_dma("gpsimd", lambda e, c_=c: e.dma_start(
            out=xout[c_ * 128:(c_ + 1) * 128, :], in_=xres[:, c_, :]),
            reads=[b_xres[c]], tag=f"stx{c}")
    p.realize()
    return nc


def build_block_l1_attn():
    nc = bass.Bass("TRN2", target_bir_lowering=False)
    xin = nc.dram_tensor("xin", [D, T_OWN], F32, kind="ExternalInput").ap()
    wqkv = nc.dram_tensor("wqkv", [D, 6144], F32, kind="ExternalInput").ap()
    wout = nc.dram_tensor("wout", [D, D], F32, kind="ExternalInput").ap()
    decay = nc.dram_tensor("decay", [1, 16], F32, kind="ExternalInput").ap()
    sel = nc.dram_tensor("sel", [16, 2], F32, kind="ExternalInput").ap()
    xout = nc.dram_tensor("xout", [D, T_OWN], F32, kind="ExternalOutput").ap()
    cc_in = nc.dram_tensor("cc_in", [2, 2048], F32).ap()
    cc_out = nc.dram_tensor("cc_out", [16, 2048], F32).ap()
    scr_q = nc.dram_tensor("scr_q", [16, 128, 2048], BF16).ap()
    scr_k = nc.dram_tensor("scr_k", [16, 128, 2048], BF16).ap()
    scr_kv = nc.dram_tensor("scr_kv", [16, 128, 2048], BF16).ap()
    scr_at = nc.dram_tensor("scr_at", [16, 128, 2048], BF16).ap()
    p = Prog(nc)
    C = Consts(p)
    ln = LNKit(p, C)

    xres = p.sbuf("xres", [128, DC, T_OWN], F32)
    b_xres = [p.buf(f"xres{c}") for c in range(DC)]
    for c in range(DC):
        p.emit("sync", lambda e, c_=c: e.dma_start(
            out=xres[:, c_, :], in_=xin[c_ * 128:(c_ + 1) * 128, :]),
            writes=[b_xres[c]], dma=True, tag=f"ldx{c}")

    io = {"wqkv": wqkv, "wout": wout, "decay": decay, "sel": sel,
          "cc_in": cc_in, "cc_out": cc_out, "scr_q": scr_q, "scr_k": scr_k,
          "scr_kv": scr_kv, "scr_at": scr_at}
    emit_l1_attn(p, C, ln, io, lambda c: (xres[:, c, :], b_xres[c]))

    for c in range(DC):
        p.out_dma("gpsimd", lambda e, c_=c: e.dma_start(
            out=xout[c_ * 128:(c_ + 1) * 128, :], in_=xres[:, c_, :]),
            reads=[b_xres[c]], tag=f"stx{c}")
    p.realize()
    return nc


def build_block_l1_ffn():
    nc = bass.Bass("TRN2", target_bir_lowering=False)
    xin = nc.dram_tensor("xin", [D, T_OWN], F32, kind="ExternalInput").ap()
    gw_t = nc.dram_tensor("gw_t", [D, 8], F32, kind="ExternalInput").ap()
    w12t = nc.dram_tensor("w12t", [2, 8, 11, 128, 16, 128], F32,
                          kind="ExternalInput").ap()
    sw12t = nc.dram_tensor("sw12t", [2, 22, 128, 16, 128], F32,
                           kind="ExternalInput").ap()
    w3catt = nc.dram_tensor("w3catt", [110, 128, 16, 128], F32,
                            kind="ExternalInput").ap()
    xout = nc.dram_tensor("xout", [D, T_OWN], F32, kind="ExternalOutput").ap()
    scr_w12 = nc.dram_tensor("scr_w12", [2, 8, 11, 128, 2048], BF16).ap()
    scr_sw12 = nc.dram_tensor("scr_sw12", [2, 22, 128, 2048], BF16).ap()
    scr_w3 = nc.dram_tensor("scr_w3", [110, 128, 2048], BF16).ap()
    scr_H = nc.dram_tensor("scr_H", [110, 128, 2048], BF16).ap()
    p = Prog(nc)
    C = Consts(p)
    ln = LNKit(p, C)

    xres = p.sbuf("xres", [128, DC, T_OWN], F32)
    b_xres = [p.buf(f"xres{c}") for c in range(DC)]
    for c in range(DC):
        p.emit("sync", lambda e, c_=c: e.dma_start(
            out=xres[:, c_, :], in_=xin[c_ * 128:(c_ + 1) * 128, :]),
            writes=[b_xres[c]], dma=True, tag=f"ldx{c}")

    io = {"gw_t": gw_t, "w12t": w12t, "sw12t": sw12t, "w3catt": w3catt,
          "scr_w12": scr_w12, "scr_sw12": scr_sw12, "scr_w3": scr_w3,
          "scr_H": scr_H}
    emit_l1_ffn(p, C, ln, io, lambda c: (xres[:, c, :], b_xres[c]))

    for c in range(DC):
        p.out_dma("gpsimd", lambda e, c_=c: e.dma_start(
            out=xout[c_ * 128:(c_ + 1) * 128, :], in_=xres[:, c_, :]),
            reads=[b_xres[c]], tag=f"stx{c}")
    p.realize()
    return nc



def _xres_io(nc, p, with_ctx, ctx_alloc=True):
    xin = nc.dram_tensor("xin", [D, XT if with_ctx else T_OWN], F32,
                         kind="ExternalInput").ap()
    xout = nc.dram_tensor("xout", [D, T_OWN], F32, kind="ExternalOutput").ap()
    xres = p.sbuf("xres", [128, DC, T_OWN], F32)
    b_xres = [p.buf(f"xres{c}") for c in range(DC)]
    off = CTX if with_ctx else 0
    for c in range(DC):
        p.emit("sync", lambda e, c_=c: e.dma_start(
            out=xres[:, c_, :], in_=xin[c_ * 128:(c_ + 1) * 128, off:off + T_OWN]),
            writes=[b_xres[c]], dma=True, tag=f"ldx{c}")
    xctx = xctx_b = None
    if with_ctx and ctx_alloc:
        xctx, xctx_b = _ctx_load(nc, p, xin)

    def store(p):
        for c in range(DC):
            p.out_dma("gpsimd", lambda e, c_=c: e.dma_start(
                out=xout[c_ * 128:(c_ + 1) * 128, :], in_=xres[:, c_, :]),
                reads=[b_xres[c]], tag=f"stx{c}")
    return (xres, b_xres, xctx, xctx_b, store, xin)


def _ctx_load(nc, p, xin):
    xctx = p.sbuf("xctx", [128, DC, CTX], BF16)
    xctx_b = [p.buf(f"xctx{c}") for c in range(DC)]
    for c in range(DC):
        p.emit("gpsimd", lambda e, c_=c: e.dma_start(
            out=xctx[:, c_, :], in_=xin[c_ * 128:(c_ + 1) * 128, 0:CTX]),
            writes=[xctx_b[c]], dma=True, tag=f"ldctx{c}")
    return xctx, xctx_b


def build_launch_l0_attn():
    nc = bass.Bass("TRN2", target_bir_lowering=False)
    p = Prog(nc)
    C = Consts(p)
    ln = LNKit(p, C)
    xres, b_xres, xctx, b_xctx, store, _ = _xres_io(nc, p, True)
    io = {k: nc.dram_tensor(k, shp, F32, kind="ExternalInput").ap()
          for k, shp in [("wqp", [D, 1024]), ("wkp", [D, 1024]),
                         ("wv_t", [D, 256]), ("wo_t", [256, D]),
                         ("wg_t", [D, D]), ("vemb", [2, 256])]}
    emit_l0_attn(p, C, ln, io,
                 lambda c: (xres[:, c, :], b_xres[c]),
                 lambda c: (xctx[:, c, :], b_xctx[c]))
    store(p)
    p.realize()
    return nc


def build_launch_l0_ffn():
    nc = bass.Bass("TRN2", target_bir_lowering=False)
    p = Prog(nc)
    C = Consts(p)
    ln = LNKit(p, C)
    xres, b_xres, _, _, store, _ = _xres_io(nc, p, False)
    io = {"wgu": nc.dram_tensor("wgu_t", [D, 16384], F32,
                                kind="ExternalInput").ap(),
          "wpr": nc.dram_tensor("wpr_t", [8192, D], F32,
                                kind="ExternalInput").ap(),
          "scr_gu": nc.dram_tensor("scr_gu", [DC, 128, 16384], BF16).ap(),
          "scr_pr": nc.dram_tensor("scr_pr", [64, 128, 2048], BF16).ap(),
          "scr_h": nc.dram_tensor("scr_h", [64, 128, T_OWN], BF16).ap()}
    emit_l0_ffn(p, C, ln, io, lambda c: (xres[:, c, :], b_xres[c]))
    store(p)
    p.realize()
    return nc


def build_launch_l1_attn():
    nc = bass.Bass("TRN2", target_bir_lowering=False)
    p = Prog(nc)
    C = Consts(p)
    ln = LNKit(p, C)
    xres, b_xres, _, _, store, _ = _xres_io(nc, p, False)
    io = {"wqkv": nc.dram_tensor("wqkv", [D, 6144], F32,
                                 kind="ExternalInput").ap(),
          "wout": nc.dram_tensor("wout", [D, D], F32,
                                 kind="ExternalInput").ap(),
          "decay": nc.dram_tensor("decay", [1, 16], F32,
                                  kind="ExternalInput").ap(),
          "sel": nc.dram_tensor("sel", [16, 2], F32,
                                kind="ExternalInput").ap(),
          "cc_in": nc.dram_tensor("cc_in", [2, 2048], F32).ap(),
          "cc_out": nc.dram_tensor("cc_out", [16, 2048], F32).ap(),
          "scr_q": nc.dram_tensor("scr_q", [16, 128, 2048], BF16).ap(),
          "scr_k": nc.dram_tensor("scr_k", [16, 128, 2048], BF16).ap(),
          "scr_kv": nc.dram_tensor("scr_kv", [16, 128, 2048], BF16).ap(),
          "scr_at": nc.dram_tensor("scr_at", [16, 128, 2048], BF16).ap()}
    emit_l1_attn(p, C, ln, io, lambda c: (xres[:, c, :], b_xres[c]))
    store(p)
    p.realize()
    return nc


def build_launch_l1_ffn():
    nc = bass.Bass("TRN2", target_bir_lowering=False)
    p = Prog(nc)
    C = Consts(p)
    ln = LNKit(p, C)
    xres, b_xres, _, _, store, _ = _xres_io(nc, p, False)
    io = {"gw_t": nc.dram_tensor("gw_t", [D, 8], F32,
                                 kind="ExternalInput").ap(),
          "w12t": nc.dram_tensor("w12t", [2, 8, 11, 128, 16, 128], F32,
                                 kind="ExternalInput").ap(),
          "sw12t": nc.dram_tensor("sw12t", [2, 22, 128, 16, 128], F32,
                                  kind="ExternalInput").ap(),
          "w3catt": nc.dram_tensor("w3catt", [110, 128, 16, 128], F32,
                                   kind="ExternalInput").ap(),
          "scr_w12": nc.dram_tensor("scr_w12", [2, 8, 11, 128, 2048],
                                    BF16).ap(),
          "scr_sw12": nc.dram_tensor("scr_sw12", [2, 22, 128, 2048],
                                     BF16).ap(),
          "scr_w3": nc.dram_tensor("scr_w3", [110, 128, 2048], BF16).ap(),
          "scr_H": nc.dram_tensor("scr_H", [110, 128, 2048], BF16).ap()}
    emit_l1_ffn(p, C, ln, io, lambda c: (xres[:, c, :], b_xres[c]))
    store(p)
    p.realize()
    return nc


def build_fused():
    nc = bass.Bass("TRN2", target_bir_lowering=False)
    p = Prog(nc)
    C = Consts(p)
    ln = LNKit(p, C)
    xres, b_xres, _, _, store, xin = _xres_io(nc, p, True, ctx_alloc=False)

    def fin(k, shp):
        return nc.dram_tensor(k, shp, F32, kind="ExternalInput").ap()

    ioA = {k: fin(k, shp)
           for k, shp in [("wqp", [D, 1024]), ("wkp", [D, 1024]),
                          ("wv_t", [D, 256]), ("wo_t", [256, D]),
                          ("wg_t", [D, D]), ("vemb", [2, 256])]}
    ioF = {"wgu": fin("wgu_t", [D, 16384]), "wpr": fin("wpr_t", [8192, D]),
           "scr_gu": nc.dram_tensor("scr_gu", [DC, 128, 16384], BF16).ap(),
           "scr_pr": nc.dram_tensor("scr_pr", [64, 128, 2048], BF16).ap(),
           "scr_h": nc.dram_tensor("scr_h", [64, 128, T_OWN], BF16).ap()}
    ioL = {"wqkv": fin("wqkv", [D, 6144]), "wout": fin("wout", [D, D]),
           "decay": fin("decay", [1, 16]), "sel": fin("sel", [16, 2]),
           "cc_in": nc.dram_tensor("cc_in", [2, 2048], F32).ap(),
           "cc_out": nc.dram_tensor("cc_out", [16, 2048], F32).ap(),
           "scr_q": nc.dram_tensor("scr_q", [16, 128, 2048], BF16).ap(),
           "scr_k": nc.dram_tensor("scr_k", [16, 128, 2048], BF16).ap(),
           "scr_kv": nc.dram_tensor("scr_kv", [16, 128, 2048], BF16).ap(),
           "scr_at": nc.dram_tensor("scr_at", [16, 128, 2048], BF16).ap()}
    ioM = {"gw_t": fin("gw_t", [D, 8]),
           "w12t": fin("w12t", [2, 8, 11, 128, 16, 128]),
           "sw12t": fin("sw12t", [2, 22, 128, 16, 128]),
           "w3catt": fin("w3catt", [110, 128, 16, 128]),
           "scr_w12": nc.dram_tensor("scr_w12", [2, 8, 11, 128, 2048],
                                     BF16).ap(),
           "scr_sw12": nc.dram_tensor("scr_sw12", [2, 22, 128, 2048],
                                      BF16).ap(),
           "scr_w3": nc.dram_tensor("scr_w3", [110, 128, 2048], BF16).ap(),
           "scr_H": nc.dram_tensor("scr_H", [110, 128, 2048], BF16).ap()}

    xacc = lambda c: (xres[:, c, :], b_xres[c])
    with p.scope():
        xctx, b_xctx = _ctx_load(nc, p, xin)
        emit_l0_attn(p, C, ln, ioA, xacc,
                     lambda c: (xctx[:, c, :], b_xctx[c]))
    with p.scope():
        emit_l0_ffn(p, C, ln, ioF, xacc)
    with p.scope():
        emit_l1_attn(p, C, ln, ioL, xacc)
    with p.scope():
        emit_l1_ffn(p, C, ln, ioM, xacc)
    store(p)
    p.realize()
    return nc


_FUSED = True

_L0A_KEYS = ["wqp", "wkp", "wv_t", "wo_t", "wg_t", "vemb"]
_L0F_KEYS = ["wgu_t", "wpr_t"]
_L1A_KEYS = ["wqkv", "wout", "decay", "sel"]
_L1F_KEYS = ["gw_t", "w12t", "sw12t", "w3catt"]


def kernel(**inputs):
    maps = shard_inputs(inputs)
    if _FUSED:
        nc = build_fused()
        res = run_bass_kernel_spmd(nc, maps, core_ids=list(range(NCORES)))
        return unshard_output(res.results)
    nc = build_launch_l0_attn()
    ms = [{"xin": m["xin"], **{k: m[k] for k in _L0A_KEYS}} for m in maps]
    res = run_bass_kernel_spmd(nc, ms, core_ids=list(range(NCORES)))
    xcur = [r["xout"] for r in res.results]
    nc = build_launch_l0_ffn()
    ms = [{"xin": xcur[c], **{k: maps[c][k] for k in _L0F_KEYS}}
          for c in range(NCORES)]
    res = run_bass_kernel_spmd(nc, ms, core_ids=list(range(NCORES)))
    xcur = [r["xout"] for r in res.results]
    nc = build_launch_l1_attn()
    ms = [{"xin": xcur[c], **{k: maps[c][k] for k in _L1A_KEYS}}
          for c in range(NCORES)]
    res = run_bass_kernel_spmd(nc, ms, core_ids=list(range(NCORES)))
    xcur = [r["xout"] for r in res.results]
    nc = build_launch_l1_ffn()
    ms = [{"xin": xcur[c], **{k: maps[c][k] for k in _L1F_KEYS}}
          for c in range(NCORES)]
    res = run_bass_kernel_spmd(nc, ms, core_ids=list(range(NCORES)))
    return unshard_output(res.results)


# version 06
# speedup vs baseline: 1.0230x; 1.0205x over previous
import numpy as np
from contextlib import ExitStack, contextmanager

import concourse.bass as bass
import concourse.mybir as mybir
from concourse.bass_utils import run_bass_kernel_spmd

F32 = mybir.dt.float32
BF16 = mybir.dt.bfloat16
AF = mybir.ActivationFunctionType
ALU = mybir.AluOpType
AX = mybir.AxisListType
U16 = mybir.dt.uint16
I32 = mybir.dt.int32

ENGINES = ("tensor", "vector", "scalar", "gpsimd", "sync")

D = 2048
DC = D // 128
T_OWN = 2048
CTX = 128
XT = T_OWN + CTX
NCORES = 8
TG = 512
NG = T_OWN // TG

LN_EPS = 1e-5
RMS_EPS = float(np.finfo(np.float32).eps)
LOG_DECAY = float(np.log(0.45))
NEG_BIG = -1e30
TERN_INV_C = float(1.0 / (0.5 + 2.0 ** -9))



class _Op:
    __slots__ = ("id", "engine", "fn", "is_dma", "deps", "needs_inc",
                 "done_sem", "done_val", "tag", "pre_wait")

    def __init__(self, oid, engine, fn, is_dma, deps, tag):
        self.id = oid
        self.engine = engine
        self.fn = fn
        self.is_dma = is_dma
        self.deps = deps
        self.needs_inc = False
        self.done_sem = None
        self.done_val = None
        self.tag = tag
        self.pre_wait = None


class Buf:
    __slots__ = ("name", "last_w", "readers")

    def __init__(self, name):
        self.name = name
        self.last_w = None
        self.readers = []


class Prog:
    def __init__(self, nc, strict=("gpsimd", "vector")):
        self.nc = nc
        self.strict = set(strict)
        self.stack = ExitStack()
        self.scopes = [self.stack]
        self.ops = {e: [] for e in ENGINES}
        self._oid = 0
        self.sbuf_bytes = 0
        self.out_dmas = []
        self._names = set()
        self.psbanks = [self.stack.enter_context(
            nc.psum_tensor(f"psb{i}", [128, 512], F32)) for i in range(8)]
        self.bankbuf = [Buf(f"bank{i}") for i in range(8)]

    def bank(self, i):
        return self.psbanks[i][:, :], self.bankbuf[i]

    def sbuf(self, name, shape, dtype):
        assert name not in self._names, name
        self._names.add(name)
        t = self.scopes[-1].enter_context(
            self.nc.sbuf_tensor(name, list(shape), dtype))
        esz = 2 if dtype == BF16 else 4
        self.sbuf_bytes += int(np.prod(shape[1:])) * esz
        assert self.sbuf_bytes < 207870, \
            f"SBUF budget blown at {name}: {self.sbuf_bytes}"
        return t

    def buf(self, name):
        return Buf(name)

    @contextmanager
    def scope(self):
        st = ExitStack()
        self.scopes.append(st)
        saved = self.sbuf_bytes
        try:
            yield
        finally:
            lasts = []
            for e in ENGINES:
                if self.ops[e]:
                    lasts.append(self.ops[e][-1])
                    lasts.extend([o for o in self.ops[e] if o.is_dma][-8:])
            joiner = self.emit("sync", lambda eng: eng.nop(), deps=lasts,
                               tag="scope_join")
            for e in ENGINES:
                if e != "sync":
                    self.emit(e, lambda eng: eng.nop(), deps=[joiner],
                              tag=f"scope_rel_{e}")
            self.scopes.pop()
            st.close()
            self.sbuf_bytes = saved

    def emit(self, engine, fn, reads=(), writes=(), slots=(), deps=(), dma=False,
             tag=""):
        d = []
        fstrict = engine in self.strict
        for b in reads:
            if b.last_w is not None:
                d.append((b.last_w, fstrict))
        for b in slots:
            if b.last_w is not None:
                d.append((b.last_w, True))
        for b in writes:
            if b.last_w is not None:
                d.append((b.last_w, fstrict))
            for r in b.readers:
                d.append((r, fstrict))
        for x in deps:
            if x is None:
                continue
            d.append(x if isinstance(x, tuple) else (x, fstrict))
        op = _Op(self._oid, engine, fn, dma, d, tag)
        self._oid += 1
        self.ops[engine].append(op)
        for b in reads:
            b.readers.append(op)
        for b in slots:
            b.readers.append(op)
        for b in writes:
            b.last_w = op
            b.readers = []
        return op

    def out_dma(self, engine, fn, reads=(), deps=(), tag="out"):
        op = self.emit(engine, fn, reads=reads, deps=deps, dma=True, tag=tag)
        self.out_dmas.append(op)
        return op

    def _verify_schedule(self):
        idx = {e: 0 for e in ENGINES}
        semval = {}
        progressed = True
        total = sum(len(v) for v in self.ops.values())
        done = 0
        while progressed:
            progressed = False
            for e in ENGINES:
                while idx[e] < len(self.ops[e]):
                    op = self.ops[e][idx[e]]
                    ok = True
                    if op.pre_wait is not None and \
                            semval.get(id(op.pre_wait[0]), 0) < op.pre_wait[1]:
                        ok = False
                    if ok:
                        for (y, force) in op.deps:
                            if y.done_sem is None:
                                continue
                            if (not force) and (not y.is_dma) and (not op.is_dma) \
                                    and y.engine == e:
                                continue
                            if semval.get(id(y.done_sem), 0) < y.done_val:
                                ok = False
                                break
                    if not ok:
                        break
                    if op.needs_inc:
                        k = id(op.done_sem)
                        semval[k] = semval.get(k, 0) + (16 if op.is_dma else 1)
                    idx[e] += 1
                    done += 1
                    progressed = True
        if done != total:
            stuck = []
            for e in ENGINES:
                if idx[e] < len(self.ops[e]):
                    op = self.ops[e][idx[e]]
                    why = [(y.engine, y.tag) for (y, f) in op.deps
                           if y.done_sem is not None
                           and semval.get(id(y.done_sem), 0) < y.done_val]
                    stuck.append((e, op.tag, why[:4]))
            raise RuntimeError(f"schedule deadlock: {done}/{total} ran; "
                               f"stuck: {stuck}")

    def realize(self):
        nc = self.nc
        tail_deps = list(self.out_dmas)
        for e in ENGINES:
            if self.ops[e]:
                tail_deps.append(self.ops[e][-1])
                tail_deps.extend([o for o in self.ops[e] if o.is_dma][-8:])
        self.emit("sync", lambda eng: eng.nop(), deps=tail_deps, tag="final")

        NS = 8
        csem, dsem = {}, {}
        for e in ENGINES:
            if any(not o.is_dma for o in self.ops[e]):
                csem[e] = self.stack.enter_context(nc.semaphore(f"c_{e}"))
            if any(o.is_dma for o in self.ops[e]):
                dsem[e] = [self.stack.enter_context(nc.semaphore(f"d_{e}{j}"))
                           for j in range(NS)]

        for e in ENGINES:
            for op in self.ops[e]:
                for (y, force) in op.deps:
                    assert y.id < op.id, (op.tag, y.tag)
                    if force or (y.engine != op.engine) or y.is_dma or op.is_dma:
                        y.needs_inc = True
        for e in ENGINES:
            c, dc = 0, 0
            uses = [0] * NS
            for op in self.ops[e]:
                if op.is_dma:
                    slot = dc % NS
                    dc += 1
                    uses[slot] += 1
                    op.done_sem, op.done_val = dsem[e][slot], uses[slot] * 16
                    op.needs_inc = True
                    if uses[slot] > 1:
                        op.pre_wait = (dsem[e][slot], (uses[slot] - 1) * 16)
                elif op.needs_inc:
                    c += 1
                    op.done_sem, op.done_val = csem[e], c

        self._verify_schedule()
        blk = self.stack.enter_context(nc.Block())

        def make_body(e):
            ops = self.ops[e]

            def body(eng):
                hiwater = {}
                for op in ops:
                    need = {}
                    for (y, force) in op.deps:
                        if y.done_sem is None:
                            continue
                        if (not force) and (not y.is_dma) and (not op.is_dma) \
                                and y.engine == e:
                            continue
                        k = y.done_sem
                        if need.get(k, -1) < y.done_val:
                            need[k] = y.done_val
                    if op.pre_wait is not None:
                        sw, vw = op.pre_wait
                        if need.get(sw, -1) < vw:
                            need[sw] = vw
                    for s, v in need.items():
                        if hiwater.get(s, 0) >= v:
                            continue
                        eng.wait_ge(s, v)
                        hiwater[s] = v
                    instr = op.fn(eng)
                    if op.needs_inc:
                        assert instr is not None, f"op {op.tag} returned no instr"
                        instr.then_inc(op.done_sem, 16 if op.is_dma else 1)
            return body

        for e in ENGINES:
            if not self.ops[e]:
                continue
            getattr(blk, e)(make_body(e))
        self.stack.close()



class Consts:
    def __init__(self, p: Prog):
        self.ident = p.sbuf("c_ident", [128, 128], BF16)
        self.ones_col_b = p.sbuf("c_ones_col_b", [128, 1], BF16)
        self.ones_row_f = p.sbuf("c_ones_row_f", [1, 128], F32)
        self.ones_row_b = p.sbuf("c_ones_row_b", [1, 128], BF16)
        self.b_ident = p.buf("ident")
        self.b_ones_col_b = p.buf("ones_col_b")
        self.b_ones_row_f = p.buf("ones_row_f")
        self.b_ones_row_b = p.buf("ones_row_b")

        p.emit("gpsimd", lambda g: g.memset(self.ident[:], 0.0),
               writes=[self.b_ident], tag="ident0")
        p.emit("gpsimd", lambda g: g.affine_select(
            out=self.ident[:], in_=self.ident[:], compare_op=ALU.not_equal,
            fill=1.0, base=0, pattern=[[-1, 128]], channel_multiplier=1),
            writes=[self.b_ident], tag="ident1")
        p.emit("gpsimd", lambda g: g.memset(self.ones_col_b[:], 1.0),
               writes=[self.b_ones_col_b], tag="ones_col")
        p.emit("gpsimd", lambda g: g.memset(self.ones_row_f[:], 1.0),
               writes=[self.b_ones_row_f], tag="ones_row")
        p.emit("gpsimd", lambda g: g.memset(self.ones_row_b[:], 1.0),
               writes=[self.b_ones_row_b], tag="ones_row_b")



class LNKit:
    MAXTT = 512

    def __init__(self, p: Prog, C: Consts, banks=(4, 5, 6, 7)):
        self.p, self.C = p, C
        TT = self.MAXTT
        self.xb = p.sbuf("ln_xb", [128, 2, 2, TT], BF16)
        self.rows = p.sbuf("ln_rows", [1, 3, TT], F32)
        self.mbs = p.sbuf("ln_mbs", [128, TT], F32)
        self.rbs = p.sbuf("ln_rbs", [128, TT], F32)
        self.tmp = p.sbuf("ln_tmp", [128, 2, TT], F32)
        self.bk = banks
        self.b_xb = [p.buf("ln_xb0"), p.buf("ln_xb1")]
        self.b_rows = p.buf("ln_rows")
        self.b_mbs, self.b_rbs = p.buf("ln_mbs"), p.buf("ln_rbs")
        self.b_tmp = [p.buf("ln_tmp0"), p.buf("ln_tmp1")]

    def emit(self, tag, TT, src, dst, resid=None, nchunks=DC, eps=LN_EPS):
        p, C = self.p, self.C
        st0, b_st0 = p.bank(self.bk[0])
        st1, b_st1 = p.bank(self.bk[1])
        psmb, b_psmb = p.bank(self.bk[2])
        psrb, b_psrb = p.bank(self.bk[3])
        sts = [(st0, b_st0), (st1, b_st1)]
        for c in range(nchunks):
            s_ap, s_b = src(c)
            r = c % 2
            xb_ap = self.xb[:, r, 0, 0:TT]
            sq_ap = self.xb[:, r, 1, 0:TT]
            p.emit("vector", lambda e, o=xb_ap, i=s_ap: e.tensor_copy(o, i),
                   reads=[s_b], writes=[self.b_xb[r]], tag=f"{tag}c{c}cast")
            p.emit("vector", lambda e, o=sq_ap, i=xb_ap: e.tensor_tensor(
                o, i, i, op=ALU.mult), writes=[self.b_xb[r]], tag=f"{tag}c{c}sq")
            for h in range(2):
                p.emit("tensor", lambda e, r_=r, c_=c, h_=h: e.matmul(
                    sts[h_][0][0:1, 0:TT], C.ones_col_b[:],
                    self.xb[:, r_, h_, 0:TT],
                    start=(c_ == 0), stop=(c_ == nchunks - 1)),
                    reads=[self.b_xb[r], C.b_ones_col_b], writes=[sts[h][1]],
                    tag=f"{tag}c{c}mm{h}")
        b_rows = self.b_rows
        m, ms, t = (self.rows[0:1, i, 0:TT] for i in range(3))
        var = ms
        rr = ms
        nfeat = float(nchunks * 128)
        p.emit("vector", lambda e: e.tensor_scalar(m, st0[0:1, 0:TT],
               1.0 / nfeat, None, op0=ALU.mult), reads=[b_st0], writes=[b_rows],
               tag=f"{tag}m")
        p.emit("vector", lambda e: e.tensor_scalar(ms, st1[0:1, 0:TT],
               1.0 / nfeat, None, op0=ALU.mult), reads=[b_st1], writes=[b_rows],
               tag=f"{tag}ms")
        p.emit("vector", lambda e: e.tensor_tensor(t, m, m, op=ALU.mult),
               writes=[b_rows], tag=f"{tag}mm")
        p.emit("vector", lambda e: e.tensor_tensor(var, ms, t, op=ALU.subtract),
               writes=[b_rows], tag=f"{tag}var")
        p.emit("scalar", lambda e: e.activation(t, var, AF.Ln,
               bias=np.float32(eps)), reads=[b_rows], writes=[b_rows],
               tag=f"{tag}ln")
        p.emit("scalar", lambda e: e.activation(rr, t, AF.Exp,
               scale=np.float32(-0.5)), writes=[b_rows], tag=f"{tag}exp")
        p.emit("tensor", lambda e: e.matmul(psmb[:, 0:TT], C.ones_row_f[:],
               m, start=True, stop=True), reads=[b_rows, C.b_ones_row_f],
               writes=[b_psmb], tag=f"{tag}bm")
        p.emit("tensor", lambda e: e.matmul(psrb[:, 0:TT], C.ones_row_f[:],
               rr, start=True, stop=True), reads=[b_rows, C.b_ones_row_f],
               writes=[b_psrb], tag=f"{tag}br")
        p.emit("vector", lambda e: e.tensor_copy(self.mbs[:, 0:TT],
               psmb[:, 0:TT]), reads=[b_psmb], writes=[self.b_mbs],
               tag=f"{tag}mbs")
        p.emit("vector", lambda e: e.tensor_copy(self.rbs[:, 0:TT],
               psrb[:, 0:TT]), reads=[b_psrb], writes=[self.b_rbs],
               tag=f"{tag}rbs")
        for c in range(nchunks):
            s_ap, s_b = src(c)
            on_pool = (c % 2 == 1)
            eng = "gpsimd" if on_pool else "vector"
            mb = (self.mbs[:, 0:TT], self.b_mbs) if on_pool else \
                 (psmb[:, 0:TT], b_psmb)
            rb = (self.rbs[:, 0:TT], self.b_rbs) if on_pool else \
                 (psrb[:, 0:TT], b_psrb)
            r = c % 2
            tm = self.tmp[:, r, 0:TT]
            p.emit(eng, lambda e, o=tm, i=s_ap, b=mb[0]: e.tensor_tensor(
                o, i, b, op=ALU.subtract), reads=[s_b, mb[1]],
                writes=[self.b_tmp[r]], tag=f"{tag}c{c}sub")
            if dst is not None:
                d_ap, d_b = dst(c)
                p.emit(eng, lambda e, o=d_ap, i=tm, b=rb[0]: e.tensor_tensor(
                    o, i, b, op=ALU.mult), reads=[self.b_tmp[r], rb[1]],
                    writes=[d_b], tag=f"{tag}c{c}mul")
            if resid is not None:
                assert dst is None
                x_ap, x_b = resid(c)
                p.emit(eng, lambda e, o=tm, b=rb[0]: e.tensor_tensor(
                    o, o, b, op=ALU.mult), reads=[rb[1]],
                    writes=[self.b_tmp[r]], tag=f"{tag}c{c}mul")
                p.emit(eng, lambda e, o=x_ap, i=tm: e.tensor_tensor(
                    o, o, i, op=ALU.add), reads=[self.b_tmp[r]],
                    writes=[x_b], tag=f"{tag}c{c}add")



def emit_l0_ffn_quant(p, C, wsrc, scr, n_cc_total, n_cols, tag):
    with p.scope():
        qind = p.sbuf(f"{tag}_qind", [128, 16, 32], BF16)
        qbc = p.sbuf(f"{tag}_qbc", [32, 16, 128], BF16)
        b_qind = p.buf(f"{tag}_qind")
        b_qbc = p.buf(f"{tag}_qbc")
        p.emit("gpsimd", lambda g: g.memset(qind[:], 0.0),
               writes=[b_qind], tag=f"{tag}qind0")
        p.emit("gpsimd", lambda g: g.affine_select(
            out=qind[0:64], in_=qind[0:64], compare_op=ALU.not_equal,
            fill=1.0, base=0, pattern=[[-2, 16], [1, 32]], channel_multiplier=0),
            writes=[b_qind], tag=f"{tag}qind1")
        p.emit("gpsimd", lambda g: g.affine_select(
            out=qind[64:128], in_=qind[64:128],
            compare_op=ALU.not_equal, fill=1.0, base=-1,
            pattern=[[-2, 16], [1, 32]], channel_multiplier=0),
            writes=[b_qind], tag=f"{tag}qind2")
        p.emit("gpsimd", lambda g: g.memset(qbc[:], 0.0),
               writes=[b_qbc], tag=f"{tag}qbc0")
        p.emit("gpsimd", lambda g: g.affine_select(
            out=qbc[:], in_=qbc[:], compare_op=ALU.not_equal,
            fill=1.0, base=0, pattern=[[2, 16], [1, 2], [0, 64]],
            channel_multiplier=-1),
            writes=[b_qbc], tag=f"{tag}qbc1")
        wb = p.sbuf(f"{tag}_wb", [128, 16, 512], BF16)
        awb = p.sbuf(f"{tag}_awb", [128, 2, 512], BF16)
        srow = p.sbuf(f"{tag}_srow", [32, 512], BF16)
        bsb = p.sbuf(f"{tag}_bsb", [128, 2, 512], BF16)
        ab = p.sbuf(f"{tag}_ab", [128, 2, 2, 512], BF16)
        wq = p.sbuf(f"{tag}_wq", [128, 2, 512], BF16)
        b_wb = [p.buf(f"{tag}_wb{i}") for i in range(16)]
        b_awb = [p.buf(f"{tag}_awb{i}") for i in range(2)]
        b_srow = p.buf(f"{tag}_srow")
        b_bsb = [p.buf(f"{tag}_bsb{i}") for i in range(2)]
        b_ab = [p.buf(f"{tag}_ab{i}") for i in range(2)]
        b_wq = [p.buf(f"{tag}_wq{i}") for i in range(2)]

        n_cs = n_cols // 512
        for qb in range(n_cc_total // 16 * n_cs):
            ccg, cs = divmod(qb, n_cs)
            ps_sc, b_ps_sc = p.bank(0)
            ps_bs, b_ps_bs = p.bank(1 + (qb % 2))
            for i in range(16):
                cc = ccg * 16 + i
                r = i % 2
                p.emit("gpsimd", lambda e, cc_=cc, cs_=cs, i_=i: e.dma_start(
                    out=wb[:, i_, :],
                    in_=wsrc[cc_ * 128:(cc_ + 1) * 128, cs_ * 512:(cs_ + 1) * 512]),
                    writes=[b_wb[i]], dma=True, tag=f"{tag}q{qb}ld{i}")
                p.emit("vector", lambda e, i_=i, r_=r: e.tensor_scalar(
                    awb[:, r_, :].bitcast(U16), wb[:, i_, :].bitcast(U16),
                    0x7fff, None, op0=ALU.bitwise_and),
                    reads=[b_wb[i]], writes=[b_awb[r]], tag=f"{tag}q{qb}abs{i}")
                p.emit("tensor", lambda e, i_=i, r_=r, ps_=ps_sc: e.matmul(
                    ps_[0:32, :], qind[:, i_, :], awb[:, r_, :],
                    start=(i_ == 0), stop=(i_ == 15)),
                    reads=[b_awb[r], b_qind], writes=[b_ps_sc],
                    tag=f"{tag}q{qb}smm{i}")
            p.emit("vector", lambda e, ps_=ps_sc: e.tensor_scalar(
                srow[:], ps_[0:32, :], 1.0 / 64.0, 1e-8, op0=ALU.mult,
                op1=ALU.max), reads=[b_ps_sc], writes=[b_srow],
                tag=f"{tag}q{qb}fin")
            for i in range(16):
                cc = ccg * 16 + i
                r = i % 2
                p.emit("tensor", lambda e, i_=i, ps_=ps_bs: e.matmul(
                    ps_[:, :], qbc[:, i_, :], srow[:],
                    start=True, stop=True),
                    reads=[b_srow, b_qbc], writes=[b_ps_bs],
                    tag=f"{tag}q{qb}bmm{i}")
                p.emit("vector", lambda e, r_=r, ps_=ps_bs: e.tensor_copy(
                    bsb[:, r_, :], ps_[:, :]),
                    reads=[b_ps_bs], writes=[b_bsb[r]], tag=f"{tag}q{qb}bsc{i}")
                p.emit("vector", lambda e, i_=i, r_=r: e.scalar_tensor_tensor(
                    ab[:, r_, 0, :], wb[:, i_, :], TERN_INV_C, bsb[:, r_, :],
                    op0=ALU.mult, op1=ALU.is_gt),
                    reads=[b_wb[i], b_bsb[r]], writes=[b_ab[r]],
                    tag=f"{tag}q{qb}A{i}")
                p.emit("vector", lambda e, i_=i, r_=r: e.scalar_tensor_tensor(
                    ab[:, r_, 1, :], wb[:, i_, :], -TERN_INV_C, bsb[:, r_, :],
                    op0=ALU.mult, op1=ALU.is_gt),
                    reads=[b_wb[i], b_bsb[r]], writes=[b_ab[r]],
                    tag=f"{tag}q{qb}B{i}")
                p.emit("gpsimd", lambda e, r_=r: e.tensor_tensor(
                    ab[:, r_, 0, :], ab[:, r_, 0, :], ab[:, r_, 1, :],
                    op=ALU.subtract), writes=[b_ab[r]], tag=f"{tag}q{qb}D{i}")
                p.emit("vector", lambda e, r_=r: e.tensor_tensor(
                    wq[:, r_, :], ab[:, r_, 0, :], bsb[:, r_, :], op=ALU.mult),
                    reads=[b_ab[r], b_bsb[r]], writes=[b_wq[r]],
                    tag=f"{tag}q{qb}W{i}")
                p.emit("gpsimd", lambda e, cc_=cc, cs_=cs, r_=r: e.dma_start(
                    out=scr[cc_, :, cs_ * 512:(cs_ + 1) * 512],
                    in_=wq[:, r_, :]),
                    reads=[b_wq[r]], dma=True, tag=f"{tag}q{qb}st{i}")


def emit_l0_ffn(p, C, ln, io, xacc):
    emit_l0_ffn_quant(p, C, io["wgu"], io["scr_gu"], DC, 16384, "qgu")
    emit_l0_ffn_quant(p, C, io["wpr"], io["scr_pr"], 64, 2048, "qpr")

    with p.scope():
        rmsrow = p.sbuf("ff_rms", [1, 2, TG], F32)
        b_rmsrow = p.buf("ff_rms")
        rbf = p.sbuf("ff_rbf", [128, TG], F32)
        b_rbf = p.buf("ff_rbf")
        for g in range(NG):
            _l0_ffn_gu(p, C, ln, io, xacc, g, rmsrow, b_rmsrow, rbf, b_rbf)
            _l0_ffn_proj(p, C, ln, io, xacc, g, rbf, b_rbf)


def _l0_ffn_gu(p, C, ln, io, xacc, g, rmsrow, b_rmsrow, rbf, b_rbf):
    with p.scope():
        xln = p.sbuf(f"ff{g}_xln", [128, DC, TG], BF16)
        b_xln = [p.buf(f"ff{g}_xln{c}") for c in range(DC)]
        wgu_sb = p.sbuf(f"ff{g}_wgu", [128, 2, DC, 256], BF16)
        b_wgu = [p.buf(f"ff{g}_wgu{i}") for i in range(2)]
        sg = p.sbuf(f"ff{g}_sg", [128, 2, TG], F32)
        b_sg = [p.buf(f"ff{g}_sg{i}") for i in range(2)]
        hh = p.sbuf(f"ff{g}_h", [128, 2, TG], BF16)
        b_hh = [p.buf(f"ff{g}_h{i}") for i in range(2)]
        hsq = p.sbuf(f"ff{g}_hsq", [128, 2, TG], BF16)
        b_hsq = [p.buf(f"ff{g}_hsq{i}") for i in range(2)]

        ln.emit(f"ff{g}ln1", TG,
                src=lambda c: (xacc(c)[0][:, g * TG:(g + 1) * TG], xacc(c)[1]),
                dst=lambda c: (xln[:, c, :], b_xln[c]))
        ps_rms, b_ps_rms = p.bank(3)
        for sc in range(32):
            rs = sc % 2
            p.emit("sync", lambda e, sc_=sc, rs_=rs: e.dma_start(
                out=wgu_sb[:, rs_, :, :],
                in_=io["scr_gu"][:, :, sc_ * 256:(sc_ + 1) * 256]
                .rearrange("c p n -> p c n")),
                writes=[b_wgu[rs]], dma=True, tag=f"ff{g}s{sc}ldg")
            for sub in range(2):
                hc = sc * 2 + sub
                pg, b_pg = p.bank(sub)
                for k in range(DC):
                    p.emit("tensor", lambda e, k_=k, rs_=rs, sub_=sub, pg_=pg: e.matmul(
                        pg_[:, 0:TG],
                        wgu_sb[:, rs_, k_, sub_ * 128:(sub_ + 1) * 128],
                        xln[:, k_, :],
                        start=(k_ == 0), stop=(k_ == DC - 1)),
                        reads=[b_wgu[rs], b_xln[k]], writes=[b_pg],
                        tag=f"ff{g}h{hc}g{k}")
                p.emit("scalar", lambda e, sub_=sub, pg_=pg: e.activation(
                    sg[:, sub_, :], pg_[:, 0:TG], AF.Silu),
                    reads=[b_pg], writes=[b_sg[sub]], tag=f"ff{g}h{hc}silu")
            p.emit("sync", lambda e, sc_=sc, rs_=rs: e.dma_start(
                out=wgu_sb[:, rs_, :, :],
                in_=io["scr_gu"][:, :, 8192 + sc_ * 256:8192 + (sc_ + 1) * 256]
                .rearrange("c p n -> p c n")),
                writes=[b_wgu[rs]], dma=True, tag=f"ff{g}s{sc}ldu")
            for sub in range(2):
                hc = sc * 2 + sub
                pg, b_pg = p.bank(sub)
                for k in range(DC):
                    p.emit("tensor", lambda e, k_=k, rs_=rs, sub_=sub, pg_=pg: e.matmul(
                        pg_[:, 0:TG],
                        wgu_sb[:, rs_, k_, sub_ * 128:(sub_ + 1) * 128],
                        xln[:, k_, :],
                        start=(k_ == 0), stop=(k_ == DC - 1)),
                        reads=[b_wgu[rs], b_xln[k]], writes=[b_pg],
                        tag=f"ff{g}h{hc}u{k}")
                p.emit("vector", lambda e, sub_=sub, pg_=pg: e.tensor_tensor(
                    hh[:, sub_, :], sg[:, sub_, :], pg_[:, 0:TG], op=ALU.mult),
                    reads=[b_sg[sub], b_pg], writes=[b_hh[sub]],
                    tag=f"ff{g}h{hc}mul")
                p.emit("gpsimd", lambda e, sub_=sub: e.tensor_tensor(
                    hsq[:, sub_, :], hh[:, sub_, :], hh[:, sub_, :],
                    op=ALU.mult),
                    reads=[b_hh[sub]], writes=[b_hsq[sub]], tag=f"ff{g}h{hc}sq")
                p.emit("tensor", lambda e, hc_=hc, sub_=sub: e.matmul(
                    ps_rms[0:1, 0:TG], C.ones_col_b[:], hsq[:, sub_, :],
                    start=(hc_ == 0), stop=(hc_ == 63)),
                    reads=[b_hsq[sub], C.b_ones_col_b], writes=[b_ps_rms],
                    tag=f"ff{g}h{hc}rmm")
                p.emit("gpsimd", lambda e, hc_=hc, g_=g, sub_=sub: e.dma_start(
                    out=io["scr_h"][hc_, :, g_ * TG:(g_ + 1) * TG],
                    in_=hh[:, sub_, :]),
                    reads=[b_hh[sub]], dma=True, tag=f"ff{g}h{hc}sth")
        ms = rmsrow[0:1, 0, :]
        tr = rmsrow[0:1, 1, :]
        p.emit("vector", lambda e: e.tensor_scalar(
            ms, ps_rms[0:1, 0:TG], 1.0 / 8192.0, None, op0=ALU.mult),
            reads=[b_ps_rms], writes=[b_rmsrow], tag=f"ff{g}rms_ms")
        p.emit("scalar", lambda e: e.activation(tr, ms, AF.Ln,
               bias=np.float32(RMS_EPS)), reads=[b_rmsrow],
               writes=[b_rmsrow], tag=f"ff{g}rms_ln")
        p.emit("scalar", lambda e: e.activation(ms, tr, AF.Exp,
               scale=np.float32(-0.5)), writes=[b_rmsrow],
               tag=f"ff{g}rms_exp")
        ps_rb, b_ps_rb = p.bank(0)
        p.emit("tensor", lambda e: e.matmul(ps_rb[:, 0:TG], C.ones_row_f[:],
               ms, start=True, stop=True),
               reads=[b_rmsrow, C.b_ones_row_f], writes=[b_ps_rb],
               tag=f"ff{g}rms_b")
        p.emit("vector", lambda e: e.tensor_copy(rbf[:], ps_rb[:, 0:TG]),
               reads=[b_ps_rb], writes=[b_rbf], tag=f"ff{g}rms_e")


def _l0_ffn_proj(p, C, ln, io, xacc, g, rbf, b_rbf):
    with p.scope():
        hin = p.sbuf(f"ff{g}_hin", [128, 2, TG], BF16)
        b_hin = [p.buf(f"ff{g}_hin{i}") for i in range(2)]
        wpr_sb = p.sbuf(f"ff{g}_wpr", [128, 2, 1024], BF16)
        b_wpr = [p.buf(f"ff{g}_wpr{i}") for i in range(2)]
        ysb = p.sbuf(f"ff{g}_y", [128, DC, TG], F32)
        b_ysb = [p.buf(f"ff{g}_y{c}") for c in range(DC)]

        for ohalf in range(2):
            pys = [p.bank(i) for i in range(8)]
            for k in range(64):
                r = k % 2
                p.emit("sync", lambda e, k_=k, g_=g, r_=r: e.dma_start(
                    out=hin[:, r_, :],
                    in_=io["scr_h"][k_, :, g_ * TG:(g_ + 1) * TG]),
                    writes=[b_hin[r]], dma=True, tag=f"ff{g}o{ohalf}k{k}ldh")
                p.emit("sync", lambda e, k_=k, oh_=ohalf, r_=r: e.dma_start(
                    out=wpr_sb[:, r_, :],
                    in_=io["scr_pr"][k_, :, oh_ * 1024:(oh_ + 1) * 1024]),
                    writes=[b_wpr[r]], dma=True, tag=f"ff{g}o{ohalf}k{k}ldw")
                for oc in range(8):
                    p.emit("tensor", lambda e, k_=k, oc_=oc, r_=r: e.matmul(
                        pys[oc_][0][:, 0:TG],
                        wpr_sb[:, r_, oc_ * 128:(oc_ + 1) * 128],
                        hin[:, r_, :],
                        start=(k_ == 0), stop=(k_ == 63)),
                        reads=[b_wpr[r], b_hin[r]], writes=[pys[oc][1]],
                        tag=f"ff{g}o{ohalf}k{k}m{oc}")
            for oc in range(8):
                c = ohalf * 8 + oc
                p.emit("vector", lambda e, oc_=oc, c_=c: e.tensor_tensor(
                    ysb[:, c_, :], pys[oc_][0][:, 0:TG], rbf[:],
                    op=ALU.mult), reads=[pys[oc][1], b_rbf],
                    writes=[b_ysb[c]], tag=f"ff{g}o{ohalf}ev{oc}")
        ln.emit(f"ff{g}ln2", TG,
                src=lambda c: (ysb[:, c, :], b_ysb[c]),
                dst=None,
                resid=lambda c: (xacc(c)[0][:, g * TG:(g + 1) * TG],
                                 xacc(c)[1]))



NHEAD = 32


K_AFM = [None, None]


def emit_l0_attn(p, C, ln, io, xacc, ctxacc):
    lam = LOG_DECAY
    afm = p.sbuf("at_afm", [128, 2, T_OWN], BF16)
    b_afm = [p.buf(f"at_afm{c}") for c in range(2)]
    K_AFM[0], K_AFM[1] = afm, b_afm
    tri = p.sbuf("at_tri", [128, 128], BF16)
    b_tri = p.buf("at_tri")
    biasc = p.sbuf("at_biasc", [128, 2], F32)
    b_biasc = p.buf("at_biasc")
    beta = p.sbuf("at_beta", [128, 16], F32)
    b_beta = p.buf("at_beta")
    dv = p.sbuf("at_dv", [128, 2, 2], F32)
    b_dv = p.buf("at_dv")
    dvb = p.sbuf("at_dvb", [128, 2, 1], BF16)
    b_dvb = p.buf("at_dvb")

    p.emit("gpsimd", lambda g: g.memset(tri[:], 0.0), writes=[b_tri], tag="tri0")
    p.emit("gpsimd", lambda g: g.affine_select(
        out=tri[:], in_=tri[:], compare_op=ALU.is_ge, fill=NEG_BIG,
        base=0, pattern=[[1, 128]], channel_multiplier=-1),
        writes=[b_tri], tag="tri1")
    p.emit("gpsimd", lambda g: g.iota(
        biasc[:, 0:1], pattern=[[0, 1]], base=0, channel_multiplier=1,
        allow_small_or_imprecise_dtypes=True), writes=[b_biasc], tag="bc0")
    p.emit("vector", lambda e: e.tensor_scalar(
        biasc[:, 1:2], biasc[:, 0:1], -lam, 192.0 * lam, op0=ALU.mult,
        op1=ALU.add), reads=[b_biasc], writes=[b_biasc], tag="bc1")
    p.emit("vector", lambda e: e.tensor_scalar(
        biasc[:, 0:1], biasc[:, 0:1], -lam, 64.0 * lam, op0=ALU.mult,
        op1=ALU.add), writes=[b_biasc], tag="bc2")
    for c in range(2):
        p.emit("gpsimd", lambda g, c_=c: g.dma_start(
            out=dv[:, c_, 0:1],
            in_=io["vemb"][0, c_ * 128:(c_ + 1) * 128].rearrange(
                "(p o) -> p o", o=1)), writes=[b_dv], dma=True, tag=f"ve0{c}")
        p.emit("gpsimd", lambda g, c_=c: g.dma_start(
            out=dv[:, c_, 1:2],
            in_=io["vemb"][1, c_ * 128:(c_ + 1) * 128].rearrange(
                "(p o) -> p o", o=1)), writes=[b_dv], dma=True, tag=f"ve1{c}")
    for c in range(2):
        p.emit("vector", lambda e, c_=c: e.tensor_copy(
            dvb[:, c_, :], dv[:, c_, 0:1]), reads=[b_dv], writes=[b_dvb],
            tag=f"dvb{c}")
        p.emit("vector", lambda e, c_=c: e.tensor_tensor(
            dv[:, c_, 1:2], dv[:, c_, 1:2], dv[:, c_, 0:1], op=ALU.subtract),
            reads=[b_dv], writes=[b_dv], tag=f"dv{c}")

    _l0_attn_phase_a(p, C, ln, io, xacc, ctxacc, afm, b_afm, biasc, b_biasc,
                     tri, b_tri)
    _l0_attn_phase_b(p, C, ln, io, xacc, afm, b_afm, beta, b_beta, dv, b_dv,
                     dvb, b_dvb)


def _l0_attn_phase_a(p, C, ln, io, xacc, ctxacc, afm, b_afm, biasc, b_biasc,
                     tri, b_tri):
    with p.scope():
        xln = p.sbuf("atA_xln", [128, DC, 256], BF16)
        b_xln = [p.buf(f"atA_xln{c}") for c in range(DC)]
        wqp = p.sbuf("atA_wqp", [128, 2, DC, 128], BF16)
        b_wqp = [p.buf(f"atA_wqp{i}") for i in range(2)]
        wv = p.sbuf("atA_wv", [128, DC, 256], BF16)
        b_wv = p.buf("atA_wv")
        qtp = p.sbuf("atA_qtp", [128, 8, 2, 128], BF16)
        b_qtp = [p.buf(f"atA_qtp{i}") for i in range(2)]
        ktp = p.sbuf("atA_ktp", [128, 8, 3, 128], BF16)
        b_ktp = [p.buf(f"atA_ktp{i}") for i in range(3)]
        vtok = p.sbuf("atA_vtok", [128, 3, 9 * NHEAD], BF16)
        b_vtok = [p.buf(f"atA_vtok{i}") for i in range(3)]
        probs = p.sbuf("atA_probs", [128, 4, 2, 128], BF16)
        b_probs = [p.buf(f"atA_probs{i}") for i in range(4)]
        atok = p.sbuf("atA_atok", [128, 2, 256], BF16)
        b_atok = [p.buf(f"atA_atok{i}") for i in range(2)]
        recip = p.sbuf("atA_recip", [128, 2, NHEAD], F32)
        b_recip = [p.buf(f"atA_recip{i}") for i in range(2)]

        p.emit("gpsimd", lambda g: g.memset(
            vtok[:].rearrange("p t (h n) -> p t h n", n=9)[:, :, :, 8:9],
            1.0), writes=[b_vtok[0], b_vtok[1], b_vtok[2]], tag="atA_vones")
        p.emit("gpsimd", lambda g: g.dma_start(
            out=wv[:],
            in_=io["wv_t"][:, :].rearrange("(c p) n -> p c n", p=128)),
            writes=[b_wv], dma=True, tag="atA_ldwv")

        def ln_src(c, grp):
            if grp == 0:
                a, b = ctxacc(c)
                return a[:, 0:CTX], b
            a, b = xacc(c)
            return a[:, (grp - 1) * 256:grp * 256], b

        for grp in range(9):
            TT = 128 if grp == 0 else 256
            ln.emit(f"atA{grp}ln", TT,
                    src=lambda c: ln_src(c, grp),
                    dst=lambda c: (xln[:, c, 0:TT], b_xln[c]))
            ext_tiles = [0] if grp == 0 else [2 * grp - 1, 2 * grp]
            for (which, dstring, b_ring, nring, wsrc) in (
                    ("q", qtp, b_qtp, 2, io["wqp"]),
                    ("k", ktp, b_ktp, 3, io["wkp"])):
                if which == "q" and grp == 0:
                    continue
                for oc in range(8):
                    r = oc % 2
                    p.emit("gpsimd", lambda g, oc_=oc, r_=r, w_=wsrc: g.dma_start(
                        out=wqp[:, r_, :, :],
                        in_=w_[:, oc_ * 128:(oc_ + 1) * 128]
                        .rearrange("(c p) n -> p c n", p=128)),
                        writes=[b_wqp[r]], dma=True,
                        tag=f"atA{grp}{which}w{oc}")
                    pq, b_pq = p.bank(oc % 2)
                    for k in range(DC):
                        p.emit("tensor", lambda e, k_=k, r_=r, TT_=TT, pq_=pq:
                               e.matmul(pq_[:, 0:TT_], wqp[:, r_, k_, :],
                                        xln[:, k_, 0:TT_],
                                        start=(k_ == 0), stop=(k_ == DC - 1)),
                               reads=[b_wqp[r], b_xln[k]], writes=[b_pq],
                               tag=f"atA{grp}{which}m{oc}_{k}")
                    for ti, te in enumerate(ext_tiles):
                        sl = te % nring
                        p.emit("scalar", lambda e, oc_=oc, ti_=ti, sl_=sl,
                               d_=dstring, pq_=pq: e.activation(
                                   d_[:, oc_, sl_, :],
                                   pq_[:, ti_ * 128:(ti_ + 1) * 128], AF.Tanh),
                               reads=[b_pq], writes=[b_ring[sl]],
                               tag=f"atA{grp}{which}e{oc}_{ti}")
            for vc in range(2):
                pv, b_pv = p.bank(vc)
                for k in range(DC):
                    p.emit("tensor", lambda e, k_=k, vc_=vc, TT_=TT, pv_=pv:
                           e.matmul(pv_[:, 0:TT_], wv[:, k_, vc_ * 128:(vc_ + 1) * 128],
                                    xln[:, k_, 0:TT_],
                                    start=(k_ == 0), stop=(k_ == DC - 1)),
                           reads=[b_wv, b_xln[k]], writes=[b_pv],
                           tag=f"atA{grp}vm{vc}_{k}")
                p.emit("scalar", lambda e, vc_=vc, TT_=TT, pv_=pv: e.activation(
                    atok[:, vc_, 0:TT_], pv_[:, 0:TT_], AF.Sigmoid),
                    reads=[b_pv], writes=[b_atok[vc]], tag=f"atA{grp}vs{vc}")
            for ti, te in enumerate(ext_tiles):
                sl = te % 3
                pt, b_pt = p.bank(2)
                for vc in range(2):
                    p.emit("tensor", lambda e, vc_=vc, ti_=ti, pt_=pt: e.transpose(
                        pt_.bitcast(BF16)[:, vc_ * 128:(vc_ + 1) * 128],
                        atok[:, vc_, ti_ * 128:(ti_ + 1) * 128], C.ident[:]),
                        reads=[b_atok[vc], C.b_ident], writes=[b_pt],
                        tag=f"atA{grp}vt{te}_{vc}")
                p.emit("vector", lambda e, sl_=sl, pt_=pt: e.tensor_copy(
                    vtok[:, sl_, :].rearrange("p (h n) -> p h n", n=9)[:, :, 0:8],
                    pt_.bitcast(BF16)[:, 0:256].rearrange(
                        "p (h n) -> p h n", n=8)),
                    reads=[b_pt], writes=[b_vtok[sl]], tag=f"atA{grp}ve{te}")
            if grp == 0:
                continue
            for ti, te in enumerate(ext_tiles):
                tq = te - 1
                qsl = te % 2
                avb, b_avb = p.bank(6 + te % 2)
                for h in range(NHEAD):
                    hb = 32 * (h % 4)
                    htile = h // 4
                    psc, b_psc = p.bank(2 + h % 4)
                    p.emit("tensor", lambda e, h_=h, hb_=hb, ht_=htile,
                           qsl_=qsl, te_=te, ps_=psc: e.matmul(
                               ps_[:, 0:128],
                               ktp[hb_:hb_ + 32, ht_, te_ % 3, :],
                               qtp[hb_:hb_ + 32, ht_, qsl_, :],
                               start=True, stop=False,
                               tile_position=(hb_, 0)),
                           reads=[b_ktp[te % 3], b_qtp[qsl]], writes=[b_psc],
                           tag=f"atA_s{te}_{h}")
                    p.emit("tensor", lambda e, ps_=psc: e.matmul(
                        ps_[:, 0:128], C.ident[:], tri[:],
                        start=False, stop=True),
                        reads=[C.b_ident, b_tri], writes=[b_psc],
                        tag=f"atA_m{te}_{h}")
                    p.emit("tensor", lambda e, h_=h, hb_=hb, ht_=htile,
                           qsl_=qsl, te_=te, ps_=psc: e.matmul(
                               ps_[:, 128:256],
                               ktp[hb_:hb_ + 32, ht_, (te_ - 1) % 3, :],
                               qtp[hb_:hb_ + 32, ht_, qsl_, :],
                               start=True, stop=True,
                               tile_position=(hb_, 0)),
                           reads=[b_ktp[(te - 1) % 3], b_qtp[qsl]],
                           writes=[b_psc], tag=f"atA_l{te}_{h}")
                    for kt in range(2):
                        p.emit("scalar", lambda e, h_=h, kt_=kt, ps_=psc:
                               e.activation(
                                   probs[:, h_ % 4, kt_, :],
                                   ps_[:, kt_ * 128:(kt_ + 1) * 128], AF.Exp,
                                   bias=biasc[:, kt_:kt_ + 1],
                                   scale=np.float32(0.125)),
                               reads=[b_psc], slots=[b_biasc],
                               writes=[b_probs[h % 4]],
                               tag=f"atA_e{te}_{h}_{kt}")
                    for kt in range(2):
                        ksl = (te - kt) % 3
                        p.emit("tensor", lambda e, h_=h, kt_=kt, ksl_=ksl,
                               av_=avb: e.matmul(
                                   av_[:, 9 * h_:9 * h_ + 9],
                                   probs[:, h_ % 4, kt_, :],
                                   vtok[:, ksl_, 9 * h_:9 * h_ + 9],
                                   start=(kt_ == 0), stop=(kt_ == 1)),
                               reads=[b_probs[h % 4], b_vtok[ksl]],
                               writes=[b_avb], tag=f"atA_av{te}_{h}_{kt}")
                avv = avb[:, 0:288].rearrange("p (h n) -> p h n", n=9)
                p.emit("vector", lambda e, av_=avv, sl_=te % 2: e.reciprocal(
                    recip[:, sl_, :], av_[:, :, 8:9]),
                    reads=[b_avb], writes=[b_recip[te % 2]],
                    tag=f"atA_r{te}")
                p.emit("vector", lambda e, av_=avv, sl_=te % 2, tq_=tq:
                       e.tensor_tensor(
                           atok[:, tq_ % 2, :].rearrange("p (h n) -> p h n", n=8),
                           av_[:, :, 0:8],
                           recip[:, sl_, :].unsqueeze(2).broadcast_to(
                               [128, NHEAD, 8]),
                           op=ALU.mult),
                       reads=[b_avb, b_recip[te % 2]], writes=[b_atok[tq % 2]],
                       tag=f"atA_o{te}")
                ptf, b_ptf = p.bank(2 + (te % 2))
                for vc in range(2):
                    p.emit("tensor", lambda e, vc_=vc, tq_=tq, pt_=ptf:
                           e.transpose(
                               pt_.bitcast(BF16)[:, vc_ * 128:(vc_ + 1) * 128],
                               atok[:, tq_ % 2, vc_ * 128:(vc_ + 1) * 128],
                               C.ident[:]),
                           reads=[b_atok[tq % 2], C.b_ident], writes=[b_ptf],
                           tag=f"atA_t{te}_{vc}")
                for vc in range(2):
                    p.emit("vector", lambda e, vc_=vc, tq_=tq, pt_=ptf:
                           e.tensor_copy(
                               afm[:, vc_, tq_ * 128:(tq_ + 1) * 128],
                               pt_.bitcast(BF16)[:, vc_ * 128:(vc_ + 1) * 128]),
                           reads=[b_ptf], writes=[b_afm[vc]],
                           tag=f"atA_f{te}_{vc}")


def _l0_attn_phase_b(p, C, ln, io, xacc, afm, b_afm, beta, b_beta, dv, b_dv,
                     dvb, b_dvb):
    with p.scope():
        xln = p.sbuf("atB_xln", [128, DC, 256], BF16)
        b_xln = [p.buf(f"atB_xln{c}") for c in range(DC)]
        wg = p.sbuf("atB_wg", [128, 2, DC, 128], BF16)
        b_wg = [p.buf(f"atB_wg{i}") for i in range(2)]
        wo = p.sbuf("atB_wo", [128, 2, 2, 128], BF16)
        b_wo = [p.buf(f"atB_wo{i}") for i in range(2)]
        gate = p.sbuf("atB_gate", [128, 2, 256], BF16)
        b_gate = [p.buf(f"atB_gate{i}") for i in range(2)]
        ysb = p.sbuf("atB_y", [128, DC, 256], F32)
        b_ysb = [p.buf(f"atB_y{c}") for c in range(DC)]

        for oc in range(16):
            r = oc % 2
            pb, b_pb = p.bank(r)
            p.emit("gpsimd", lambda g, oc_=oc, r_=r: g.dma_start(
                out=wo[:, r_, :, :],
                in_=io["wo_t"][:, oc_ * 128:(oc_ + 1) * 128]
                .rearrange("(c p) n -> p c n", p=128)),
                writes=[b_wo[r]], dma=True, tag=f"atBbw{oc}")
            for c in range(2):
                p.emit("tensor", lambda e, c_=c, r_=r, pb_=pb: e.matmul(
                    pb_[:, 0:1], wo[:, r_, c_, :], dvb[:, c_, :],
                    start=(c_ == 0), stop=(c_ == 1)),
                    reads=[b_wo[r], b_dvb], writes=[b_pb], tag=f"atBbm{oc}_{c}")
            p.emit("vector", lambda e, oc_=oc, pb_=pb: e.tensor_copy(
                beta[:, oc_:oc_ + 1], pb_[:, 0:1]),
                reads=[b_pb], writes=[b_beta], tag=f"atBbe{oc}")

        for grp in range(8):
            ln.emit(f"atB{grp}ln", 256,
                    src=lambda c: (xacc(c)[0][:, grp * 256:(grp + 1) * 256],
                                   xacc(c)[1]),
                    dst=lambda c: (xln[:, c, :], b_xln[c]))
            for oc in range(DC):
                r = oc % 2
                pgt, b_pgt = p.bank(r)
                p.emit("gpsimd", lambda g, oc_=oc, r_=r: g.dma_start(
                    out=wg[:, r_, :, :],
                    in_=io["wg_t"][:, oc_ * 128:(oc_ + 1) * 128]
                    .rearrange("(c p) n -> p c n", p=128)),
                    writes=[b_wg[r]], dma=True, tag=f"atB{grp}gw{oc}")
                for k in range(DC):
                    p.emit("tensor", lambda e, k_=k, r_=r, pg_=pgt: e.matmul(
                        pg_[:, 0:256], wg[:, r_, k_, :], xln[:, k_, :],
                        start=(k_ == 0), stop=(k_ == DC - 1)),
                        reads=[b_wg[r], b_xln[k]], writes=[b_pgt],
                        tag=f"atB{grp}gm{oc}_{k}")
                p.emit("scalar", lambda e, r_=r, pg_=pgt: e.activation(
                    gate[:, r_, :], pg_[:, 0:256], AF.Sigmoid),
                    reads=[b_pgt], writes=[b_gate[r]], tag=f"atB{grp}gs{oc}")
                pat, b_pat = p.bank(2 + r)
                p.emit("gpsimd", lambda g, oc_=oc, r_=r: g.dma_start(
                    out=wo[:, r_, :, :],
                    in_=io["wo_t"][:, oc_ * 128:(oc_ + 1) * 128]
                    .rearrange("(c p) n -> p c n", p=128)),
                    writes=[b_wo[r]], dma=True, tag=f"atB{grp}ow{oc}")
                for c in range(2):
                    p.emit("vector", lambda e, c_=c, r_=r: e.tensor_scalar(
                        wo[:, r_, c_, :], wo[:, r_, c_, :], dv[:, c_, 1:2],
                        None, op0=ALU.mult),
                        reads=[b_dv], writes=[b_wo[r]], tag=f"atB{grp}os{oc}_{c}")
                for c in range(2):
                    p.emit("tensor", lambda e, c_=c, r_=r, grp_=grp, pa_=pat:
                           e.matmul(
                               pa_[:, 0:256], wo[:, r_, c_, :],
                               afm[:, c_, grp_ * 256:(grp_ + 1) * 256],
                               start=(c_ == 0), stop=(c_ == 1)),
                           reads=[b_wo[r], b_afm[c]], writes=[b_pat],
                           tag=f"atB{grp}om{oc}_{c}")
                p.emit("vector", lambda e, oc_=oc, r_=r, pa_=pat:
                       e.scalar_tensor_tensor(
                           ysb[:, oc_, :], pa_[:, 0:256], beta[:, oc_:oc_ + 1],
                           gate[:, r_, :], op0=ALU.add, op1=ALU.mult),
                       reads=[b_pat, b_gate[r]], slots=[b_beta],
                       writes=[b_ysb[oc]], tag=f"atB{grp}ev{oc}")
            ln.emit(f"atB{grp}ln2", 256,
                    src=lambda c: (ysb[:, c, :], b_ysb[c]),
                    dst=None,
                    resid=lambda c: (xacc(c)[0][:, grp * 256:(grp + 1) * 256],
                                     xacc(c)[1]))



LH = 16
LD = 128


def emit_l1_attn(p, C, ln, io, xacc):
    _l1_main(p, C, ln, io, xacc)
    _l1_p3(p, C, ln, io, xacc)


def _l1_main(p, C, ln, io, xacc):
    with p.scope():
        _l1_main2(p, C, ln, io, xacc)


def _l1_main2(p, C, ln, io, xacc):
    dk = p.sbuf("la_dk", [1, 4, LH], F32)
    b_dk = p.buf("la_dk")
    lndb = p.sbuf("la_lndb", [128, LH], F32)
    b_lndb = p.buf("la_lndb")
    mm = p.sbuf("la_mm", [128, 128], F32)
    b_mm = p.buf("la_mm")
    tri01 = p.sbuf("la_tri01", [128, 128], BF16)
    b_tri01 = p.buf("la_tri01")
    Ah = p.sbuf("la_A", [128, LH, 128], BF16)
    b_Ah = p.buf("la_A")
    af = p.sbuf("la_af", [128, 1, 128], F32)
    b_af = p.buf("la_af")
    dprow = p.sbuf("la_dprow", [1, LH, 128], BF16)
    b_dprow = p.buf("la_dprow")
    dprf = p.sbuf("la_dprf", [1, 1, 128], F32)
    b_dprf = p.buf("la_dprf")
    d128b = p.sbuf("la_d128b", [1, LH], BF16)
    b_d128b = p.buf("la_d128b")
    dvall = p.sbuf("la_dvall", [128, 16, LH], BF16)
    b_dvall = p.buf("la_dvall")
    dvf = p.sbuf("la_dvf", [128, 2, 16], F32)
    b_dvf = p.buf("la_dvf")

    p.emit("gpsimd", lambda g: g.dma_start(out=dk[:, 0, :], in_=io["decay"][:, :]),
           writes=[b_dk], dma=True, tag="la_lddk")
    p.emit("scalar", lambda e: e.activation(dk[:, 1, :], dk[:, 0, :], AF.Sigmoid),
           reads=[b_dk], writes=[b_dk], tag="la_sig")
    p.emit("scalar", lambda e: e.activation(dk[:, 2, :], dk[:, 1, :], AF.Ln),
           writes=[b_dk], tag="la_ln")
    p.emit("scalar", lambda e: e.activation(dk[:, 3, :], dk[:, 2, :], AF.Exp,
           scale=np.float32(128.0)), writes=[b_dk], tag="la_d128")
    p.emit("vector", lambda e: e.tensor_copy(d128b[:, :], dk[:, 3, :]),
           reads=[b_dk], writes=[b_d128b], tag="la_d128b")
    pb0, b_pb0 = p.bank(0)
    p.emit("tensor", lambda e, ps_=pb0: e.matmul(ps_[:, 0:LH], C.ones_row_f[:],
           dk[:, 2, :], start=True, stop=True),
           reads=[b_dk, C.b_ones_row_f], writes=[b_pb0], tag="la_bl")
    p.emit("vector", lambda e, ps_=pb0: e.tensor_copy(lndb[:], ps_[:, 0:LH]),
           reads=[b_pb0], writes=[b_lndb], tag="la_ble")
    p.emit("gpsimd", lambda g: g.iota(mm[:], pattern=[[1, 128]], base=0,
           channel_multiplier=-1, allow_small_or_imprecise_dtypes=True),
           writes=[b_mm], tag="la_m0")
    p.emit("vector", lambda e: e.tensor_scalar(mm[:], mm[:], 0.0, None,
           op0=ALU.max), reads=[b_mm], writes=[b_mm], tag="la_m1")
    p.emit("gpsimd", lambda g: g.memset(tri01[:], 0.0), writes=[b_tri01],
           tag="la_t0")
    p.emit("gpsimd", lambda g: g.affine_select(
        out=tri01[:], in_=tri01[:], compare_op=ALU.is_gt, fill=1.0,
        base=0, pattern=[[-1, 128]], channel_multiplier=1),
        writes=[b_tri01], tag="la_t1")
    p.emit("gpsimd", lambda g: g.iota(dprf[:, 0, :], pattern=[[1, 128]], base=1,
           channel_multiplier=0, allow_small_or_imprecise_dtypes=True),
           writes=[b_dprf], tag="la_dp0")
    p.emit("gpsimd", lambda g: g.iota(dvf[:, 0, :], pattern=[[-128, 16]],
           base=2047, channel_multiplier=-1,
           allow_small_or_imprecise_dtypes=True),
           writes=[b_dvf], tag="la_dv0")
    for h in range(LH):
        p.emit("scalar", lambda e, h_=h: e.activation(
            af[:, 0, :], mm[:], AF.Exp, scale=lndb[:, h_:h_ + 1]),
            reads=[b_mm], slots=[b_lndb], writes=[b_af], tag=f"la_Ae{h}")
        p.emit("vector", lambda e, h_=h: e.tensor_tensor(
            Ah[:, h_, :], af[:, 0, :], tri01[:], op=ALU.mult),
            reads=[b_af, b_tri01], writes=[b_Ah], tag=f"la_Am{h}")
        p.emit("scalar", lambda e, h_=h: e.activation(
            af[0:1, 0, :], dprf[:, 0, :], AF.Exp, scale=dk[:, 2, h_:h_ + 1]),
            reads=[b_dprf], slots=[b_dk], writes=[b_af], tag=f"la_dpe{h}")
        p.emit("vector", lambda e, h_=h: e.tensor_copy(
            dprow[:, h_, :], af[0:1, 0, :]),
            reads=[b_af], writes=[b_dprow], tag=f"la_dpc{h}")
        p.emit("scalar", lambda e, h_=h: e.activation(
            dvf[:, 1, :], dvf[:, 0, :], AF.Exp, scale=lndb[:, h_:h_ + 1]),
            reads=[b_dvf], slots=[b_lndb], writes=[b_dvf], tag=f"la_dve{h}")
        p.emit("vector", lambda e, h_=h: e.tensor_copy(
            dvall[:, :, h_], dvf[:, 1, :]),
            reads=[b_dvf], writes=[b_dvall], tag=f"la_dvc{h}")

    gkv = p.sbuf("la_gkv", [1, 2048], F32)
    b_gkv = p.buf("la_gkv")
    gks = p.sbuf("la_gks", [1, 2048], F32)
    b_gks = p.buf("la_gks")
    p.emit("gpsimd", lambda g: g.memset(gkv[:], 0.0), writes=[b_gkv], tag="la_g0")
    p.emit("gpsimd", lambda g: g.memset(gks[:], 0.0), writes=[b_gks], tag="la_g1")

    _l1_p1(p, C, ln, io, xacc, dvall, b_dvall, gkv, b_gkv, gks, b_gks)
    Tkv, b_Tkv = gkv, b_gkv
    Tks, b_Tks = gks, b_gks
    _l1_exchange(p, C, io, gkv, b_gkv, gks, b_gks, Tkv, b_Tkv, Tks, b_Tks)
    _l1_p2(p, C, io, Ah, b_Ah, dprow, b_dprow, d128b, b_d128b,
           Tkv, b_Tkv, Tks, b_Tks)


def _l1_p1(p, C, ln, io, xacc, dvall, b_dvall, gkv, b_gkv, gks, b_gks):
    NBORDER = [0, 1, 2, 3, 4, 8, 5, 9, 6, 10, 7, 11]
    with p.scope():
        xln = p.sbuf("l1a_xln", [128, DC, TG], BF16)
        b_xln = [p.buf(f"l1a_xln{c}") for c in range(DC)]
        wt = p.sbuf("l1a_wt", [128, 4, 512], BF16)
        b_wt = [p.buf(f"l1a_wt{i}") for i in range(4)]
        s1 = p.sbuf("l1a_s1", [128, 1, 512], F32)
        b_s1 = [p.buf("l1a_s10")]
        kb = p.sbuf("l1a_kb", [128, 4, 512], BF16)
        b_kb = [p.buf(f"l1a_kb{i}") for i in range(4)]
        vb = p.sbuf("l1a_vb", [128, 1, 512], BF16)
        b_vb = [p.buf("l1a_vb0"), None]
        kvb = p.sbuf("l1a_kvb", [128, 2, 512], BF16)
        b_kvb = [p.buf(f"l1a_kvb{i}") for i in range(2)]

        for g in range(NG):
            ln.emit(f"l1a{g}ln", TG,
                    src=lambda c: (xacc(c)[0][:, g * TG:(g + 1) * TG], xacc(c)[1]),
                    dst=lambda c: (xln[:, c, :], b_xln[c]))
            for nbpair in range(6):
              pair = NBORDER[2 * nbpair:2 * nbpair + 2]
              for kc in range(DC):
                for pi, nb in enumerate(pair):
                    r = (2 * kc + pi) % 4
                    p.emit("gpsimd", lambda e, kc_=kc, nb_=nb, r_=r:
                           e.dma_start(
                               out=wt[:, r_, :],
                               in_=io["wqkv"][kc_ * 128:(kc_ + 1) * 128,
                                              nb_ * 512:(nb_ + 1) * 512]),
                           writes=[b_wt[r]], dma=True,
                           tag=f"l1a{g}w{nb}_{kc}")
                for tt in range(4):
                    for pi, nb in enumerate(pair):
                        r = (2 * kc + pi) % 4
                        pq, b_pq = p.bank(4 * pi + tt)
                        p.emit("tensor", lambda e, kc_=kc, tt_=tt, r_=r,
                               pq_=pq: e.matmul(
                                   pq_[:, 0:512],
                                   xln[:, kc_, tt_ * 128:(tt_ + 1) * 128],
                                   wt[:, r_, :],
                                   start=(kc_ == 0), stop=(kc_ == DC - 1)),
                               reads=[b_wt[r], b_xln[kc]], writes=[b_pq],
                               tag=f"l1a{g}m{nb}_{tt}_{kc}")
              for nb in pair:
                kind = nb // 4
                co = nb % 4
                pi = NBORDER.index(nb) % 2
                pqs = [p.bank(4 * pi + t) for t in range(4)]
                for tt in range(4):
                    cc = 4 * g + tt
                    pq, b_pq = pqs[tt]
                    rr = tt % 2
                    if kind == 0:
                        _phi_evac(p, pq, b_pq, s1, b_s1, kvb, b_kvb, rr,
                                  f"l1a{g}q{nb}_{tt}")
                        p.emit("gpsimd", lambda e, cc_=cc, co_=co, r_=rr:
                               e.dma_start(
                                   out=io["scr_q"][cc_, :,
                                                   co_ * 512:(co_ + 1) * 512],
                                   in_=kvb[:, r_, :]),
                               reads=[b_kvb[rr]], dma=True,
                               tag=f"l1a{g}qs{nb}_{tt}")
                    elif kind == 1:
                        _phi_evac(p, pq, b_pq, s1, b_s1, kb, b_kb, tt,
                                  f"l1a{g}k{nb}_{tt}")
                        p.emit("gpsimd", lambda e, cc_=cc, co_=co, tt_=tt:
                               e.dma_start(
                                   out=io["scr_k"][cc_, :,
                                                   co_ * 512:(co_ + 1) * 512],
                                   in_=kb[:, tt_, :]),
                               reads=[b_kb[tt]], dma=True,
                               tag=f"l1a{g}ks{nb}_{tt}")
                    else:
                        p.emit("vector", lambda e, pq_=pq:
                               e.tensor_copy(vb[:, 0, :], pq_[:, 0:512]),
                               reads=[b_pq], writes=[b_vb[0]],
                               tag=f"l1a{g}v{nb}_{tt}")
                        p.emit("vector", lambda e, tt_=tt, r_=rr:
                               e.tensor_tensor(
                                   kvb[:, r_, :], kb[:, tt_, :], vb[:, 0, :],
                                   op=ALU.mult),
                               reads=[b_kb[tt], b_vb[0]], writes=[b_kvb[rr]],
                               tag=f"l1a{g}kv{nb}_{tt}")
                        p.emit("gpsimd", lambda e, cc_=cc, co_=co, r_=rr:
                               e.dma_start(
                                   out=io["scr_kv"][cc_, :,
                                                    co_ * 512:(co_ + 1) * 512],
                                   in_=kvb[:, r_, :]),
                               reads=[b_kvb[rr]], dma=True,
                               tag=f"l1a{g}kvs{nb}_{tt}")
                        p4, b_p4 = p.bank(0)
                        p5, b_p5 = p.bank(1)
                        for hh in range(4):
                            h = co * 4 + hh
                            p.emit("tensor", lambda e, h_=h, hh_=hh, cc_=cc,
                                   r_=rr, ps_=p4: e.matmul(
                                       ps_[0:1, 128 * hh_:128 * hh_ + 128],
                                       dvall[:, cc_, h_:h_ + 1],
                                       kvb[:, r_, 128 * hh_:128 * hh_ + 128],
                                       start=True, stop=True),
                                   reads=[b_kvb[rr], b_dvall],
                                   writes=[b_p4], tag=f"l1a{g}st{nb}_{tt}_{hh}")
                            p.emit("tensor", lambda e, h_=h, hh_=hh, cc_=cc,
                                   tt_=tt, ps_=p5: e.matmul(
                                       ps_[0:1, 128 * hh_:128 * hh_ + 128],
                                       dvall[:, cc_, h_:h_ + 1],
                                       kb[:, tt_, 128 * hh_:128 * hh_ + 128],
                                       start=True, stop=True),
                                   reads=[b_kb[tt], b_dvall],
                                   writes=[b_p5], tag=f"l1a{g}su{nb}_{tt}_{hh}")
                        cols = slice(co * 512, co * 512 + 512)
                        p.emit("vector", lambda e, cols_=cols, ps_=p4:
                               e.tensor_tensor(gkv[0:1, cols_],
                                               gkv[0:1, cols_],
                                               ps_[0:1, 0:512], op=ALU.add),
                               reads=[b_p4], writes=[b_gkv],
                               tag=f"l1a{g}ga{nb}_{tt}")
                        p.emit("vector", lambda e, cols_=cols, ps_=p5:
                               e.tensor_tensor(gks[0:1, cols_],
                                               gks[0:1, cols_],
                                               ps_[0:1, 0:512], op=ALU.add),
                               reads=[b_p5], writes=[b_gks],
                               tag=f"l1a{g}gb{nb}_{tt}")


def _phi_evac(p, pq, b_pq, s1, b_s1, dst, b_dst, slot, tag):
    r = 0
    p.emit("vector", lambda e, r_=r, pq_=pq: e.tensor_scalar(
        s1[:, r_, :], pq_[:, 0:512], 0.0, None, op0=ALU.min),
        reads=[b_pq], writes=[b_s1[r]], tag=tag + "min")
    p.emit("scalar", lambda e, r_=r: e.activation(
        s1[:, r_, :], s1[:, r_, :], AF.Exp),
        reads=[b_s1[r]], writes=[b_s1[r]], tag=tag + "exp")
    p.emit("vector", lambda e, r_=r, s_=slot, pq_=pq: e.scalar_tensor_tensor(
        dst[:, s_, :], pq_[:, 0:512], 0.0, s1[:, r_, :],
        op0=ALU.max, op1=ALU.add),
        reads=[b_pq, b_s1[r]], writes=[b_dst[slot]], tag=tag + "stt")


def _l1_exchange(p, C, io, gkv, b_gkv, gks, b_gks, Tkv, b_Tkv, Tks, b_Tks):
    with p.scope():
        _l1_exchange2(p, C, io, gkv, b_gkv, gks, b_gks, Tkv, b_Tkv, Tks, b_Tks)


def _l1_exchange2(p, C, io, gkv, b_gkv, gks, b_gks, Tkv, b_Tkv, Tks, b_Tks):
    selb = p.sbuf("la_sel", [16, 2], F32)
    b_selb = p.buf("la_sel")
    gath = p.sbuf("la_gath", [16, 2048], F32)
    b_gath = p.buf("la_gath")
    p.emit("gpsimd", lambda g: g.dma_start(out=selb[:], in_=io["sel"][:, :]),
           writes=[b_selb], dma=True, tag="la_ldsel")
    w0 = p.emit("gpsimd", lambda g: g.dma_start(out=io["cc_in"][0:1, :],
                in_=gkv[:]), reads=[b_gkv], dma=True, tag="la_ccw0")
    w1 = p.emit("gpsimd", lambda g: g.dma_start(out=io["cc_in"][1:2, :],
                in_=gks[:]), reads=[b_gks], dma=True, tag="la_ccw1")
    cc_op = p.emit("gpsimd", lambda g: g.collective_compute(
        "AllGather", ALU.bypass, replica_groups=[list(range(NCORES))],
        ins=[io["cc_in"].opt()], outs=[io["cc_out"].opt()]),
        deps=[w0, w1], tag="la_cc")
    p.emit("gpsimd", lambda g: g.dma_start(out=gath[:], in_=io["cc_out"][:, :]),
           deps=[cc_op], writes=[b_gath], dma=True, tag="la_ccr")
    for j in range(4):
        pr, b_pr = p.bank(j % 2)
        pr2, b_pr2 = p.bank(2 + j % 2)
        p.emit("tensor", lambda e, j_=j, pr_=pr: e.matmul(
            pr_[0:1, 0:512], selb[:, 0:1], gath[:, j_ * 512:(j_ + 1) * 512],
            start=True, stop=True),
            reads=[b_selb, b_gath], writes=[b_pr], tag=f"la_sm{j}")
        p.emit("tensor", lambda e, j_=j, pr_=pr2: e.matmul(
            pr_[0:1, 0:512], selb[:, 1:2], gath[:, j_ * 512:(j_ + 1) * 512],
            start=True, stop=True),
            reads=[b_selb, b_gath], writes=[b_pr2], tag=f"la_sn{j}")
        p.emit("vector", lambda e, j_=j, pr_=pr: e.tensor_copy(
            Tkv[0:1, 512 * j_:512 * (j_ + 1)], pr_[0:1, 0:512]),
            reads=[b_pr], writes=[b_Tkv], tag=f"la_se0{j}")
        p.emit("vector", lambda e, j_=j, pr_=pr2: e.tensor_copy(
            Tks[0:1, 512 * j_:512 * (j_ + 1)], pr_[0:1, 0:512]),
            reads=[b_pr2], writes=[b_Tks], tag=f"la_se1{j}")


def _l1_p2(p, C, io, Ah, b_Ah, dprow, b_dprow, d128b, b_d128b,
           Tkv, b_Tkv, Tks, b_Tks):
    with p.scope():
        kvt = p.sbuf("l1b_kvt", [128, 1, 2048], BF16)
        b_kvt = [p.buf("l1b_kvt0")]
        kt = p.sbuf("l1b_kt", [128, 1, 2048], BF16)
        b_kt = [p.buf("l1b_kt0")]
        qt = p.sbuf("l1b_qt", [128, 2048], BF16)
        b_qt = p.buf("l1b_qt")
        Tbf = p.sbuf("l1b_Tbf", [1, 2, 2048], BF16)
        b_Tbf = p.buf("l1b_Tbf")
        tmp = p.sbuf("l1b_tmp", [128, 512], F32)
        b_tmp = p.buf("l1b_tmp")
        den = p.sbuf("l1b_den", [128, 2, LH], F32)
        b_den = p.buf("l1b_den")
        atk = p.sbuf("l1b_atk", [128, 2, 512], BF16)
        b_atk = [p.buf(f"l1b_atk{i}") for i in range(2)]

        for cc in range(16):
            r = 0
            p.emit("sync", lambda e, cc_=cc, r_=r: e.dma_start(
                out=kvt[:, r_, :], in_=io["scr_kv"][cc_, :, :]),
                writes=[b_kvt[r]], dma=True, tag=f"l1b{cc}lkv")
            p.emit("sync", lambda e, cc_=cc, r_=r: e.dma_start(
                out=kt[:, r_, :], in_=io["scr_k"][cc_, :, :]),
                writes=[b_kt[r]], dma=True, tag=f"l1b{cc}lk")
            p.emit("sync", lambda e, cc_=cc: e.dma_start(
                out=qt[:], in_=io["scr_q"][cc_, :, :]),
                writes=[b_qt], dma=True, tag=f"l1b{cc}lq")
            p.emit("gpsimd", lambda g: g.dma_start(out=Tbf[0:1, 0, :],
                   in_=Tkv[:]), reads=[b_Tkv], writes=[b_Tbf], dma=True,
                   tag=f"l1b{cc}cb0")
            p.emit("gpsimd", lambda g: g.dma_start(out=Tbf[0:1, 1, :],
                   in_=Tks[:]), reads=[b_Tks], writes=[b_Tbf], dma=True,
                   tag=f"l1b{cc}cb1")
            for sc in range(2):
                srcb = (b_kvt if sc == 0 else b_kt)[r]
                srct = kvt if sc == 0 else kt
                for h in range(LH):
                    bk, b_bk = p.bank(4 * sc + h // 4)
                    colr = slice(128 * (h % 4), 128 * (h % 4) + 128)
                    p.emit("tensor", lambda e, h_=h, colr_=colr, bk_=bk,
                           st_=srct, r_=r: e.matmul(
                               bk_[:, colr_], Ah[:, h_, :],
                               st_[:, r_, 128 * h_:128 * h_ + 128],
                               start=True, stop=False),
                           reads=[srcb, b_Ah], writes=[b_bk],
                           tag=f"l1b{cc}A{sc}_{h}")
                    p.emit("tensor", lambda e, h_=h, sc_=sc, colr_=colr,
                           bk_=bk: e.matmul(
                               bk_[:, colr_], dprow[:, h_, :],
                               Tbf[0:1, sc_, 128 * h_:128 * h_ + 128],
                               start=False, stop=True),
                           reads=[b_dprow, b_Tbf], writes=[b_bk],
                           tag=f"l1b{cc}C{sc}_{h}")
            for j in range(4):
                bk, b_bk = p.bank(4 + j)
                p.emit("vector", lambda e, j_=j, bk_=bk: e.tensor_tensor(
                    tmp[:], qt[:, 512 * j_:512 * (j_ + 1)], bk_[:, :],
                    op=ALU.mult), reads=[b_qt, b_bk], writes=[b_tmp],
                    tag=f"l1b{cc}dm{j}")
                p.emit("vector", lambda e, j_=j: e.tensor_reduce(
                    den[:, 0, 4 * j_:4 * j_ + 4],
                    tmp[:].rearrange("p (h d) -> p h d", d=128),
                    axis=AX.X, op=ALU.add),
                    reads=[b_tmp], writes=[b_den], tag=f"l1b{cc}dr{j}")
            p.emit("vector", lambda e: e.tensor_scalar(
                den[:, 0, :], den[:, 0, :], 1e-6, None, op0=ALU.max),
                writes=[b_den], tag=f"l1b{cc}dc")
            p.emit("vector", lambda e: e.reciprocal(den[:, 1, :], den[:, 0, :]),
                   reads=[b_den], writes=[b_den], tag=f"l1b{cc}di")
            for j in range(4):
                bk, b_bk = p.bank(j)
                p.emit("vector", lambda e, j_=j, bk_=bk: e.tensor_tensor(
                    tmp[:], qt[:, 512 * j_:512 * (j_ + 1)], bk_[:, :],
                    op=ALU.mult), reads=[b_qt, b_bk], writes=[b_tmp],
                    tag=f"l1b{cc}om{j}")
                p.emit("vector", lambda e, j_=j: e.tensor_tensor(
                    atk[:, j_ % 2, :].rearrange("p (h d) -> p h d", d=128),
                    tmp[:].rearrange("p (h d) -> p h d", d=128),
                    den[:, 1, 4 * j_:4 * j_ + 4].unsqueeze(2)
                    .broadcast_to([128, 4, 128]),
                    op=ALU.mult),
                    reads=[b_tmp, b_den], writes=[b_atk[j % 2]],
                    tag=f"l1b{cc}oo{j}")
                p.emit("gpsimd", lambda e, cc_=cc, j_=j: e.dma_start(
                    out=io["scr_at"][cc_, :, 512 * j_:512 * (j_ + 1)],
                    in_=atk[:, j_ % 2, :]),
                    reads=[b_atk[j % 2]], dma=True, tag=f"l1b{cc}os{j}")
            if cc < 15:
                for sc in range(2):
                    srcb = (b_kvt if sc == 0 else b_kt)[r]
                    srct = kvt if sc == 0 else kt
                    for h in range(LH):
                        bk, b_bk = p.bank(4 * sc + h // 4)
                        colr = slice(128 * (h % 4), 128 * (h % 4) + 128)
                        p.emit("tensor", lambda e, h_=h, colr_=colr, bk_=bk,
                               st_=srct, r_=r: e.matmul(
                                   bk_[0:1, colr_], Ah[:, h_, 127:128],
                                   st_[:, r_, 128 * h_:128 * h_ + 128],
                                   start=True, stop=False),
                               reads=[srcb, b_Ah], writes=[b_bk],
                               tag=f"l1b{cc}sv{sc}_{h}")
                        p.emit("tensor", lambda e, h_=h, sc_=sc, colr_=colr,
                               bk_=bk: e.matmul(
                                   bk_[0:1, colr_], d128b[:, h_:h_ + 1],
                                   Tbf[0:1, sc_, 128 * h_:128 * h_ + 128],
                                   start=False, stop=True),
                               reads=[b_d128b, b_Tbf], writes=[b_bk],
                               tag=f"l1b{cc}sw{sc}_{h}")
                for j in range(4):
                    b0, bb0 = p.bank(j)
                    b1, bb1 = p.bank(4 + j)
                    p.emit("vector", lambda e, j_=j, b_=b0: e.tensor_copy(
                        Tkv[0:1, 512 * j_:512 * (j_ + 1)], b_[0:1, 0:512]),
                        reads=[bb0], writes=[b_Tkv], tag=f"l1b{cc}te0{j}")
                    p.emit("vector", lambda e, j_=j, b_=b1: e.tensor_copy(
                        Tks[0:1, 512 * j_:512 * (j_ + 1)], b_[0:1, 0:512]),
                        reads=[bb1], writes=[b_Tks], tag=f"l1b{cc}te1{j}")


def _l1_p3(p, C, ln, io, xacc):
    with p.scope():
        ain = p.sbuf("l1c_ain", [128, 1, 2048], BF16)
        b_ain = [p.buf("l1c_ain0")]
        afm = p.sbuf("l1c_afm", [128, DC, 256], BF16)
        b_afm = [p.buf(f"l1c_afm{i}") for i in range(DC)]
        wout = p.sbuf("l1c_wout", [128, 2, DC, 128], BF16)
        b_wout = [p.buf(f"l1c_wout{i}") for i in range(2)]
        ysb = p.sbuf("l1c_y", [128, DC, 256], F32)
        b_ysb = [p.buf(f"l1c_y{c}") for c in range(DC)]

        for grp in range(8):
            for ti in range(2):
                cc = 2 * grp + ti
                r = 0
                p.emit("sync", lambda e, cc_=cc, r_=r: e.dma_start(
                    out=ain[:, r_, :], in_=io["scr_at"][cc_, :, :]),
                    writes=[b_ain[r]], dma=True, tag=f"l1c{cc}l")
                for c in range(DC):
                    pt, b_pt = p.bank(c % 2)
                    p.emit("tensor", lambda e, c_=c, r_=r, pt_=pt: e.transpose(
                        pt_.bitcast(BF16)[:, 0:128],
                        ain[:, r_, c_ * 128:(c_ + 1) * 128], C.ident[:]),
                        reads=[b_ain[r], C.b_ident], writes=[b_pt],
                        tag=f"l1c{cc}t{c}")
                    p.emit("vector", lambda e, c_=c, ti_=ti, pt_=pt:
                           e.tensor_copy(
                               afm[:, c_, ti_ * 128:(ti_ + 1) * 128],
                               pt_.bitcast(BF16)[:, 0:128]),
                           reads=[b_pt], writes=[b_afm[c]], tag=f"l1c{cc}e{c}")
            for oc in range(DC):
                r = oc % 2
                po, b_po = p.bank(2 + r)
                p.emit("gpsimd", lambda e, oc_=oc, r_=r: e.dma_start(
                    out=wout[:, r_, :, :],
                    in_=io["wout"][:, oc_ * 128:(oc_ + 1) * 128]
                    .rearrange("(c p) n -> p c n", p=128)),
                    writes=[b_wout[r]], dma=True, tag=f"l1c{grp}w{oc}")
                for k in range(DC):
                    p.emit("tensor", lambda e, k_=k, r_=r, po_=po: e.matmul(
                        po_[:, 0:256], wout[:, r_, k_, :], afm[:, k_, :],
                        start=(k_ == 0), stop=(k_ == DC - 1)),
                        reads=[b_wout[r], b_afm[k]], writes=[b_po],
                        tag=f"l1c{grp}m{oc}_{k}")
                p.emit("vector", lambda e, oc_=oc, po_=po: e.tensor_copy(
                    ysb[:, oc_, :], po_[:, 0:256]),
                    reads=[b_po], writes=[b_ysb[oc]], tag=f"l1c{grp}ev{oc}")
            ln.emit(f"l1c{grp}ln2", 256,
                    src=lambda c: (ysb[:, c, :], b_ysb[c]),
                    dst=None,
                    resid=lambda c: (xacc(c)[0][:, grp * 256:(grp + 1) * 256],
                                     xacc(c)[1]))



MTG = 1024
NIC = {0: 11, 1: 11, 2: 11, 3: 11, 4: 11, 5: 11, 6: 11, 7: 11, 8: 22}


def emit_l1_ffn(p, C, ln, io, xacc):
    b_scrw = p.buf("mf_scrw")
    b_scrw3 = p.buf("mf_scrw3")
    p.b_scrw, p.b_scrw3 = b_scrw, b_scrw3
    for w in range(2):
        for e in range(8):
            p.emit("gpsimd", lambda g, w_=w, e_=e: g.dma_start(
                out=io["scr_w12"][w_, e_],
                in_=io["w12t"][w_, e_].rearrange("a p k n -> a p (k n)")),
                writes=[b_scrw], dma=True, tag=f"mfW{w}_{e}")
        for hf in range(2):
            p.emit("gpsimd", lambda g, w_=w, hf_=hf: g.dma_start(
                out=io["scr_sw12"][w_, hf_ * 11:(hf_ + 1) * 11],
                in_=io["sw12t"][w_, hf_ * 11:(hf_ + 1) * 11]
                .rearrange("a p k n -> a p (k n)")),
                writes=[b_scrw], dma=True, tag=f"mfWs{w}_{hf}")
    for j in range(10):
        p.emit("gpsimd", lambda g, j_=j: g.dma_start(
            out=io["scr_w3"][j_ * 11:(j_ + 1) * 11],
            in_=io["w3catt"][j_ * 11:(j_ + 1) * 11]
            .rearrange("a p k n -> a p (k n)")),
            writes=[b_scrw3], dma=True, tag=f"mfW3_{j}")

    combT = p.sbuf("mf_combT", [8, 2048], BF16)
    b_combT = p.buf("mf_combT")
    eye8 = p.sbuf("mf_eye8", [8, 8, 128], BF16)
    b_eye8 = p.buf("mf_eye8")
    p.emit("gpsimd", lambda g: g.memset(eye8[:], 0.0), writes=[b_eye8],
           tag="mf_eye0")
    p.emit("gpsimd", lambda g: g.affine_select(
        out=eye8[:], in_=eye8[:], compare_op=ALU.not_equal, fill=1.0,
        base=0, pattern=[[-1, 8], [0, 128]], channel_multiplier=1),
        writes=[b_eye8], tag="mf_eye1")
    for g in range(2):
        _l1f_group(p, C, ln, io, xacc, g, combT, b_combT, eye8, b_eye8)
    _l1f_y(p, C, ln, io, xacc)


def _l1f_group(p, C, ln, io, xacc, g, combT, b_combT, eye8, b_eye8):
    with p.scope():
        xln = p.sbuf(f"mf{g}_xln", [128, DC, MTG], BF16)
        b_xln = [p.buf(f"mf{g}_xln{c}") for c in range(DC)]
        gw = p.sbuf(f"mf{g}_gw", [128, DC, 8], BF16)
        b_gw = p.buf(f"mf{g}_gw")
        lg = p.sbuf(f"mf{g}_lg", [128, 6, 8], F32)
        b_lg = p.buf(f"mf{g}_lg")
        lrow = p.sbuf(f"mf{g}_lrow", [128, 4], F32)
        b_lrow = p.buf(f"mf{g}_lrow")
        lgb = p.sbuf(f"mf{g}_lgb", [128, 8], BF16)
        b_lgb = p.buf(f"mf{g}_lgb")
        wt = p.sbuf(f"mf{g}_wt", [128, 2, DC, 128], BF16)
        b_wt = [p.buf(f"mf{g}_wt{i}") for i in range(2)]
        cb = p.sbuf(f"mf{g}_cb", [128, 2, 512], BF16)
        b_cb = [p.buf(f"mf{g}_cb{i}") for i in range(2)]
        sv = p.sbuf(f"mf{g}_sv", [128, 512], F32)
        b_sv = p.buf(f"mf{g}_sv")
        uv = p.sbuf(f"mf{g}_uv", [128, 1, 512], F32)
        b_uv = [p.buf(f"mf{g}_uv0"), None]
        hv = p.sbuf(f"mf{g}_hv", [128, 2, 512], BF16)
        b_hv = [p.buf(f"mf{g}_hv{i}") for i in range(2)]

        p.emit("gpsimd", lambda e: e.dma_start(
            out=gw[:],
            in_=io["gw_t"][:, :].rearrange("(c p) n -> p c n", p=128)),
            writes=[b_gw], dma=True, tag=f"mf{g}gw")
        for hf in range(2):
            ln.emit(f"mf{g}ln{hf}", 512,
                    src=lambda c: (xacc(c)[0][:, g * MTG + hf * 512:
                                              g * MTG + (hf + 1) * 512],
                                   xacc(c)[1]),
                    dst=lambda c: (xln[:, c, hf * 512:(hf + 1) * 512],
                                   b_xln[c]))
        for tt in range(8):
            pgt, b_pgt = p.bank(tt % 2)
            for k in range(DC):
                p.emit("tensor", lambda e, k_=k, tt_=tt, pg_=pgt: e.matmul(
                    pg_[:, 0:8], xln[:, k_, tt_ * 128:(tt_ + 1) * 128],
                    gw[:, k_, :], start=(k_ == 0), stop=(k_ == DC - 1)),
                    reads=[b_gw, b_xln[k]], writes=[b_pgt],
                    tag=f"mf{g}g{tt}_{k}")
            p.emit("vector", lambda e, pg_=pgt: e.tensor_copy(
                lg[:, 0, :], pg_[:, 0:8]), reads=[b_pgt], writes=[b_lg],
                tag=f"mf{g}s0_{tt}")
            p.emit("vector", lambda e: e.tensor_reduce(
                lrow[:, 0:1], lg[:, 0, :], axis=AX.X, op=ALU.max, negate=True),
                reads=[b_lg], writes=[b_lrow], tag=f"mf{g}s1_{tt}")
            p.emit("scalar", lambda e: e.activation(
                lg[:, 1, :], lg[:, 0, :], AF.Exp, bias=lrow[:, 0:1]),
                reads=[b_lg], slots=[b_lrow], writes=[b_lg],
                tag=f"mf{g}s2_{tt}")
            p.emit("vector", lambda e: e.tensor_reduce(
                lrow[:, 1:2], lg[:, 1, :], axis=AX.X, op=ALU.add),
                reads=[b_lg], writes=[b_lrow], tag=f"mf{g}s3_{tt}")
            p.emit("vector", lambda e: e.reciprocal(
                lrow[:, 3:4], lrow[:, 1:2]), reads=[b_lrow], writes=[b_lrow],
                tag=f"mf{g}s4_{tt}")
            p.emit("vector", lambda e: e.tensor_scalar(
                lg[:, 2, :], lg[:, 1, :], lrow[:, 3:4], None, op0=ALU.mult),
                reads=[b_lg], slots=[b_lrow], writes=[b_lg],
                tag=f"mf{g}s5_{tt}")
            p.emit("vector", lambda e: e.tensor_reduce(
                lrow[:, 0:1], lg[:, 2, :], axis=AX.X, op=ALU.max),
                reads=[b_lg], writes=[b_lrow], tag=f"mf{g}t0_{tt}")
            p.emit("vector", lambda e: e.tensor_scalar(
                lg[:, 3, :], lg[:, 2, :], lrow[:, 0:1], None, op0=ALU.is_lt),
                reads=[b_lg], slots=[b_lrow], writes=[b_lg],
                tag=f"mf{g}t1_{tt}")
            p.emit("vector", lambda e: e.tensor_tensor(
                lg[:, 4, :], lg[:, 2, :], lg[:, 3, :], op=ALU.mult),
                writes=[b_lg], tag=f"mf{g}t2_{tt}")
            p.emit("vector", lambda e: e.tensor_reduce(
                lrow[:, 2:3], lg[:, 4, :], axis=AX.X, op=ALU.max),
                reads=[b_lg], writes=[b_lrow], tag=f"mf{g}t3_{tt}")
            p.emit("vector", lambda e: e.tensor_scalar(
                lg[:, 5, :], lg[:, 2, :], lrow[:, 2:3], None, op0=ALU.is_ge),
                reads=[b_lg], slots=[b_lrow], writes=[b_lg],
                tag=f"mf{g}t4_{tt}")
            p.emit("vector", lambda e: e.tensor_tensor(
                lg[:, 5, :], lg[:, 5, :], lg[:, 2, :], op=ALU.mult),
                writes=[b_lg], tag=f"mf{g}t5_{tt}")
            p.emit("vector", lambda e: e.tensor_copy(
                lg[:, 5, :].bitcast(U16)[:, 0:8].bitcast(BF16) if False else
                lgb[:, :], lg[:, 5, :]),
                reads=[b_lg], writes=[b_lgb], tag=f"mf{g}tb{tt}")
            ptc, b_ptc = p.bank(2 + tt % 2)
            p.emit("tensor", lambda e, pt_=ptc: e.transpose(
                pt_.bitcast(BF16)[0:8, 0:128], lgb[:, :], C.ident[:]),
                reads=[b_lgb, C.b_ident], writes=[b_ptc],
                tag=f"mf{g}tc{tt}")
            p.emit("vector", lambda e, g_=g, tt_=tt, pt_=ptc: e.tensor_copy(
                combT[:, g_ * MTG + tt_ * 128:g_ * MTG + (tt_ + 1) * 128],
                pt_.bitcast(BF16)[0:8, 0:128]),
                reads=[b_ptc], writes=[b_combT], tag=f"mf{g}te{tt}")
        for e9 in range(9):
            for tt in range(2):
                if e9 < 8:
                    pcb, b_pcb = p.bank(4 + tt % 2)
                    p.emit("tensor", lambda e, e_=e9, tt_=tt, g_=g, ps_=pcb:
                           e.matmul(ps_[:, 0:512], eye8[:, e_, :],
                                    combT[:, g_ * MTG + tt_ * 512:
                                          g_ * MTG + (tt_ + 1) * 512],
                                    start=True, stop=True),
                           reads=[b_combT, b_eye8], writes=[b_pcb],
                           tag=f"mf{g}cb{e9}_{tt}")
                    p.emit("vector", lambda e, tt_=tt, ps_=pcb: e.tensor_copy(
                        cb[:, tt_, :], ps_[:, 0:512]),
                        reads=[b_pcb], writes=[b_cb[tt]],
                        tag=f"mf{g}cbe{e9}_{tt}")
            for ic in range(NIC[e9]):
                row = e9 * 11 + ic if e9 < 8 else 88 + ic
                for w in range(2):
                    r = w
                    src = (io["scr_w12"][w, e9, ic] if e9 < 8 else
                           io["scr_sw12"][w, ic])
                    p.emit("sync", lambda e, r_=r, src_=src: e.dma_start(
                        out=wt[:, r_, :, :].rearrange("p c n -> p (c n)"),
                        in_=src_),
                        reads=[p.b_scrw], writes=[b_wt[r]], dma=True,
                        tag=f"mf{g}wl{e9}_{ic}_{w}")
                    for tt in range(2):
                        ph, b_ph = p.bank(w * 2 + tt)
                        for k in range(DC):
                            p.emit("tensor", lambda e, k_=k, tt_=tt, r_=r,
                                   ph_=ph: e.matmul(
                                       ph_[:, 0:512], wt[:, r_, k_, :],
                                       xln[:, k_, tt_ * 512:(tt_ + 1) * 512],
                                       start=(k_ == 0), stop=(k_ == DC - 1)),
                                   reads=[b_wt[r], b_xln[k]], writes=[b_ph],
                                   tag=f"mf{g}hm{e9}_{ic}_{w}_{tt}_{k}")
                for tt in range(2):
                    p1, b_p1 = p.bank(tt)
                    p2, b_p2 = p.bank(2 + tt)
                    p.emit("scalar", lambda e, ps_=p1: e.activation(
                        sv[:, :], ps_[:, 0:512], AF.Silu),
                        reads=[b_p1], writes=[b_sv], tag=f"mf{g}si{e9}_{ic}_{tt}")
                    p.emit("vector", lambda e, ps_=p2: e.tensor_tensor(
                        uv[:, 0, :], sv[:, :], ps_[:, 0:512], op=ALU.mult),
                        reads=[b_sv, b_p2], writes=[b_uv[0]],
                        tag=f"mf{g}u{e9}_{ic}_{tt}")
                    if e9 < 8:
                        p.emit("gpsimd", lambda e, tt_=tt: e.tensor_tensor(
                            hv[:, tt_, :], uv[:, 0, :], cb[:, tt_, :],
                            op=ALU.mult),
                            reads=[b_uv[0], b_cb[tt]], writes=[b_hv[tt]],
                            tag=f"mf{g}hc{e9}_{ic}_{tt}")
                    else:
                        p.emit("gpsimd", lambda e, tt_=tt: e.tensor_copy(
                            hv[:, tt_, :], uv[:, 0, :]),
                            reads=[b_uv[0]], writes=[b_hv[tt]],
                            tag=f"mf{g}hc{e9}_{ic}_{tt}")
                    p.emit("gpsimd", lambda e, row_=row, g_=g, tt_=tt:
                           e.dma_start(
                               out=io["scr_H"][row_, :,
                                               g_ * MTG + tt_ * 512:
                                               g_ * MTG + (tt_ + 1) * 512],
                               in_=hv[:, tt_, :]),
                           reads=[b_hv[tt]], dma=True,
                           tag=f"mf{g}hs{e9}_{ic}_{tt}")


def _l1f_y(p, C, ln, io, xacc):
    with p.scope():
        hrow = p.sbuf("mfy_h", [128, 2, 512], BF16)
        b_hrow = [p.buf(f"mfy_h{i}") for i in range(2)]
        w3 = p.sbuf("mfy_w3", [128, 2, 8, 128], BF16)
        b_w3 = [p.buf(f"mfy_w3{i}") for i in range(2)]
        ysb = p.sbuf("mfy_y", [128, DC, 512], F32)
        b_ysb = [p.buf(f"mfy_y{c}") for c in range(DC)]

        for tt in range(4):
            for oh in range(2):
                pys = [p.bank(i) for i in range(8)]
                for kc in range(110):
                    r = kc % 2
                    p.emit("sync", lambda e, kc_=kc, tt_=tt, r_=r: e.dma_start(
                        out=hrow[:, r_, :],
                        in_=io["scr_H"][kc_, :, tt_ * 512:(tt_ + 1) * 512]),
                        writes=[b_hrow[r]], dma=True, tag=f"mfy{tt}_{oh}l{kc}")
                    p.emit("sync", lambda e, kc_=kc, oh_=oh, r_=r: e.dma_start(
                        out=w3[:, r_, :, :].rearrange("p c n -> p (c n)"),
                        in_=io["scr_w3"][kc_, :, oh_ * 1024:(oh_ + 1) * 1024]),
                        reads=[p.b_scrw3], writes=[b_w3[r]], dma=True,
                        tag=f"mfy{tt}_{oh}w{kc}")
                    for oc in range(8):
                        p.emit("tensor", lambda e, kc_=kc, oc_=oc, r_=r,
                               ps_=pys[oc][0]: e.matmul(
                                   ps_[:, 0:512], w3[:, r_, oc_, :],
                                   hrow[:, r_, :],
                                   start=(kc_ == 0), stop=(kc_ == 109)),
                               reads=[b_w3[r], b_hrow[r]], writes=[pys[oc][1]],
                               tag=f"mfy{tt}_{oh}m{kc}_{oc}")
                for oc in range(8):
                    c = oh * 8 + oc
                    p.emit("vector", lambda e, c_=c, ps_=pys[oc][0]:
                           e.tensor_copy(ysb[:, c_, :], ps_[:, 0:512]),
                           reads=[pys[oc][1]], writes=[b_ysb[c]],
                           tag=f"mfy{tt}_{oh}e{oc}")
            ln.emit(f"mfy{tt}ln", 512,
                    src=lambda c: (ysb[:, c, :], b_ysb[c]),
                    dst=None,
                    resid=lambda c: (xacc(c)[0][:, tt * 512:(tt + 1) * 512],
                                     xacc(c)[1]))



def _t(a):
    return np.ascontiguousarray(np.transpose(a))


def _tile_w12(w1, w2):
    out = []
    for w in (w1, w2):
        wt = np.transpose(np.asarray(w), (0, 2, 1))
        out.append(wt.reshape(8, 16, 128, 11, 128).transpose(0, 3, 2, 1, 4))
    return np.ascontiguousarray(np.stack(out), dtype=np.float32)


def _tile_sw12(sw1, sw2):
    out = []
    for w in (sw1, sw2):
        wt = _t(np.asarray(w))
        out.append(wt.reshape(16, 128, 22, 128).transpose(2, 1, 0, 3))
    return np.ascontiguousarray(np.stack(out), dtype=np.float32)


def _tile_w3(w3, sw3):
    w3t = np.transpose(np.asarray(w3), (0, 2, 1)).reshape(11264, 2048)
    cat = np.concatenate([w3t, _t(np.asarray(sw3))], axis=0)
    return np.ascontiguousarray(cat.reshape(110, 128, 16, 128),
                                dtype=np.float32)


def _pad_qk(w_t):
    w = np.zeros((D, 1024), dtype=np.float32)
    for h in range(NHEAD):
        base = 128 * (h // 4) + 32 * (h % 4)
        w[:, base:base + 8] = w_t[:, 8 * h:8 * h + 8]
    return w


def shard_inputs(inputs):
    x = inputs["L0_attn_hidden_states"]
    w_shared = {
        "wqp": _pad_qk(_t(inputs["L0_attn_Wq"])),
        "wkp": _pad_qk(_t(inputs["L0_attn_Wk"])),
        "wv_t": _t(inputs["L0_attn_Wv"]),
        "wo_t": _t(inputs["L0_attn_Wo"]),
        "wg_t": _t(inputs["L0_attn_Wg"]),
        "vemb": np.stack([np.asarray(inputs["L0_attn_v_emb0"]),
                          np.asarray(inputs["L0_attn_v_emb1"])]),
        "wgu_t": _t(inputs["L0_ffn_w_gate_up"]),
        "wpr_t": _t(inputs["L0_ffn_w_proj"]),
        "wqkv": np.ascontiguousarray(inputs["L1_attn_w_qkv"]),
        "wout": np.ascontiguousarray(inputs["L1_attn_w_out"]),
        "decay": np.asarray(inputs["L1_attn_decay_param"]).reshape(1, 16),
        "gw_t": _t(inputs["L1_ffn_gate_w"]),
        "w12t": _tile_w12(inputs["L1_ffn_w1"], inputs["L1_ffn_w2"]),
        "sw12t": _tile_sw12(inputs["L1_ffn_sw1"], inputs["L1_ffn_sw2"]),
        "w3catt": _tile_w3(inputs["L1_ffn_w3"], inputs["L1_ffn_sw3"]),
    }
    w_shared = {k: np.ascontiguousarray(v, dtype=np.float32)
                for k, v in w_shared.items()}
    maps = []
    for c in range(NCORES):
        b, half = c // 2, c % 2
        t0 = half * T_OWN
        xs = np.asarray(x[b], dtype=np.float32)
        own = xs[t0:t0 + T_OWN]
        if half == 0:
            ctx = np.zeros((CTX, D), dtype=np.float32)
        else:
            ctx = xs[t0 - CTX:t0]
        ext = np.concatenate([ctx, own], axis=0)
        xin = np.ascontiguousarray(ext.T)
        sel = np.zeros((16, 2), dtype=np.float32)
        if half == 1:
            prev = c - 1
            sel[2 * prev, 0] = 1.0
            sel[2 * prev + 1, 1] = 1.0
        m = {"xin": xin, "sel": sel}
        m.update(w_shared)
        maps.append(m)
    return maps


def unshard_output(results):
    out = np.empty((4, 4096, D), dtype=np.float32)
    for c in range(NCORES):
        b, half = c // 2, c % 2
        xo = results[c]["xout"]
        out[b, half * T_OWN:(half + 1) * T_OWN] = xo.T
    return out



def build_block_l0_ffn():
    nc = bass.Bass("TRN2", target_bir_lowering=False)
    xin = nc.dram_tensor("xin", [D, T_OWN], F32, kind="ExternalInput").ap()
    wgu = nc.dram_tensor("wgu_t", [D, 16384], F32, kind="ExternalInput").ap()
    wpr = nc.dram_tensor("wpr_t", [8192, D], F32, kind="ExternalInput").ap()
    xout = nc.dram_tensor("xout", [D, T_OWN], F32, kind="ExternalOutput").ap()
    scr_gu = nc.dram_tensor("scr_gu", [DC, 128, 16384], BF16).ap()
    scr_pr = nc.dram_tensor("scr_pr", [64, 128, 2048], BF16).ap()
    scr_h = nc.dram_tensor("scr_h", [64, 128, T_OWN], BF16).ap()
    p = Prog(nc)
    C = Consts(p)
    ln = LNKit(p, C)

    xres = p.sbuf("xres", [128, DC, T_OWN], F32)
    b_xres = [p.buf(f"xres{c}") for c in range(DC)]
    for c in range(DC):
        p.emit("sync", lambda e, c_=c: e.dma_start(
            out=xres[:, c_, :], in_=xin[c_ * 128:(c_ + 1) * 128, :]),
            writes=[b_xres[c]], dma=True, tag=f"ldx{c}")

    io = {"wgu": wgu, "wpr": wpr, "scr_gu": scr_gu, "scr_pr": scr_pr,
          "scr_h": scr_h}
    emit_l0_ffn(p, C, ln, io, lambda c: (xres[:, c, :], b_xres[c]))

    for c in range(DC):
        p.out_dma("gpsimd", lambda e, c_=c: e.dma_start(
            out=xout[c_ * 128:(c_ + 1) * 128, :], in_=xres[:, c_, :]),
            reads=[b_xres[c]], tag=f"stx{c}")
    p.realize()
    return nc


def build_block_l0_attn(debug=False):
    nc = bass.Bass("TRN2", target_bir_lowering=False)
    xin = nc.dram_tensor("xin", [D, XT], F32, kind="ExternalInput").ap()
    wqp = nc.dram_tensor("wqp", [D, 1024], F32, kind="ExternalInput").ap()
    wkp = nc.dram_tensor("wkp", [D, 1024], F32, kind="ExternalInput").ap()
    wv_t = nc.dram_tensor("wv_t", [D, 256], F32, kind="ExternalInput").ap()
    wo_t = nc.dram_tensor("wo_t", [256, D], F32, kind="ExternalInput").ap()
    wg_t = nc.dram_tensor("wg_t", [D, D], F32, kind="ExternalInput").ap()
    vemb = nc.dram_tensor("vemb", [2, 256], F32, kind="ExternalInput").ap()
    xout = nc.dram_tensor("xout", [D, T_OWN], F32, kind="ExternalOutput").ap()
    p = Prog(nc)
    C = Consts(p)
    ln = LNKit(p, C)

    xres = p.sbuf("xres", [128, DC, T_OWN], F32)
    b_xres = [p.buf(f"xres{c}") for c in range(DC)]
    xctx = p.sbuf("xctx", [128, DC, CTX], BF16)
    b_xctx = [p.buf(f"xctx{c}") for c in range(DC)]
    for c in range(DC):
        p.emit("gpsimd", lambda e, c_=c: e.dma_start(
            out=xctx[:, c_, :], in_=xin[c_ * 128:(c_ + 1) * 128, 0:CTX]),
            writes=[b_xctx[c]], dma=True, tag=f"ldctx{c}")
        p.emit("sync", lambda e, c_=c: e.dma_start(
            out=xres[:, c_, :], in_=xin[c_ * 128:(c_ + 1) * 128, CTX:XT]),
            writes=[b_xres[c]], dma=True, tag=f"ldx{c}")

    io = {"wqp": wqp, "wkp": wkp, "wv_t": wv_t, "wo_t": wo_t, "wg_t": wg_t,
          "vemb": vemb}
    emit_l0_attn(p, C, ln, io,
                 lambda c: (xres[:, c, :], b_xres[c]),
                 lambda c: (xctx[:, c, :], b_xctx[c]))

    if debug:
        dbg = nc.dram_tensor("dbg", [256, T_OWN], F32, kind="ExternalOutput").ap()
        for c in range(2):
            for t in range(4):
                p.emit("vector", lambda e, c_=c, t_=t: e.tensor_copy(
                    ln.tmp[:, 0, :], K_AFM[0][:, c_, t_ * 512:(t_ + 1) * 512]),
                    reads=[K_AFM[1][c]], writes=[ln.b_tmp[0]],
                    tag=f"dbgu{c}_{t}")
                p.out_dma("gpsimd", lambda e, c_=c, t_=t: e.dma_start(
                    out=dbg[c_ * 128:(c_ + 1) * 128, t_ * 512:(t_ + 1) * 512],
                    in_=ln.tmp[:, 0, :]),
                    reads=[ln.b_tmp[0]], tag=f"dbgs{c}_{t}")
    for c in range(DC):
        p.out_dma("gpsimd", lambda e, c_=c: e.dma_start(
            out=xout[c_ * 128:(c_ + 1) * 128, :], in_=xres[:, c_, :]),
            reads=[b_xres[c]], tag=f"stx{c}")
    p.realize()
    return nc


def build_block_l1_attn():
    nc = bass.Bass("TRN2", target_bir_lowering=False)
    xin = nc.dram_tensor("xin", [D, T_OWN], F32, kind="ExternalInput").ap()
    wqkv = nc.dram_tensor("wqkv", [D, 6144], F32, kind="ExternalInput").ap()
    wout = nc.dram_tensor("wout", [D, D], F32, kind="ExternalInput").ap()
    decay = nc.dram_tensor("decay", [1, 16], F32, kind="ExternalInput").ap()
    sel = nc.dram_tensor("sel", [16, 2], F32, kind="ExternalInput").ap()
    xout = nc.dram_tensor("xout", [D, T_OWN], F32, kind="ExternalOutput").ap()
    cc_in = nc.dram_tensor("cc_in", [2, 2048], F32).ap()
    cc_out = nc.dram_tensor("cc_out", [16, 2048], F32).ap()
    scr_q = nc.dram_tensor("scr_q", [16, 128, 2048], BF16).ap()
    scr_k = nc.dram_tensor("scr_k", [16, 128, 2048], BF16).ap()
    scr_kv = nc.dram_tensor("scr_kv", [16, 128, 2048], BF16).ap()
    scr_at = nc.dram_tensor("scr_at", [16, 128, 2048], BF16).ap()
    p = Prog(nc)
    C = Consts(p)
    ln = LNKit(p, C)

    xres = p.sbuf("xres", [128, DC, T_OWN], F32)
    b_xres = [p.buf(f"xres{c}") for c in range(DC)]
    for c in range(DC):
        p.emit("sync", lambda e, c_=c: e.dma_start(
            out=xres[:, c_, :], in_=xin[c_ * 128:(c_ + 1) * 128, :]),
            writes=[b_xres[c]], dma=True, tag=f"ldx{c}")

    io = {"wqkv": wqkv, "wout": wout, "decay": decay, "sel": sel,
          "cc_in": cc_in, "cc_out": cc_out, "scr_q": scr_q, "scr_k": scr_k,
          "scr_kv": scr_kv, "scr_at": scr_at}
    emit_l1_attn(p, C, ln, io, lambda c: (xres[:, c, :], b_xres[c]))

    for c in range(DC):
        p.out_dma("gpsimd", lambda e, c_=c: e.dma_start(
            out=xout[c_ * 128:(c_ + 1) * 128, :], in_=xres[:, c_, :]),
            reads=[b_xres[c]], tag=f"stx{c}")
    p.realize()
    return nc


def build_block_l1_ffn():
    nc = bass.Bass("TRN2", target_bir_lowering=False)
    xin = nc.dram_tensor("xin", [D, T_OWN], F32, kind="ExternalInput").ap()
    gw_t = nc.dram_tensor("gw_t", [D, 8], F32, kind="ExternalInput").ap()
    w12t = nc.dram_tensor("w12t", [2, 8, 11, 128, 16, 128], F32,
                          kind="ExternalInput").ap()
    sw12t = nc.dram_tensor("sw12t", [2, 22, 128, 16, 128], F32,
                           kind="ExternalInput").ap()
    w3catt = nc.dram_tensor("w3catt", [110, 128, 16, 128], F32,
                            kind="ExternalInput").ap()
    xout = nc.dram_tensor("xout", [D, T_OWN], F32, kind="ExternalOutput").ap()
    scr_w12 = nc.dram_tensor("scr_w12", [2, 8, 11, 128, 2048], BF16).ap()
    scr_sw12 = nc.dram_tensor("scr_sw12", [2, 22, 128, 2048], BF16).ap()
    scr_w3 = nc.dram_tensor("scr_w3", [110, 128, 2048], BF16).ap()
    scr_H = nc.dram_tensor("scr_H", [110, 128, 2048], BF16).ap()
    p = Prog(nc)
    C = Consts(p)
    ln = LNKit(p, C)

    xres = p.sbuf("xres", [128, DC, T_OWN], F32)
    b_xres = [p.buf(f"xres{c}") for c in range(DC)]
    for c in range(DC):
        p.emit("sync", lambda e, c_=c: e.dma_start(
            out=xres[:, c_, :], in_=xin[c_ * 128:(c_ + 1) * 128, :]),
            writes=[b_xres[c]], dma=True, tag=f"ldx{c}")

    io = {"gw_t": gw_t, "w12t": w12t, "sw12t": sw12t, "w3catt": w3catt,
          "scr_w12": scr_w12, "scr_sw12": scr_sw12, "scr_w3": scr_w3,
          "scr_H": scr_H}
    emit_l1_ffn(p, C, ln, io, lambda c: (xres[:, c, :], b_xres[c]))

    for c in range(DC):
        p.out_dma("gpsimd", lambda e, c_=c: e.dma_start(
            out=xout[c_ * 128:(c_ + 1) * 128, :], in_=xres[:, c_, :]),
            reads=[b_xres[c]], tag=f"stx{c}")
    p.realize()
    return nc



def _xres_io(nc, p, with_ctx, ctx_alloc=True):
    xin = nc.dram_tensor("xin", [D, XT if with_ctx else T_OWN], F32,
                         kind="ExternalInput").ap()
    xout = nc.dram_tensor("xout", [D, T_OWN], F32, kind="ExternalOutput").ap()
    xres = p.sbuf("xres", [128, DC, T_OWN], F32)
    b_xres = [p.buf(f"xres{c}") for c in range(DC)]
    off = CTX if with_ctx else 0
    for c in range(DC):
        p.emit("sync", lambda e, c_=c: e.dma_start(
            out=xres[:, c_, :], in_=xin[c_ * 128:(c_ + 1) * 128, off:off + T_OWN]),
            writes=[b_xres[c]], dma=True, tag=f"ldx{c}")
    xctx = xctx_b = None
    if with_ctx and ctx_alloc:
        xctx, xctx_b = _ctx_load(nc, p, xin)

    def store(p):
        for c in range(DC):
            p.out_dma("gpsimd", lambda e, c_=c: e.dma_start(
                out=xout[c_ * 128:(c_ + 1) * 128, :], in_=xres[:, c_, :]),
                reads=[b_xres[c]], tag=f"stx{c}")
    return (xres, b_xres, xctx, xctx_b, store, xin)


def _ctx_load(nc, p, xin):
    xctx = p.sbuf("xctx", [128, DC, CTX], BF16)
    xctx_b = [p.buf(f"xctx{c}") for c in range(DC)]
    for c in range(DC):
        p.emit("gpsimd", lambda e, c_=c: e.dma_start(
            out=xctx[:, c_, :], in_=xin[c_ * 128:(c_ + 1) * 128, 0:CTX]),
            writes=[xctx_b[c]], dma=True, tag=f"ldctx{c}")
    return xctx, xctx_b


def build_launch_l0_attn():
    nc = bass.Bass("TRN2", target_bir_lowering=False)
    p = Prog(nc)
    C = Consts(p)
    ln = LNKit(p, C)
    xres, b_xres, xctx, b_xctx, store, _ = _xres_io(nc, p, True)
    io = {k: nc.dram_tensor(k, shp, F32, kind="ExternalInput").ap()
          for k, shp in [("wqp", [D, 1024]), ("wkp", [D, 1024]),
                         ("wv_t", [D, 256]), ("wo_t", [256, D]),
                         ("wg_t", [D, D]), ("vemb", [2, 256])]}
    emit_l0_attn(p, C, ln, io,
                 lambda c: (xres[:, c, :], b_xres[c]),
                 lambda c: (xctx[:, c, :], b_xctx[c]))
    store(p)
    p.realize()
    return nc


def build_launch_l0_ffn():
    nc = bass.Bass("TRN2", target_bir_lowering=False)
    p = Prog(nc)
    C = Consts(p)
    ln = LNKit(p, C)
    xres, b_xres, _, _, store, _ = _xres_io(nc, p, False)
    io = {"wgu": nc.dram_tensor("wgu_t", [D, 16384], F32,
                                kind="ExternalInput").ap(),
          "wpr": nc.dram_tensor("wpr_t", [8192, D], F32,
                                kind="ExternalInput").ap(),
          "scr_gu": nc.dram_tensor("scr_gu", [DC, 128, 16384], BF16).ap(),
          "scr_pr": nc.dram_tensor("scr_pr", [64, 128, 2048], BF16).ap(),
          "scr_h": nc.dram_tensor("scr_h", [64, 128, T_OWN], BF16).ap()}
    emit_l0_ffn(p, C, ln, io, lambda c: (xres[:, c, :], b_xres[c]))
    store(p)
    p.realize()
    return nc


def build_launch_l1_attn():
    nc = bass.Bass("TRN2", target_bir_lowering=False)
    p = Prog(nc)
    C = Consts(p)
    ln = LNKit(p, C)
    xres, b_xres, _, _, store, _ = _xres_io(nc, p, False)
    io = {"wqkv": nc.dram_tensor("wqkv", [D, 6144], F32,
                                 kind="ExternalInput").ap(),
          "wout": nc.dram_tensor("wout", [D, D], F32,
                                 kind="ExternalInput").ap(),
          "decay": nc.dram_tensor("decay", [1, 16], F32,
                                  kind="ExternalInput").ap(),
          "sel": nc.dram_tensor("sel", [16, 2], F32,
                                kind="ExternalInput").ap(),
          "cc_in": nc.dram_tensor("cc_in", [2, 2048], F32).ap(),
          "cc_out": nc.dram_tensor("cc_out", [16, 2048], F32).ap(),
          "scr_q": nc.dram_tensor("scr_q", [16, 128, 2048], BF16).ap(),
          "scr_k": nc.dram_tensor("scr_k", [16, 128, 2048], BF16).ap(),
          "scr_kv": nc.dram_tensor("scr_kv", [16, 128, 2048], BF16).ap(),
          "scr_at": nc.dram_tensor("scr_at", [16, 128, 2048], BF16).ap()}
    emit_l1_attn(p, C, ln, io, lambda c: (xres[:, c, :], b_xres[c]))
    store(p)
    p.realize()
    return nc


def build_launch_l1_ffn():
    nc = bass.Bass("TRN2", target_bir_lowering=False)
    p = Prog(nc)
    C = Consts(p)
    ln = LNKit(p, C)
    xres, b_xres, _, _, store, _ = _xres_io(nc, p, False)
    io = {"gw_t": nc.dram_tensor("gw_t", [D, 8], F32,
                                 kind="ExternalInput").ap(),
          "w12t": nc.dram_tensor("w12t", [2, 8, 11, 128, 16, 128], F32,
                                 kind="ExternalInput").ap(),
          "sw12t": nc.dram_tensor("sw12t", [2, 22, 128, 16, 128], F32,
                                  kind="ExternalInput").ap(),
          "w3catt": nc.dram_tensor("w3catt", [110, 128, 16, 128], F32,
                                   kind="ExternalInput").ap(),
          "scr_w12": nc.dram_tensor("scr_w12", [2, 8, 11, 128, 2048],
                                    BF16).ap(),
          "scr_sw12": nc.dram_tensor("scr_sw12", [2, 22, 128, 2048],
                                     BF16).ap(),
          "scr_w3": nc.dram_tensor("scr_w3", [110, 128, 2048], BF16).ap(),
          "scr_H": nc.dram_tensor("scr_H", [110, 128, 2048], BF16).ap()}
    emit_l1_ffn(p, C, ln, io, lambda c: (xres[:, c, :], b_xres[c]))
    store(p)
    p.realize()
    return nc


def build_fused():
    nc = bass.Bass("TRN2", target_bir_lowering=False)
    p = Prog(nc)
    C = Consts(p)
    ln = LNKit(p, C)
    xres, b_xres, _, _, store, xin = _xres_io(nc, p, True, ctx_alloc=False)

    def fin(k, shp):
        return nc.dram_tensor(k, shp, F32, kind="ExternalInput").ap()

    ioA = {k: fin(k, shp)
           for k, shp in [("wqp", [D, 1024]), ("wkp", [D, 1024]),
                          ("wv_t", [D, 256]), ("wo_t", [256, D]),
                          ("wg_t", [D, D]), ("vemb", [2, 256])]}
    ioF = {"wgu": fin("wgu_t", [D, 16384]), "wpr": fin("wpr_t", [8192, D]),
           "scr_gu": nc.dram_tensor("scr_gu", [DC, 128, 16384], BF16).ap(),
           "scr_pr": nc.dram_tensor("scr_pr", [64, 128, 2048], BF16).ap(),
           "scr_h": nc.dram_tensor("scr_h", [64, 128, T_OWN], BF16).ap()}
    ioL = {"wqkv": fin("wqkv", [D, 6144]), "wout": fin("wout", [D, D]),
           "decay": fin("decay", [1, 16]), "sel": fin("sel", [16, 2]),
           "cc_in": nc.dram_tensor("cc_in", [2, 2048], F32).ap(),
           "cc_out": nc.dram_tensor("cc_out", [16, 2048], F32).ap(),
           "scr_q": nc.dram_tensor("scr_q", [16, 128, 2048], BF16).ap(),
           "scr_k": nc.dram_tensor("scr_k", [16, 128, 2048], BF16).ap(),
           "scr_kv": nc.dram_tensor("scr_kv", [16, 128, 2048], BF16).ap(),
           "scr_at": nc.dram_tensor("scr_at", [16, 128, 2048], BF16).ap()}
    ioM = {"gw_t": fin("gw_t", [D, 8]),
           "w12t": fin("w12t", [2, 8, 11, 128, 16, 128]),
           "sw12t": fin("sw12t", [2, 22, 128, 16, 128]),
           "w3catt": fin("w3catt", [110, 128, 16, 128]),
           "scr_w12": nc.dram_tensor("scr_w12", [2, 8, 11, 128, 2048],
                                     BF16).ap(),
           "scr_sw12": nc.dram_tensor("scr_sw12", [2, 22, 128, 2048],
                                      BF16).ap(),
           "scr_w3": nc.dram_tensor("scr_w3", [110, 128, 2048], BF16).ap(),
           "scr_H": nc.dram_tensor("scr_H", [110, 128, 2048], BF16).ap()}

    xacc = lambda c: (xres[:, c, :], b_xres[c])
    with p.scope():
        xctx, b_xctx = _ctx_load(nc, p, xin)
        emit_l0_attn(p, C, ln, ioA, xacc,
                     lambda c: (xctx[:, c, :], b_xctx[c]))
    with p.scope():
        emit_l0_ffn(p, C, ln, ioF, xacc)
    with p.scope():
        emit_l1_attn(p, C, ln, ioL, xacc)
    with p.scope():
        emit_l1_ffn(p, C, ln, ioM, xacc)
    store(p)
    p.realize()
    return nc


_FUSED = True

_L0A_KEYS = ["wqp", "wkp", "wv_t", "wo_t", "wg_t", "vemb"]
_L0F_KEYS = ["wgu_t", "wpr_t"]
_L1A_KEYS = ["wqkv", "wout", "decay", "sel"]
_L1F_KEYS = ["gw_t", "w12t", "sw12t", "w3catt"]


def kernel(**inputs):
    maps = shard_inputs(inputs)
    if _FUSED:
        nc = build_fused()
        res = run_bass_kernel_spmd(nc, maps, core_ids=list(range(NCORES)))
        return unshard_output(res.results)
    nc = build_launch_l0_attn()
    ms = [{"xin": m["xin"], **{k: m[k] for k in _L0A_KEYS}} for m in maps]
    res = run_bass_kernel_spmd(nc, ms, core_ids=list(range(NCORES)))
    xcur = [r["xout"] for r in res.results]
    nc = build_launch_l0_ffn()
    ms = [{"xin": xcur[c], **{k: maps[c][k] for k in _L0F_KEYS}}
          for c in range(NCORES)]
    res = run_bass_kernel_spmd(nc, ms, core_ids=list(range(NCORES)))
    xcur = [r["xout"] for r in res.results]
    nc = build_launch_l1_attn()
    ms = [{"xin": xcur[c], **{k: maps[c][k] for k in _L1A_KEYS}}
          for c in range(NCORES)]
    res = run_bass_kernel_spmd(nc, ms, core_ids=list(range(NCORES)))
    xcur = [r["xout"] for r in res.results]
    nc = build_launch_l1_ffn()
    ms = [{"xin": xcur[c], **{k: maps[c][k] for k in _L1F_KEYS}}
          for c in range(NCORES)]
    res = run_bass_kernel_spmd(nc, ms, core_ids=list(range(NCORES)))
    return unshard_output(res.results)
